# version 02
# baseline (speedup 1.0000x reference)
import numpy as np
import concourse.bass as bass
import concourse.mybir as mybir
from concourse.bass_utils import run_bass_kernel_spmd

F32 = mybir.dt.float32
BF16 = mybir.dt.bfloat16
I32 = mybir.dt.int32
AF = mybir.ActivationFunctionType
ALU = mybir.AluOpType
AX = mybir.AxisListType

S = 4096
D = 1024
DC = 8
NSL = 8
FF = 4096
FFC = 32
NE = 8
MI = 2048
L1H = 3072
L1HC = 24
NEG = -1.0e9
GAMMA = 1.0 / 32.0
QSCALE = 1.0 / 127.0
EPERM = [0, 2, 4, 6, 1, 3, 5, 7]
EPOS = [0, 4, 1, 5, 2, 6, 3, 7]

ENGINES = ("tensor", "vector", "scalar", "sync", "gpsimd")


class Ph:

    def __init__(self, nc, sems, cnt):
        self.nc = nc
        self.sems = sems
        self.cnt = cnt
        self.ops = {e: [] for e in ENGINES}

    def c(self, name):
        return self.cnt[name]

    def emit(self, eng, fn, wait=(), inc=None, incby=1):
        w = tuple(wait)
        if inc is not None:
            self.cnt[inc] += incby
        sems = self.sems

        def op(e, _w=w, _fn=fn, _inc=inc, _by=incby):
            for s, v in _w:
                e.wait_ge(sems[s], v)
            inst = _fn(e)
            if _inc is not None:
                inst.then_inc(sems[_inc], _by)
        self.ops[eng].append(op)

    def wait(self, eng, *pairs):
        sems = self.sems

        def op(e, _p=tuple(pairs)):
            for s, v in _p:
                e.wait_ge(sems[s], v)
        self.ops[eng].append(op)

    def barrier(self):
        target = self.cnt["bar"] + len(ENGINES)
        for eng in ENGINES:
            self.emit(eng, lambda e: e.nop(), inc="bar")
        for eng in ENGINES:
            self.wait(eng, ("bar", target))

    def run(self):
        with self.nc.Block() as b:
            for eng in ENGINES:
                ops = self.ops[eng]
                if ops:
                    getattr(b, eng)(lambda e, _o=ops: [f(e) for f in _o])
        self.ops = {e: [] for e in ENGINES}


def _ln_tm(ph, x_ap, sqd, tp, sd, rr, eps, nfree=D, wait_in=()):
    inv = 1.0 / nfree
    ph.emit("vector", lambda e: e.drain())
    ph.emit("vector", lambda e, _x=x_ap: e.reduce_sum(tp[:, 0:1], _x, axis=AX.X))
    ph.emit("vector", lambda e: e.drain(), inc="ve")
    ph.emit("scalar", lambda e, _x=x_ap: e.activation(
        sqd[:, :nfree], _x, AF.Square, accum_out=tp[:, 1:2]),
        wait=tuple(wait_in), inc="se")
    ph.wait("vector", ("se", ph.c("se")))
    ph.emit("vector", lambda e: e.tensor_scalar_mul(tp[:, 4:5], tp[:, 0:1], inv))
    ph.emit("vector", lambda e: e.tensor_scalar_mul(tp[:, 3:4], tp[:, 1:2], inv))
    ph.emit("vector", lambda e: e.drain())
    ph.emit("vector", lambda e: e.tensor_mul(tp[:, 2:3], tp[:, 4:5], tp[:, 4:5]))
    ph.emit("vector", lambda e: e.drain())
    ph.emit("vector", lambda e: e.tensor_sub(tp[:, 5:6], tp[:, 3:4], tp[:, 2:3]),
            inc="ve")
    ph.emit("scalar", lambda e: e.activation(sd[:], tp[:, 5:6], AF.Sqrt, bias=eps),
            wait=(("ve", ph.c("ve")),), inc="se")
    ph.emit("vector", lambda e: e.reciprocal(rr[:], sd[:]),
            wait=(("se", ph.c("se")),))
    ph.emit("vector", lambda e: e.drain(), inc="ve")


def build(dump=(), nsl=NSL):
    nc = bass.Bass("TRN2", target_bir_lowering=False)

    def din(name, shape, dt=F32):
        return nc.dram_tensor(name, shape, dt, kind="ExternalInput").ap()

    x_in = din("x", [S, D])
    l0_wq = din("l0_wq", [NSL, 128, 8, 8, 128])
    l0_wk = din("l0_wk", [NSL, 128, 8, 8, 128])
    l0_wv = din("l0_wv", [NSL, 128, 8, 8, 128])
    l0_wo = din("l0_wo", [NSL, 128, 8, 8, 128])
    l0_w1 = din("l0_w1", [NSL, 128, 8, 32, 128])
    l0_w2 = din("l0_w2", [NSL, 128, 32, 8, 128])
    l0_cb = din("l0_cb", [128, NSL, 104])
    moe_gw = din("moe_gw", [128, 8, 8])
    moe_w13 = din("moe_w13", [NE, 128, 8, 32, 128])
    moe_w2 = din("moe_w2", [NE, 128, 16, 1024])
    l1_wq = din("l1_wq", [128, 8, 8, 128])
    l1_wk = din("l1_wk", [128, 8, 8, 128])
    l1_wv = din("l1_wv", [128, 8, 1024])
    l1_wg = din("l1_wg", [128, 8, 24, 128], I32)
    l1_wu = din("l1_wu", [128, 8, 24, 128], I32)
    l1_wd = din("l1_wd", [128, 24, 1024], I32)
    identb = din("identb", [128, 128], BF16)
    identf = din("identf", [128, 128])
    ones_stat = din("ones_stat", [128, 1], BF16)
    ones_row = din("ones_row", [1, 128])
    causal = din("causal", [128, 128])

    out_t = nc.dram_tensor("out", [S, D], F32, kind="ExternalOutput").ap()

    def scr(name, shape, dt=F32):
        return nc.dram_tensor(name, shape, dt, kind="Internal").ap()

    Yd = scr("Yd", [128, DC, S])
    Ybd = scr("Ybd", [128, DC, S], BF16)
    YMd = scr("YMd", [128, DC, S], BF16)
    Hd = scr("Hd", [128, FFC, S], BF16)
    X1d = scr("X1d", [S, D])
    XN2d = scr("XN2d", [128, DC, S], BF16)

    dbg_map = {"Y": Yd, "YB": Ybd, "YM": YMd, "X1": X1d, "XN2": XN2d, "H": Hd}
    dbg_outs = []
    dbg_comb = None
    for i, name in enumerate(dump):
        if name == "COMB":
            dbg_comb = nc.dram_tensor(f"dbg{i}", [128, 32, 8], F32,
                                      kind="ExternalOutput").ap()
            continue
        src = dbg_map[name]
        dt = BF16 if name in ("XN2", "H", "YM", "YB") else F32
        dbg_outs.append((nc.dram_tensor(f"dbg{i}", list(src.shape), dt,
                                        kind="ExternalOutput").ap(), src))

    import contextlib
    stack = contextlib.ExitStack()
    pA = stack.enter_context(nc.psum_tensor("pA", [128, 512], F32))
    pB = stack.enter_context(nc.psum_tensor("pB", [128, 512], F32))
    pC = stack.enter_context(nc.psum_tensor("pC", [128, 512], F32))
    pD = stack.enter_context(nc.psum_tensor("pD", [128, 512], F32))
    pO = stack.enter_context(nc.psum_tensor("pO", [128, 1024], F32))

    sem_names = ["ld", "lc", "st", "sg", "pe", "ve", "se", "ge", "bar",
                 "w0", "w1", "w2", "ca", "cb",
                 "da", "db", "ea", "eb", "ec", "ed",
                 "fa", "fb", "ga", "gb", "gc", "gd", "ha", "hb",
                 "qa", "qb", "ra", "rb", "rc", "rd"]
    sems = {n: stack.enter_context(nc.semaphore(n)) for n in sem_names}
    cnt = {n: 0 for n in sem_names}

    idb = stack.enter_context(nc.sbuf_tensor("idb", [128, 128], BF16))
    idf = stack.enter_context(nc.sbuf_tensor("idf", [128, 128], F32))
    onst = stack.enter_context(nc.sbuf_tensor("onst", [128, 1], BF16))
    onsr = stack.enter_context(nc.sbuf_tensor("onsr", [1, 128], F32))
    caus = stack.enter_context(nc.sbuf_tensor("caus", [128, 128], F32))
    combt = stack.enter_context(nc.sbuf_tensor("combt", [128, 32, 8], F32))
    constv = stack.enter_context(nc.sbuf_tensor("constv", [128, 8], F32))
    CONST_VALS = [0.0, 1.0, 1.0e-5, 1.0e-6, GAMMA, QSCALE, -1.0, 2.0]
    for i, v in enumerate(CONST_VALS):
        nc.const_aps.register(F32, v, constv[:, i:i + 1])

    ph = Ph(nc, sems, cnt)
    for i, v in enumerate(CONST_VALS):
        ph.emit("vector", lambda e, _i=i, _v=v: e.memset(constv[:, _i:_i + 1], _v))

    for dst, src in ((idb, identb), (idf, identf), (onst, ones_stat),
                     (onsr, ones_row), (caus, causal)):
        ph.emit("sync", lambda e, _d=dst, _s=src: e.dma_start(out=_d[:], in_=_s),
                inc="ld", incby=16)
    ph.wait("sync", ("ld", ph.c("ld")))
    ph.barrier()
    ph.run()

    with (
        nc.sbuf_tensor("p0_x", [128, 2, D], F32) as xb,
        nc.sbuf_tensor("p0_xn", [128, D], F32) as xnb,
        nc.sbuf_tensor("p0_yt", [128, 2, DC, 128], F32) as ytb,
        nc.sbuf_tensor("p0_sq", [128, D], F32) as sqd,
        nc.sbuf_tensor("p0_tp", [128, 8], F32) as tp,
        nc.sbuf_tensor("p0_sd", [128, 1], F32) as sd,
        nc.sbuf_tensor("p0_rr", [128, 1], F32) as rr,
    ):
        ph = Ph(nc, sems, cnt)
        ve_done = {}
        se_done = {}
        sg_done = {}
        st_done = {}
        pe_evac = {}
        prev_last_evac = 0
        for t in range(32):
            sl = t % 2
            sld = ("da", "db")[sl]
            sle = ("ra", "rb")[sl]
            slf = ("fa", "fb")[sl]
            w = []
            if t >= 2:
                w.append(("ve", ve_done[t - 2]))
                w.append(("se", se_done[t - 2]))
                w.append(sg_done[t - 2])
            ph.emit("sync", lambda e, _s=sl, _t=t: e.dma_start(
                out=xb[:, _s, :], in_=x_in[_t * 128:(_t + 1) * 128, :]),
                wait=w, inc=sld, incby=16)
            ld_now = (sld, ph.c(sld))
            ph.emit("gpsimd", lambda e, _s=sl, _t=t: e.dma_start(
                out=X1d[_t * 128:(_t + 1) * 128, :], in_=xb[:, _s, :]),
                wait=[ld_now], inc=sle, incby=16)
            sg_done[t] = (sle, ph.c(sle))
            ph.wait("vector", ld_now)
            _ln_tm(ph, xb[:, sl, :], sqd, tp, sd, rr, 1.0e-5,
                   wait_in=[ld_now])
            se_done[t] = ph.c("se")
            wx = [("pe", ph.c("pe"))] if t >= 1 else []
            ph.emit("vector", lambda e, _s=sl: e.tensor_scalar(
                xnb[:], xb[:, _s, :], tp[:, 4:5], rr[:], ALU.subtract, ALU.mult),
                wait=wx, inc="ve")
            ve_done[t] = ph.c("ve")
            ve_now = ph.c("ve")
            for c in range(DC):
                bank = (pA, pB)[c % 2]
                w = [("ve", ve_now)]
                if c >= 2:
                    w.append(("ve", pe_evac[c - 2]))
                elif prev_last_evac:
                    w.append(("ve", prev_last_evac))
                ph.emit("tensor", lambda e, _b=bank, _c=c: e.transpose(
                    _b[:, 0:128], xnb[:, _c * 128:(_c + 1) * 128], idf[:]),
                    wait=w, inc="pe")
                pe_now = ph.c("pe")
                wv = [("pe", pe_now)]
                if t >= 2:
                    wv.append(st_done[t - 2])
                ph.emit("vector", lambda e, _b=bank, _c=c, _s=sl: e.tensor_copy(
                    ytb[:, _s, _c, :], _b[:, 0:128]), wait=wv, inc="ve")
                pe_evac[c] = ph.c("ve")
            prev_last_evac = ph.c("ve")
            ph.emit("sync", lambda e, _s=sl, _t=t: e.dma_start(
                out=Yd[:, :, _t * 128:(_t + 1) * 128], in_=ytb[:, _s]),
                wait=[("ve", ph.c("ve"))], inc=slf, incby=16)
            st_done[t] = (slf, ph.c(slf))
        for nmx in ("fa", "fb"):
            ph.wait("sync", (nmx, ph.c(nmx)))
        for nmx in ("ra", "rb"):
            ph.wait("gpsimd", (nmx, ph.c(nmx)))
        ph.barrier()
        ph.run()

    with (
        nc.sbuf_tensor("ga_y", [128, 2, 8, 512], F32) as ybf,
        nc.sbuf_tensor("ga_gw", [128, 8, 8], F32) as gwb,
        nc.sbuf_tensor("ga_sc", [128, 8], F32) as gsc,
        nc.sbuf_tensor("ga_esc", [128, 8], F32) as esc,
        nc.sbuf_tensor("ga_sm", [128, 8], F32) as smb,
        nc.sbuf_tensor("ga_gs", [128, 4], F32) as gst4,
        nc.sbuf_tensor("ga_m1g", [128, 4], F32) as m1g4,
        nc.sbuf_tensor("ga_gs2", [128, 4], F32) as gs24,
        nc.sbuf_tensor("ga_gmk", [128, 4], F32) as gmk4,
        nc.sbuf_tensor("ga_msc", [128, 8], F32) as msc,
        nc.sbuf_tensor("ga_mk1", [128, 8], F32) as mk1,
        nc.sbuf_tensor("ga_ms2", [128, 8], F32) as ms2,
        nc.sbuf_tensor("ga_mk2", [128, 8], F32) as mk2,
        nc.sbuf_tensor("ga_c1", [128, 8], F32) as c1b,
        nc.sbuf_tensor("ga_t2", [128, 16], F32) as t2,
    ):
        ph = Ph(nc, sems, cnt)
        ph.emit("sync", lambda e: e.dma_start(out=gwb[:], in_=moe_gw),
                inc="ld", incby=16)
        ld_gw = ph.c("ld")
        pe_tile = {}
        ve_grp = {}
        for tl in range(8):
            sl = tl % 2
            sld = ("da", "db")[sl]
            w = [("pe", pe_tile[tl - 2])] if tl >= 2 else []
            ph.emit("sync", lambda e, _s=sl, _t=tl: e.dma_start(
                out=ybf[:, _s], in_=Yd[:, :, _t * 512:(_t + 1) * 512]),
                wait=w, inc=sld, incby=16)
            lc_now = (sld, ph.c(sld))
            for gi in range(4):
                g = tl * 4 + gi
                bank = (pA, pB)[g % 2]
                for kc in range(DC):
                    wm = [lc_now, ("ld", ld_gw)] if kc == 0 else []
                    if kc == 0 and g >= 2:
                        wm.append(("ve", ve_grp[g - 2]))
                    ph.emit("tensor", lambda e, _b=bank, _k=kc, _s=sl, _gi=gi: e.matmul(
                        _b[:, 0:8], ybf[:, _s, _k, _gi * 128:(_gi + 1) * 128],
                        gwb[:, _k, :], start=(_k == 0), stop=(_k == DC - 1)),
                        wait=wm, inc="pe" if kc == DC - 1 else None)
                pe_now = ph.c("pe")
                E = ph.emit
                sep = lambda: E("vector", lambda e: e.drain())
                E("vector", lambda e, _b=bank: e.tensor_copy(gsc[:], _b[:, 0:8]),
                  wait=[("pe", pe_now)])
                sep()
                E("vector", lambda e: e.reduce_max(t2[:, 0:1], gsc[:], axis=AX.X))
                sep()
                E("vector", lambda e: e.tensor_scalar_mul(t2[:, 1:2], t2[:, 0:1], -1.0),
                  inc="ve")
                ve_grp[g] = ph.c("ve")
                E("scalar", lambda e: e.activation(esc[:], gsc[:], AF.Exp,
                                                   bias=t2[:, 1:2],
                                                   accum_out=t2[:, 2:3]),
                  wait=[("ve", ph.c("ve"))], inc="se")
                E("vector", lambda e: e.reciprocal(t2[:, 3:4], t2[:, 2:3]),
                  wait=[("se", ph.c("se"))])
                sep()
                E("vector", lambda e: e.tensor_scalar_mul(smb[:], esc[:], t2[:, 3:4]))
                sep()
                E("vector", lambda e: e.tensor_max(gst4[:], smb[:, 0:4], smb[:, 4:8]))
                sep()
                E("vector", lambda e: e.reduce_max(t2[:, 4:5], gst4[:], axis=AX.X))
                sep()
                E("vector", lambda e: e.tensor_scalar(
                    m1g4[:], gst4[:], t2[:, 4:5], None, ALU.is_equal))
                sep()
                E("vector", lambda e: e.scalar_tensor_tensor(
                    gs24[:], m1g4[:], NEG, gst4[:], ALU.mult, ALU.add))
                sep()
                E("vector", lambda e: e.reduce_max(t2[:, 5:6], gs24[:], axis=AX.X))
                sep()
                E("vector", lambda e: e.tensor_scalar(
                    gmk4[:], gst4[:], t2[:, 5:6], None, ALU.is_ge))
                sep()
                E("vector", lambda e: e.tensor_mul(msc[:, 0:4], smb[:, 0:4], gmk4[:]))
                E("vector", lambda e: e.tensor_mul(msc[:, 4:8], smb[:, 4:8], gmk4[:]))
                sep()
                E("vector", lambda e: e.reduce_max(t2[:, 6:7], msc[:], axis=AX.X))
                sep()
                E("vector", lambda e: e.tensor_scalar(
                    mk1[:], msc[:], t2[:, 6:7], None, ALU.is_equal))
                sep()
                E("vector", lambda e: e.scalar_tensor_tensor(
                    ms2[:], mk1[:], NEG, msc[:], ALU.mult, ALU.add))
                sep()
                E("vector", lambda e: e.reduce_max(t2[:, 7:8], ms2[:], axis=AX.X))
                sep()
                E("vector", lambda e: e.tensor_add(t2[:, 8:9], t2[:, 6:7], t2[:, 7:8]))
                sep()
                E("vector", lambda e: e.reciprocal(t2[:, 9:10], t2[:, 8:9]))
                sep()
                E("vector", lambda e: e.tensor_mul(t2[:, 10:11], t2[:, 6:7], t2[:, 9:10]))
                E("vector", lambda e: e.tensor_mul(t2[:, 11:12], t2[:, 7:8], t2[:, 9:10]))
                E("vector", lambda e: e.tensor_scalar(
                    mk2[:], msc[:], t2[:, 7:8], None, ALU.is_equal))
                sep()
                E("vector", lambda e: e.tensor_scalar_mul(c1b[:], mk1[:], t2[:, 10:11]))
                sep()
                E("vector", lambda e, _g=g: e.scalar_tensor_tensor(
                    combt[:, _g, :], mk2[:], t2[:, 11:12], c1b[:],
                    ALU.mult, ALU.add), inc="ve")
            pe_tile[tl] = ph.c("pe")
        ph.barrier()
        ph.run()

    TMOE = 256
    NTM = S // TMOE
    with (
        nc.sbuf_tensor("m_w13", [128, 2, 8, 32, 128], BF16) as w13b,
        nc.sbuf_tensor("m_w2", [128, 16, 1024], BF16) as w2b,
        nc.sbuf_tensor("m_xn", [128, 2, 8, TMOE], BF16) as xnb,
        nc.sbuf_tensor("m_act", [128, 16, TMOE], BF16) as actb,
        nc.sbuf_tensor("m_st", [128, 2, TMOE], F32) as stb,
        nc.sbuf_tensor("m_out", [128, 4, 1024], F32) as outsb,
    ):
        ph = Ph(nc, sems, cnt)
        pe_w13_done = {}
        pe_w2_done = {}
        pe_xn_done = {}
        lc_w13 = {}
        lc_w2 = {}
        ve_act_done = {}
        se_g = {}
        ve_u = {}
        sg_out = {}
        oslot_i = 0
        ph.emit("gpsimd", lambda e: e.dma_start(out=w13b[:, 0], in_=moe_w13[0]),
                inc="w0", incby=16)
        lc_w13[0] = ("w0", ph.c("w0"))
        for e_ in range(NE):
            s13 = e_ % 2
            if e_ + 1 < NE:
                w = [("pe", pe_w13_done[e_ - 1])] if e_ >= 1 else []
                snm = ("w0", "w1")[(e_ + 1) % 2]
                ph.emit("gpsimd", lambda e, _e=e_: e.dma_start(
                    out=w13b[:, (_e + 1) % 2], in_=moe_w13[_e + 1]),
                    wait=w, inc=snm, incby=16)
                lc_w13[e_ + 1] = (snm, ph.c(snm))
            w = [("pe", pe_w2_done[e_ - 1])] if e_ >= 1 else []
            ph.emit("gpsimd", lambda e, _e=e_: e.dma_start(
                out=w2b[:], in_=moe_w2[_e]), wait=w, inc="w2", incby=16)
            lc_w2[e_] = ("w2", ph.c("w2"))
            for t in range(NTM):
                sl = t % 2
                w = []
                pt = (e_, t - 2) if t >= 2 else (e_ - 1, NTM - 2 + t)
                if pt in pe_xn_done:
                    w.append(("pe", pe_xn_done[pt]))
                snm = ("qa", "qb")[sl]
                ph.emit("gpsimd", lambda e, _s=sl, _t=t: e.dma_start(
                    out=xnb[:, _s], in_=Yd[:, :, _t * TMOE:(_t + 1) * TMOE]),
                    wait=w, inc=snm, incby=16)
                lc_xn = (snm, ph.c(snm))
                for c in range(16):
                    bankG = (pA, pC)[c % 2]
                    bankU = (pB, pD)[c % 2]
                    for kc in range(DC):
                        wm = []
                        if kc == 0:
                            wm = [lc_xn, lc_w13[e_]]
                            key = (e_, t, c - 2)
                            if key in se_g:
                                wm.append(("se", se_g[key]))
                            elif c < 2 and (e_, t) != (0, 0):
                                wm.append(("se", prev_last_se))
                        ph.emit("tensor", lambda e, _b=bankG, _k=kc, _c=c, _s=sl, _sl3=s13: e.matmul(
                            _b[:, 0:TMOE], w13b[:, _sl3, _k, _c, :],
                            xnb[:, _s, _k, :], start=(_k == 0), stop=(_k == DC - 1)),
                            wait=wm, inc="pe" if kc == DC - 1 else None)
                    pe_gate = ph.c("pe")
                    for kc in range(DC):
                        wm = []
                        if kc == 0:
                            key = (e_, t, c - 2)
                            if key in ve_u:
                                wm.append(("ve", ve_u[key]))
                            elif c < 2 and (e_, t) != (0, 0):
                                wm.append(("ve", prev_last_veu))
                        ph.emit("tensor", lambda e, _b=bankU, _k=kc, _c=c, _s=sl, _sl3=s13: e.matmul(
                            _b[:, 0:TMOE], w13b[:, _sl3, _k, _c + 16, :],
                            xnb[:, _s, _k, :], start=(_k == 0), stop=(_k == DC - 1)),
                            wait=wm, inc="pe" if kc == DC - 1 else None)
                    pe_up = ph.c("pe")
                    ssl = c % 2
                    ws = [("pe", pe_gate)]
                    key = (e_, t, c - 2)
                    if key in ve_u:
                        ws.append(("ve", ve_u[key]))
                    elif c < 2 and (e_, t) != (0, 0):
                        ws.append(("ve", prev_last_veu))
                    ph.emit("scalar", lambda e, _b=bankG, _s=ssl: e.activation(
                        stb[:, _s, :], _b[:, 0:TMOE], AF.Silu),
                        wait=ws, inc="se")
                    se_g[(e_, t, c)] = ph.c("se")
                    ph.emit("vector", lambda e, _b=bankU, _c=c, _s=ssl: e.scalar_tensor_tensor(
                        actb[:, _c, :], _b[:, 0:TMOE], 1.0, stb[:, _s, :],
                        ALU.bypass, ALU.mult),
                        wait=[("se", ph.c("se")), ("pe", pe_up)], inc="ve")
                    ve_u[(e_, t, c)] = ph.c("ve")
                prev_last_se = se_g[(e_, t, 15)]
                prev_last_veu = ve_u[(e_, t, 15)]
                ve_act = ph.c("ve")
                pe_xn_done[(e_, t)] = 0
                for grp in range(TMOE // 128):
                    gabs = t * (TMOE // 128) + grp
                    for half in range(2):
                        for kc in range(16):
                            wm = []
                            if kc == 0:
                                wm = [("ve", ve_act), lc_w2[e_]]
                                if oslot_i >= 1:
                                    wm.append(("ve", ve_o_evac[oslot_i - 1]))
                            ph.emit("tensor", lambda e, _k=kc, _h=half, _g=grp: e.matmul(
                                pO[:, _h * 512:(_h + 1) * 512],
                                actb[:, _k, _g * 128:(_g + 1) * 128],
                                w2b[:, _k, _h * 512:(_h + 1) * 512],
                                start=(_k == 0), stop=(_k == 15)),
                                wait=wm, inc="pe" if kc == 15 else None)
                    pe_o = ph.c("pe")
                    osl = oslot_i % 4
                    w = [("pe", pe_o)]
                    if oslot_i >= 4:
                        w.append(sg_out[oslot_i - 4])
                    ph.emit("vector", lambda e, _o=osl, _g=gabs, _p=EPOS[e_]: e.tensor_scalar_mul(
                        outsb[:, _o, :], pO[:], combt[:, _g, _p:_p + 1]),
                        wait=w, inc="ve")
                    if oslot_i == 0:
                        ve_o_evac = {}
                    ve_o_evac[oslot_i] = ph.c("ve")
                    snm = ("ra", "rb", "rc", "rd")[osl]
                    ph.emit("gpsimd", lambda e, _o=osl, _g=gabs, _e=e_: e.dma_start(
                        out=X1d[_g * 128:(_g + 1) * 128, :], in_=outsb[:, _o, :],
                        accum_op=ALU.add),
                        wait=[("ve", ph.c("ve"))], inc=snm, incby=16)
                    sg_out[oslot_i] = (snm, ph.c(snm))
                    oslot_i += 1
                pe_xn_done[(e_, t)] = ph.c("pe")
            pe_w13_done[e_] = ph.c("pe")
            pe_w2_done[e_] = ph.c("pe")
        for nmx in ("ra", "rb", "rc", "rd"):
            ph.wait("gpsimd", (nmx, ph.c(nmx)))
        ph.barrier()
        ph.run()

    T2 = 256
    NT2 = S // T2
    bankcred = {}

    def bwait(*banks):
        return [bankcred[b] for b in banks if b in bankcred]

    with contextlib.ExitStack() as p2stk:
        WS = p2stk.enter_context(nc.sbuf_tensor("p2_ws", [128, 2, 32768], BF16))
        ycb = p2stk.enter_context(nc.sbuf_tensor("p2_yc", [128, 2, 8, T2], BF16))
        attb = p2stk.enter_context(nc.sbuf_tensor("p2_att", [128, 8, T2], BF16))
        tmin = p2stk.enter_context(nc.sbuf_tensor("p2_tm", [128, T2], F32))
        emin = p2stk.enter_context(nc.sbuf_tensor("p2_em", [128, T2], F32))
        kfb = p2stk.enter_context(nc.sbuf_tensor("p2_kf", [128, T2], F32))
        vbb = p2stk.enter_context(nc.sbuf_tensor("p2_vb", [128, T2], F32))
        prb = p2stk.enter_context(nc.sbuf_tensor("p2_pr", [128, T2], F32))
        tgb = p2stk.enter_context(nc.sbuf_tensor("p2_tg", [128, 2, T2], F32))
        zbf = p2stk.enter_context(nc.sbuf_tensor("p2_zb", [128, 8, T2], F32))
        zb16 = p2stk.enter_context(nc.sbuf_tensor("p2_z16", [128, 8, T2], BF16))
        ymb16 = p2stk.enter_context(nc.sbuf_tensor("p2_ym16", [128, 2, 8, T2], BF16))
        hshr = p2stk.enter_context(nc.sbuf_tensor("p2_h", [128, 2, 32, T2], BF16))
        mrb = p2stk.enter_context(nc.sbuf_tensor("p2_mr", [128, T2], F32))
        msb = p2stk.enter_context(nc.sbuf_tensor("p2_ms", [1, T2], F32))
        qsb = p2stk.enter_context(nc.sbuf_tensor("p2_qs", [1, T2], F32))
        kvacc = p2stk.enter_context(nc.sbuf_tensor("p2_kv", [128, 8], F32))
        skacc = p2stk.enter_context(nc.sbuf_tensor("p2_sk", [128, 8], F32))
        redb = p2stk.enter_context(nc.sbuf_tensor("p2_red", [128, 2], F32))
        cbs = p2stk.enter_context(nc.sbuf_tensor("p2_cb", [128, 2, 104], F32))
        cbds = p2stk.enter_context(nc.sbuf_tensor("p2_cbd", [128, 2, 16], F32))
        pSM = p2stk.enter_context(nc.psum_tensor("pSM", [128, 512], F32))
        pSQ = p2stk.enter_context(nc.psum_tensor("pSQ", [128, 512], F32))
        ph = Ph(nc, sems, cnt)

        def wv_qkv(sl, half):
            return WS[:, sl, half * 8192:(half + 1) * 8192].rearrange(
                "p (a b c) -> p a b c", a=8, b=8, c=128)

        def wv_w1(sl):
            return WS[:, sl, :].rearrange("p (a b c) -> p a b c", a=8, b=32, c=128)

        def wv_w2(sl):
            return WS[:, sl, :].rearrange("p (a b c) -> p a b c", a=32, b=8, c=128)

        pe_pass_end = {}
        stY = {}
        stYM = {}
        stH = {}

        def load_pass_weights(gp):
            i, p = gp // 4, gp % 4
            sl = gp % 2
            snm = ("w0", "w1")[sl]
            w = [("pe", pe_pass_end[gp - 2])] if gp - 2 in pe_pass_end else []
            if p == 0:
                ph.emit("gpsimd", lambda e, _i=i, _s=sl: e.dma_start(
                    out=wv_qkv(_s, 0), in_=l0_wk[_i]), wait=w, inc=snm, incby=16)
                ph.emit("gpsimd", lambda e, _i=i, _s=sl: e.dma_start(
                    out=wv_qkv(_s, 1), in_=l0_wv[_i]), inc=snm, incby=16)
            elif p == 1:
                ph.emit("gpsimd", lambda e, _i=i, _s=sl: e.dma_start(
                    out=wv_qkv(_s, 0), in_=l0_wq[_i]), wait=w, inc=snm, incby=16)
                ph.emit("gpsimd", lambda e, _i=i, _s=sl: e.dma_start(
                    out=wv_qkv(_s, 1), in_=l0_wo[_i]), inc=snm, incby=16)
            elif p == 2:
                ph.emit("gpsimd", lambda e, _i=i, _s=sl: e.dma_start(
                    out=wv_w1(_s), in_=l0_w1[_i]), wait=w, inc=snm, incby=16)
            else:
                ph.emit("gpsimd", lambda e, _i=i, _s=sl: e.dma_start(
                    out=wv_w2(_s), in_=l0_w2[_i]), wait=w, inc=snm, incby=16)
            return (snm, ph.c(snm))

        def ln_fm_tail(i, t, csl, scol, bcol, store_dram, zdone_se, stsems,
                       st_prev):
            for mc in range(DC):
                w = [("ve", ph.c("ve"))] if mc == 0 else []
                if mc == 0:
                    w += bwait("pSM")
                ph.emit("tensor", lambda e, _m=mc: e.matmul(
                    pSM[0:1, 0:T2], onst[:], zb16[:, _m, :],
                    start=(_m == 0), stop=(_m == DC - 1)),
                    wait=w, inc="pe" if mc == DC - 1 else None)
            for mc in range(DC):
                w = [("se", zdone_se)] if mc == 0 else []
                if mc == 0:
                    w += bwait("pSQ")
                ph.emit("tensor", lambda e, _m=mc: e.matmul(
                    pSQ[0:1, 0:T2], onst[:], attb[:, _m, :],
                    start=(_m == 0), stop=(_m == DC - 1)),
                    wait=w, inc="pe" if mc == DC - 1 else None)
            pe_stats = ph.c("pe")
            ph.emit("vector", lambda e: e.tensor_copy(msb[0:1, 0:T2], pSM[0:1, 0:T2]),
                    wait=[("pe", pe_stats)])
            ph.emit("vector", lambda e: e.tensor_copy(qsb[0:1, 0:T2], pSQ[0:1, 0:T2]))
            ph.emit("vector", lambda e: e.drain(), inc="ve")
            bankcred["pSM"] = ("ve", ph.c("ve"))
            bankcred["pSQ"] = ("ve", ph.c("ve"))
            w = [("ve", ph.c("ve"))] + bwait("pO")
            ph.emit("tensor", lambda e: e.matmul(
                pO[:, 0:T2], onsr[:], msb[0:1, 0:T2], start=True, stop=True),
                wait=w, inc="pe")
            ph.emit("tensor", lambda e: e.matmul(
                pO[:, 512:512 + T2], onsr[:], qsb[0:1, 0:T2], start=True, stop=True),
                inc="pe")
            pe_bc = ph.c("pe")
            ph.emit("vector", lambda e: e.tensor_copy(mrb[:], pO[:, 0:T2]),
                    wait=[("pe", pe_bc)])
            ph.emit("vector", lambda e: e.drain())
            ph.emit("vector", lambda e: e.tensor_mul(tmin[:], mrb[:], mrb[:]))
            ph.emit("vector", lambda e: e.drain())
            ph.emit("vector", lambda e: e.tensor_sub(
                emin[:], pO[:, 512:512 + T2], tmin[:]), inc="ve")
            bankcred["pO"] = ("ve", ph.c("ve"))
            ph.emit("scalar", lambda e: e.activation(
                kfb[:], emin[:], AF.Sqrt, bias=1.0e-6),
                wait=[("ve", ph.c("ve"))], inc="se")
            ph.emit("vector", lambda e: e.reciprocal(vbb[:], kfb[:]),
                    wait=[("se", ph.c("se"))])
            ph.emit("vector", lambda e: e.drain())
            ssl = t % 2
            snm = stsems[ssl]
            for mp in range(DC // 2):
                m0, m1 = 2 * mp, 2 * mp + 1
                wn = [st_prev] if (mp == 0 and st_prev is not None) else []
                ph.emit("vector", lambda e, _m=m0: e.tensor_sub(
                    prb[:], zbf[:, _m, :], mrb[:]), wait=wn)
                ph.emit("vector", lambda e, _m=m1: e.tensor_sub(
                    tmin[:], zbf[:, _m, :], mrb[:]))
                ph.emit("vector", lambda e: e.drain())
                ph.emit("vector", lambda e: e.tensor_mul(emin[:], prb[:], vbb[:]))
                ph.emit("vector", lambda e: e.tensor_mul(kfb[:], tmin[:], vbb[:]))
                ph.emit("vector", lambda e: e.drain())
                for _m, src in ((m0, emin), (m1, kfb)):
                    ph.emit("vector", lambda e, _mm=_m, _sr=src, _cs=csl, _ss=ssl: e.tensor_scalar(
                        ymb16[:, _ss, _mm, :], _sr[:],
                        cbs[:, _cs, scol + _mm:scol + _mm + 1],
                        cbs[:, _cs, bcol + _mm:bcol + _mm + 1], ALU.mult, ALU.add),
                        inc="ve" if _m == DC - 1 else None)
            ve_norm = ph.c("ve")
            ph.emit("sync", lambda e, _t=t, _d=store_dram, _ss=ssl: e.dma_start(
                out=_d[:, :, _t * T2:(_t + 1) * T2], in_=ymb16[:, _ss]),
                wait=[("ve", ve_norm)], inc=snm, incby=16)
            return (snm, ph.c(snm))

        lw = {0: load_pass_weights(0), 1: load_pass_weights(1)}
        cbs_free = {}
        for i in range(nsl):
            csl = i % 2
            snm_cb = ("ca", "cb")[csl]
            wcb = [("ve", cbs_free[i - 2])] if i >= 2 else []
            ph.emit("sync", lambda e, _i=i, _c=csl: e.dma_start(
                out=cbs[:, _c, :], in_=l0_cb[:, _i, :]),
                wait=wcb, inc=snm_cb, incby=16)
            ld_cb = (snm_cb, ph.c(snm_cb))
            ph.wait("vector", ld_cb)
            ph.emit("vector", lambda e, _c=csl: e.tensor_scalar_add(
                cbds[:, _c, 0:8], cbs[:, _c, 0:8], 1.0))
            ph.emit("vector", lambda e, _c=csl: e.tensor_scalar_add(
                cbds[:, _c, 8:16], cbs[:, _c, 8:16], 1.0))
            ph.emit("vector", lambda e: e.drain(), inc="ve")

            gp = 4 * i
            slA = gp % 2
            if gp + 1 not in lw:
                lw[gp + 1] = load_pass_weights(gp + 1)
            ph.emit("vector", lambda e: e.memset(kvacc[:], 0.0))
            ph.emit("vector", lambda e: e.memset(skacc[:], 0.0))
            ph.emit("vector", lambda e: e.drain(), inc="ve")
            yc_cons = {}
            for t in range(NT2):
                sl = t % 2
                snm = ("da", "db")[sl]
                w = []
                if t >= 2:
                    w = [("ve", yc_cons[t - 2])]
                elif 4 * i - 1 in pe_pass_end:
                    w = [("ve", prev_ycb_free)]
                if (i - 1, t) in stY:
                    w.append(stY[(i - 1, t)])
                if i == 0:
                    snm = ("qa", "qb")[sl]
                    ph.emit("gpsimd", lambda e, _s=sl, _t=t: e.dma_start(
                        out=ycb[:, _s], in_=Yd[:, :, _t * T2:(_t + 1) * T2]),
                        wait=w, inc=snm, incby=16)
                else:
                    ph.emit("sync", lambda e, _s=sl, _t=t: e.dma_start(
                        out=ycb[:, _s], in_=Ybd[:, :, _t * T2:(_t + 1) * T2]),
                        wait=w, inc=snm, incby=16)
                lc_y = (snm, ph.c(snm))
                for mc in range(DC):
                    bK = ("pA", "pB")[mc % 2]
                    bV = ("pC", "pD")[mc % 2]
                    pbK = (pA, pB)[mc % 2]
                    pbV = (pC, pD)[mc % 2]
                    for kc in range(DC):
                        w = []
                        if kc == 0:
                            w = [lc_y, lw[gp]] + bwait(bK)
                        ph.emit("tensor", lambda e, _b=pbK, _k=kc, _m=mc, _s=sl, _sa=slA: e.matmul(
                            _b[:, 0:T2], wv_qkv(_sa, 0)[:, _k, _m, :],
                            ycb[:, _s, _k, :], start=(_k == 0), stop=(_k == DC - 1)),
                            wait=w, inc="pe" if kc == DC - 1 else None)
                    pe_K = ph.c("pe")
                    for kc in range(DC):
                        w = []
                        if kc == 0:
                            w = bwait(bV)
                        ph.emit("tensor", lambda e, _b=pbV, _k=kc, _m=mc, _s=sl, _sa=slA: e.matmul(
                            _b[:, 0:T2], wv_qkv(_sa, 1)[:, _k, _m, :],
                            ycb[:, _s, _k, :], start=(_k == 0), stop=(_k == DC - 1)),
                            wait=w, inc="pe" if kc == DC - 1 else None)
                    pe_V = ph.c("pe")
                    ph.emit("vector", lambda e, _b=pbK, _c=csl, _m=mc: e.tensor_scalar(
                        tmin[:], _b[:, 0:T2], cbs[:, _c, 8 + _m:9 + _m], 0.0,
                        ALU.add, ALU.min), wait=[("pe", pe_K)], inc="ve")
                    ph.emit("scalar", lambda e: e.activation(emin[:], tmin[:], AF.Exp),
                            wait=[("ve", ph.c("ve"))], inc="se")
                    ph.emit("vector", lambda e, _b=pbK, _c=csl, _m=mc: e.scalar_tensor_tensor(
                        kfb[:], _b[:, 0:T2], cbds[:, _c, 8 + _m:9 + _m], emin[:],
                        ALU.add, ALU.max), wait=[("se", ph.c("se"))], inc="ve")
                    bankcred[bK] = ("ve", ph.c("ve"))
                    ph.emit("vector", lambda e, _b=pbV, _c=csl, _m=mc: e.tensor_scalar_add(
                        vbb[:], _b[:, 0:T2], cbs[:, _c, 16 + _m:17 + _m]),
                        wait=[("pe", pe_V)], inc="ve")
                    bankcred[bV] = ("ve", ph.c("ve"))
                    ph.emit("vector", lambda e: e.drain())
                    ph.emit("vector", lambda e: e.tensor_mul(prb[:], kfb[:], vbb[:]))
                    ph.emit("vector", lambda e: e.drain())
                    ph.emit("vector", lambda e: e.reduce_sum(redb[:, 0:1], prb[:], axis=AX.X))
                    ph.emit("vector", lambda e: e.reduce_sum(redb[:, 1:2], kfb[:], axis=AX.X))
                    ph.emit("vector", lambda e: e.drain())
                    ph.emit("vector", lambda e, _m=mc: e.tensor_add(
                        kvacc[:, _m:_m + 1], kvacc[:, _m:_m + 1], redb[:, 0:1]))
                    ph.emit("vector", lambda e, _m=mc: e.tensor_add(
                        skacc[:, _m:_m + 1], skacc[:, _m:_m + 1], redb[:, 1:2]),
                        inc="ve")
                yc_cons[t] = ph.c("ve")
            pe_pass_end[gp] = ph.c("pe")
            prev_ycb_free = ph.c("ve")
            ph.emit("vector", lambda e: e.drain(), inc="ve")

            gp = 4 * i + 1
            slB = gp % 2
            if gp + 1 not in lw:
                lw[gp + 1] = load_pass_weights(gp + 1)
            yc_cons = {}
            ve_z = {}
            pe_wo_end = 0
            for t in range(NT2):
                sl = t % 2
                snm = ("da", "db")[sl]
                w = []
                if t >= 2:
                    w = [("ve", yc_cons[t - 2])]
                else:
                    w = [("ve", prev_ycb_free)]
                if (i - 1, t) in stY:
                    w.append(stY[(i - 1, t)])
                if i == 0:
                    snm = ("qa", "qb")[sl]
                    ph.emit("gpsimd", lambda e, _s=sl, _t=t: e.dma_start(
                        out=ycb[:, _s], in_=Yd[:, :, _t * T2:(_t + 1) * T2]),
                        wait=w, inc=snm, incby=16)
                else:
                    ph.emit("sync", lambda e, _s=sl, _t=t: e.dma_start(
                        out=ycb[:, _s], in_=Ybd[:, :, _t * T2:(_t + 1) * T2]),
                        wait=w, inc=snm, incby=16)
                lc_y = (snm, ph.c(snm))
                for mc in range(DC):
                    bQ = ("pA", "pB")[mc % 2]
                    pbQ = (pA, pB)[mc % 2]
                    for kc in range(DC):
                        w = []
                        if kc == 0:
                            w = [lc_y, lw[gp]] + bwait(bQ)
                        ph.emit("tensor", lambda e, _b=pbQ, _k=kc, _m=mc, _s=sl, _sb=slB: e.matmul(
                            _b[:, 0:T2], wv_qkv(_sb, 0)[:, _k, _m, :],
                            ycb[:, _s, _k, :], start=(_k == 0), stop=(_k == DC - 1)),
                            wait=w, inc="pe" if kc == DC - 1 else None)
                    pe_Q = ph.c("pe")
                    ph.emit("vector", lambda e, _b=pbQ, _c=csl, _m=mc: e.tensor_scalar(
                        tmin[:], _b[:, 0:T2], cbs[:, _c, 0 + _m:1 + _m], 0.0,
                        ALU.add, ALU.min), wait=[("pe", pe_Q)], inc="ve")
                    ph.emit("scalar", lambda e: e.activation(emin[:], tmin[:], AF.Exp),
                            wait=[("ve", ph.c("ve"))], inc="se")
                    ph.emit("vector", lambda e, _b=pbQ, _c=csl, _m=mc: e.scalar_tensor_tensor(
                        kfb[:], _b[:, 0:T2], cbds[:, _c, 0 + _m:1 + _m], emin[:],
                        ALU.add, ALU.max), wait=[("se", ph.c("se"))], inc="ve")
                    bankcred[bQ] = ("ve", ph.c("ve"))
                    ph.emit("vector", lambda e: e.drain())
                    ph.emit("vector", lambda e, _m=mc: e.tensor_scalar(
                        vbb[:], kfb[:], skacc[:, _m:_m + 1], 1.0e-6,
                        ALU.mult, ALU.add))
                    ph.emit("vector", lambda e: e.drain())
                    ph.emit("vector", lambda e: e.reciprocal(prb[:], vbb[:]))
                    ph.emit("vector", lambda e: e.drain())
                    wat = [("pe", pe_wo_end)] if (mc == 0 and pe_wo_end) else []
                    ph.emit("vector", lambda e, _m=mc: e.scalar_tensor_tensor(
                        attb[:, _m, :], kfb[:], kvacc[:, _m:_m + 1], prb[:],
                        ALU.mult, ALU.mult), wait=wat, inc="ve")
                ve_att = ph.c("ve")
                pe_wo_prev = ph.c("pe")
                zdone_se = 0
                for mc in range(DC):
                    bW = ("pC", "pD")[mc % 2]
                    pbW = (pC, pD)[mc % 2]
                    for kc in range(DC):
                        w = []
                        if kc == 0:
                            w = [("ve", ve_att)] + bwait(bW)
                        ph.emit("tensor", lambda e, _b=pbW, _k=kc, _m=mc, _sb=slB: e.matmul(
                            _b[:, 0:T2], wv_qkv(_sb, 1)[:, _k, _m, :],
                            attb[:, _k, :], start=(_k == 0), stop=(_k == DC - 1)),
                            wait=w, inc="pe" if kc == DC - 1 else None)
                    pe_W = ph.c("pe")
                    tsl = mc % 2
                    w = [("pe", pe_W)]
                    zkey = mc - 2 if mc >= 2 else mc + 6
                    if zkey in ve_z:
                        w.append(("ve", ve_z[zkey]))
                    ph.emit("scalar", lambda e, _b=pbW, _c=csl, _m=mc, _t=tsl: e.activation(
                        tgb[:, _t, :], _b[:, 0:T2], AF.Gelu_apprx_tanh,
                        bias=cbs[:, _c, 24 + _m:25 + _m]), wait=w, inc="se")
                    bankcred[bW] = ("se", ph.c("se"))
                    ph.emit("vector", lambda e, _m=mc, _t=tsl, _s=sl: e.tensor_add(
                        zbf[:, _m, :], tgb[:, _t, :], ycb[:, _s, _m, :]),
                        wait=[("se", ph.c("se"))], inc="ve")
                    ve_z[mc] = ph.c("ve")
                pe_wo_end = pe_W
                ph.emit("vector", lambda e: e.drain())
                for mc in range(DC):
                    ph.emit("vector", lambda e, _m=mc: e.tensor_copy(
                        zb16[:, _m, :], zbf[:, _m, :]),
                        inc="ve" if mc == DC - 1 else None)
                vz16 = ph.c("ve")
                for mc in range(DC):
                    wsq = [("ve", vz16), ("pe", pe_wo_end)] if mc == 0 else []
                    ph.emit("scalar", lambda e, _m=mc: e.activation(
                        attb[:, _m, :], zbf[:, _m, :], AF.Square),
                        wait=wsq, inc="se" if mc == DC - 1 else None)
                zdone_se = ph.c("se")
                yc_cons[t] = ph.c("ve")
                stp = stYM.get((i, t - 2)) or stY.get((i - 1, NT2 - 2 + t))
                stYM[(i, t)] = ln_fm_tail(i, t, csl, 32, 40, YMd, zdone_se,
                                          ("ea", "eb"), stp)
                pe_wo_end = ph.c("pe")
            pe_pass_end[gp] = ph.c("pe")
            prev_ycb_free = ph.c("ve")

            gp = 4 * i + 2
            slC = gp % 2
            if gp + 1 not in lw:
                lw[gp + 1] = load_pass_weights(gp + 1)
            yc_cons = {}
            st_h = {}
            for t in range(NT2):
                sl = t % 2
                w = []
                if t >= 2:
                    w = [("pe", yc_cons[t - 2])]
                else:
                    w = [("ve", prev_ycb_free)]
                w.append(stYM[(i, t)])
                snm = ("da", "db")[sl]
                ph.emit("sync", lambda e, _s=sl, _t=t: e.dma_start(
                    out=ycb[:, _s], in_=YMd[:, :, _t * T2:(_t + 1) * T2]),
                    wait=w, inc=snm, incby=16)
                lc_y = (snm, ph.c(snm))
                for mc in range(32):
                    bW = ("pA", "pB")[mc % 2]
                    pbW = (pA, pB)[mc % 2]
                    for kc in range(DC):
                        w = []
                        if kc == 0:
                            w = [lc_y, lw[gp]] + bwait(bW)
                        ph.emit("tensor", lambda e, _b=pbW, _k=kc, _m=mc, _s=sl, _sc=slC: e.matmul(
                            _b[:, 0:T2], wv_w1(_sc)[:, _k, _m, :],
                            ycb[:, _s, _k, :], start=(_k == 0), stop=(_k == DC - 1)),
                            wait=w, inc="pe" if kc == DC - 1 else None)
                    pe_W = ph.c("pe")
                    w = [("pe", pe_W)]
                    if t >= 2 and mc == 0:
                        w.append(st_h[t - 2])
                    ph.emit("scalar", lambda e, _b=pbW, _c=csl, _m=mc, _s=sl: e.activation(
                        hshr[:, _s, _m, :], _b[:, 0:T2], AF.Gelu_apprx_tanh,
                        bias=cbs[:, _c, 64 + _m:65 + _m]), wait=w, inc="se")
                    bankcred[bW] = ("se", ph.c("se"))
                yc_cons[t] = ph.c("pe")
                snmh = ("ha", "hb")[sl]
                ph.emit("sync", lambda e, _s=sl, _t=t: e.dma_start(
                    out=Hd[:, :, _t * T2:(_t + 1) * T2], in_=hshr[:, _s]),
                    wait=[("se", ph.c("se"))], inc=snmh, incby=16)
                st_h[t] = (snmh, ph.c(snmh))
                stH[(i, t)] = (snmh, ph.c(snmh))
            pe_pass_end[gp] = ph.c("pe")
            prev_ycb_free = ph.c("pe")

            gp = 4 * i + 3
            slD = gp % 2
            if gp + 1 not in lw and gp + 1 < 4 * NSL:
                lw[gp + 1] = load_pass_weights(gp + 1)
            hb_cons = {}
            ym_cons = {}
            ve_z = {}
            for t in range(NT2):
                sl = t % 2
                w = [stH[(i, t)]]
                if t >= 2:
                    w.append(("pe", hb_cons[t - 2]))
                else:
                    w.append(stH[(i, NT2 - 2 + t)])
                snmg = ("ga", "gb")[sl]
                ph.emit("sync", lambda e, _s=sl, _t=t: e.dma_start(
                    out=hshr[:, _s], in_=Hd[:, :, _t * T2:(_t + 1) * T2]),
                    wait=w, inc=snmg, incby=16)
                ld_h = (snmg, ph.c(snmg))
                w = [stYM[(i, t)]]
                if t >= 2:
                    w.append(("ve", ym_cons[t - 2]))
                else:
                    w.append(("pe", pe_pass_end[gp - 1]))
                snm = ("da", "db")[sl]
                ph.emit("sync", lambda e, _s=sl, _t=t: e.dma_start(
                    out=ycb[:, _s], in_=YMd[:, :, _t * T2:(_t + 1) * T2]),
                    wait=w, inc=snm, incby=16)
                lc_y = (snm, ph.c(snm))
                zdone_se = 0
                for mc in range(DC):
                    bW = ("pA", "pB")[mc % 2]
                    pbW = (pA, pB)[mc % 2]
                    for kc in range(32):
                        w = []
                        if kc == 0:
                            w = [ld_h, lc_y, lw[gp]] + bwait(bW)
                        ph.emit("tensor", lambda e, _b=pbW, _k=kc, _m=mc, _s=sl, _sd=slD: e.matmul(
                            _b[:, 0:T2], wv_w2(_sd)[:, _k, _m, :],
                            hshr[:, _s, _k, :], start=(_k == 0), stop=(_k == 31)),
                            wait=w, inc="pe" if kc == 31 else None)
                    pe_W = ph.c("pe")
                    tsl = mc % 2
                    w = [("pe", pe_W)]
                    zkey = mc - 2 if mc >= 2 else mc + 6
                    if zkey in ve_z:
                        w.append(("ve", ve_z[zkey]))
                    ph.emit("scalar", lambda e, _b=pbW, _c=csl, _m=mc, _t=tsl: e.activation(
                        tgb[:, _t, :], _b[:, 0:T2], AF.Gelu_apprx_tanh,
                        bias=cbs[:, _c, 96 + _m:97 + _m]), wait=w, inc="se")
                    bankcred[bW] = ("se", ph.c("se"))
                    ph.emit("vector", lambda e, _m=mc, _t=tsl, _s=sl: e.tensor_add(
                        zbf[:, _m, :], tgb[:, _t, :], ycb[:, _s, _m, :]),
                        wait=[("se", ph.c("se"))], inc="ve")
                    ve_z[mc] = ph.c("ve")
                pe_w2_end = pe_W
                ph.emit("vector", lambda e: e.drain())
                for mc in range(DC):
                    ph.emit("vector", lambda e, _m=mc: e.tensor_copy(
                        zb16[:, _m, :], zbf[:, _m, :]),
                        inc="ve" if mc == DC - 1 else None)
                vz16 = ph.c("ve")
                for mc in range(DC):
                    wsq = [("ve", vz16), ("pe", pe_w2_end)] if mc == 0 else []
                    ph.emit("scalar", lambda e, _m=mc: e.activation(
                        attb[:, _m, :], zbf[:, _m, :], AF.Square),
                        wait=wsq, inc="se" if mc == DC - 1 else None)
                zdone_se = ph.c("se")
                hb_cons[t] = ph.c("pe")
                ym_cons[t] = ph.c("ve")
                stp = stY.get((i, t - 2)) or stYM.get((i, NT2 - 2 + t))
                stY[(i, t)] = ln_fm_tail(i, t, csl, 48, 56, Ybd, zdone_se,
                                         ("fa", "fb"), stp)
            pe_pass_end[gp] = ph.c("pe")
            prev_ycb_free = ph.c("ve")
            cbs_free[i] = ph.c("ve")
        for nmx in ("ea", "eb", "fa", "fb", "ha", "hb"):
            ph.wait("sync", (nmx, ph.c(nmx)))
        ph.barrier()
        ph.run()

    with contextlib.ExitStack() as p3s:
        x1b = p3s.enter_context(nc.sbuf_tensor("p3_x1", [128, 2, D], F32))
        x1n = p3s.enter_context(nc.sbuf_tensor("p3_x1n", [128, D], F32))
        ytb = p3s.enter_context(nc.sbuf_tensor("p3_yt", [128, 2, 8, 128], BF16))
        pTY = p3s.enter_context(nc.psum_tensor("pTY", [128, 1024], BF16))
        sqd3 = p3s.enter_context(nc.sbuf_tensor("p3_sq", [128, D], F32))
        tp3 = p3s.enter_context(nc.sbuf_tensor("p3_tp", [128, 8], F32))
        sd3 = p3s.enter_context(nc.sbuf_tensor("p3_sd", [128, 1], F32))
        rr3 = p3s.enter_context(nc.sbuf_tensor("p3_rr", [128, 1], F32))
        xnt = p3s.enter_context(nc.sbuf_tensor("p3_xnt", [128, D], BF16))
        xnf = p3s.enter_context(nc.sbuf_tensor("p3_xnf", [128, 2, 8, 128], BF16))
        pT3 = p3s.enter_context(nc.psum_tensor("pT3", [128, 1024], BF16))
        ph = Ph(nc, sems, cnt)
        ld_c = {}
        ve_c = {}
        st_c = {}
        pe_xn = {}
        stx1_p3 = {}
        for t in range(32):
            sl = t % 2
            snd = ("da", "db")[sl]
            sng = ("ga", "gb")[sl]
            w = []
            if t >= 2:
                w.append(("ve", ve_c[t - 2]))
            ph.emit("sync", lambda e, _s=sl, _t=t: e.dma_start(
                out=x1b[:, _s], in_=X1d[_t * 128:(_t + 1) * 128, :]),
                wait=w, inc=snd, incby=16)
            ld_x1p = (snd, ph.c(snd))
            wy = [("pe", pe_xn[t - 2])] if t >= 2 else []
            ph.emit("sync", lambda e, _s=sl, _t=t: e.dma_start(
                out=ytb[:, _s], in_=Ybd[:, :, _t * 128:(_t + 1) * 128]),
                wait=wy, inc=sng, incby=16)
            ld_c[t] = (sng, ph.c(sng))
            for c in range(DC):
                w = []
                if c == 0:
                    w = [ld_x1p, ld_c[t]] + bwait("pTY")
                ph.emit("tensor", lambda e, _c=c, _s=sl: e.transpose(
                    pTY[:, _c * 128:(_c + 1) * 128],
                    ytb[:, _s, _c, :], idb[:]),
                    wait=w, inc="pe" if c == DC - 1 else None)
            pe_now = ph.c("pe")
            for c2 in range(DC):
                wtt = [("pe", pe_now)] if c2 == 0 else []
                if c2 == 0 and t >= 1:
                    wtt.append(stx1_p3[t - 1])
                ph.emit("vector", lambda e, _c2=c2, _s=sl: e.tensor_add(
                    x1n[:, _c2 * 128:(_c2 + 1) * 128],
                    x1b[:, _s, _c2 * 128:(_c2 + 1) * 128],
                    pTY[:, _c2 * 128:(_c2 + 1) * 128]),
                    wait=wtt,
                    inc="ve" if c2 == DC - 1 else None)
            bankcred["pTY"] = ("ve", ph.c("ve"))
            ph.emit("sync", lambda e, _t=t: e.dma_start(
                out=X1d[_t * 128:(_t + 1) * 128, :], in_=x1n[:]),
                wait=[("ve", ph.c("ve"))], inc="ea", incby=16)
            stx1_p3[t] = ("ea", ph.c("ea"))
            _ln_tm(ph, x1n[:], sqd3, tp3, sd3, rr3, 1.0e-5,
                   wait_in=[("ve", ph.c("ve"))])
            wx = [("pe", pe_xn[t - 1])] if t >= 1 else []
            ph.emit("vector", lambda e: e.tensor_scalar(
                xnt[:], x1n[:], tp3[:, 4:5], rr3[:], ALU.subtract, ALU.mult),
                wait=wx, inc="ve")
            ve_now = ph.c("ve")
            w = [("ve", ve_now)]
            if t >= 1:
                w.append(("ve", xnf_ev[t - 1]))
            for c in range(DC):
                ph.emit("tensor", lambda e, _c=c: e.transpose(
                    pT3[:, _c * 128:(_c + 1) * 128],
                    xnt[:, _c * 128:(_c + 1) * 128], idb[:]),
                    wait=w if c == 0 else (), inc="pe" if c == DC - 1 else None)
            pe_xn[t] = ph.c("pe")
            wv = [("pe", pe_xn[t])]
            if t >= 2:
                wv.append(st_c[t - 2])
            ph.emit("vector", lambda e, _s=sl: e.tensor_copy(
                xnf[:, _s].rearrange("p a b -> p (a b)"), pT3[:]),
                wait=wv, inc="ve")
            if t == 0:
                xnf_ev = {}
            xnf_ev[t] = ph.c("ve")
            snf = ("fa", "fb")[sl]
            ph.emit("sync", lambda e, _s=sl, _t=t: e.dma_start(
                out=XN2d[:, :, _t * 128:(_t + 1) * 128], in_=xnf[:, _s]),
                wait=[("ve", ph.c("ve"))], inc=snf, incby=16)
            st_c[t] = (snf, ph.c(snf))
            ve_c[t] = ph.c("ve")
        for nmx in ("ea", "fa", "fb"):
            ph.wait("sync", (nmx, ph.c(nmx)))
        ph.barrier()
        ph.run()

    with contextlib.ExitStack() as p4s:
        Kfm = p4s.enter_context(nc.sbuf_tensor("p4_k", [128, 8, S], BF16))
        Vtm = p4s.enter_context(nc.sbuf_tensor("p4_v", [128, 32, 1024], BF16))
        xnb4 = p4s.enter_context(nc.sbuf_tensor("p4_xn", [128, 2, 8, 512], BF16))
        pT4 = p4s.enter_context(nc.psum_tensor("pT4", [128, 128], BF16))
        p4i = p4s.enter_context(contextlib.ExitStack())
        wqb = p4i.enter_context(nc.sbuf_tensor("p4_wq", [128, 8, 8, 128], BF16))
        wkb = p4i.enter_context(nc.sbuf_tensor("p4_wk", [128, 8, 8, 128], BF16))
        wvb = p4i.enter_context(nc.sbuf_tensor("p4_wv", [128, 8, 1024], BF16))
        ph = Ph(nc, sems, cnt)
        for dst, src in ((wqb, l1_wq), (wkb, l1_wk), (wvb, l1_wv)):
            ph.emit("gpsimd", lambda e, _d=dst, _s=src: e.dma_start(
                out=_d[:], in_=_s), inc="lc", incby=16)
        lw4 = ph.c("lc")
        xn_cons = {}
        for t in range(8):
            sl = t % 2
            snm = ("da", "db")[sl]
            w = [("pe", xn_cons[t - 2])] if t >= 2 else []
            ph.emit("sync", lambda e, _s=sl, _t=t: e.dma_start(
                out=xnb4[:, _s], in_=XN2d[:, :, _t * 512:(_t + 1) * 512]),
                wait=w, inc=snm, incby=16)
            ld_x = (snm, ph.c(snm))
            for mc in range(DC):
                bK = ("pA", "pB")[mc % 2]
                pbK = (pA, pB)[mc % 2]
                for kc in range(DC):
                    w = []
                    if kc == 0:
                        w = [ld_x, ("lc", lw4)] + bwait(bK)
                    ph.emit("tensor", lambda e, _b=pbK, _k=kc, _m=mc, _s=sl: e.matmul(
                        _b[:, 0:512], wkb[:, _k, _m, :], xnb4[:, _s, _k, :],
                        start=(_k == 0), stop=(_k == DC - 1)),
                        wait=w, inc="pe" if kc == DC - 1 else None)
                ph.emit("vector", lambda e, _b=pbK, _m=mc, _t=t: e.tensor_copy(
                    Kfm[:, _m, _t * 512:(_t + 1) * 512], _b[:, 0:512]),
                    wait=[("pe", ph.c("pe"))], inc="ve")
                bankcred[bK] = ("ve", ph.c("ve"))
            for g in range(4):
                gabs = t * 4 + g
                for half in range(2):
                    for kc in range(DC):
                        w = []
                        if kc == 0 and half == 0:
                            w = bwait("pO")
                        ph.emit("tensor", lambda e, _k=kc, _h=half, _g=g, _s=sl: e.matmul(
                            pO[:, _h * 512:(_h + 1) * 512],
                            xnb4[:, _s, _k, _g * 128:(_g + 1) * 128],
                            wvb[:, _k, _h * 512:(_h + 1) * 512],
                            start=(_k == 0), stop=(_k == DC - 1)),
                            wait=w, inc="pe" if (kc == DC - 1 and half == 1) else None)
                ph.emit("vector", lambda e, _g=gabs: e.tensor_copy(
                    Vtm[:, _g, :], pO[:]), wait=[("pe", ph.c("pe"))], inc="ve")
                bankcred["pO"] = ("ve", ph.c("ve"))
            xn_cons[t] = ph.c("pe")
        pe_p4a_end = ph.c("pe")
        ph.barrier()
        ph.run()
        p4i.close()
        wqb = p4s.enter_context(nc.sbuf_tensor("p4_wq2", [128, 8, 8, 128], BF16))
        qfm = p4s.enter_context(nc.sbuf_tensor("p4_q", [128, 8, 512], BF16))
        psb = p4s.enter_context(nc.sbuf_tensor("p4_p", [128, 128], BF16))
        ptb = p4s.enter_context(nc.sbuf_tensor("p4_pt", [128, 128], BF16))
        osb = p4s.enter_context(nc.sbuf_tensor("p4_o", [128, 2, 1024], F32))
        oac = p4s.enter_context(nc.sbuf_tensor("p4_oa", [128, 1024], F32))
        mst = p4s.enter_context(nc.sbuf_tensor("p4_m", [128, 8], F32))
        ph = Ph(nc, sems, cnt)
        ph.emit("gpsimd", lambda e: e.dma_start(out=wqb[:], in_=l1_wq),
                inc="w0", incby=16)
        ph.wait("tensor", ("w0", ph.c("w0")))
        sg_o = {}
        pe_tr_prev = 0
        osb_free = None
        for qc in range(32):
            qg, qi = qc // 4, qc % 4
            if qi == 0:
                sl = qg % 2
                snm = ("da", "db")[sl]
                w = ([("pe", xn_cons[qg - 2])] if qg >= 2
                     else [("pe", pe_p4a_end)])
                ph.emit("sync", lambda e, _s=sl, _g=qg: e.dma_start(
                    out=xnb4[:, _s], in_=XN2d[:, :, _g * 512:(_g + 1) * 512]),
                    wait=w, inc=snm, incby=16)
                ld_x = (snm, ph.c(snm))
                for mc in range(DC):
                    bQ = ("pC", "pD")[mc % 2]
                    pbQ = (pC, pD)[mc % 2]
                    for kc in range(DC):
                        w = []
                        if kc == 0:
                            w = [ld_x] + bwait(bQ)
                        ph.emit("tensor", lambda e, _b=pbQ, _k=kc, _m=mc, _s=sl: e.matmul(
                            _b[:, 0:512], wqb[:, _k, _m, :], xnb4[:, _s, _k, :],
                            start=(_k == 0), stop=(_k == DC - 1)),
                            wait=w, inc="pe" if kc == DC - 1 else None)
                    ph.emit("vector", lambda e, _b=pbQ, _m=mc: e.tensor_copy(
                        qfm[:, _m, :], _b[:, 0:512]),
                        wait=[("pe", ph.c("pe"))], inc="ve")
                    bankcred[bQ] = ("ve", ph.c("ve"))
                xn_cons[qg] = ph.c("pe")
                ve_q = ph.c("ve")
            ph.emit("vector", lambda e: e.memset(mst[:, 0:1], -1.0e30))
            ph.emit("vector", lambda e: e.memset(mst[:, 1:2], 0.0))
            ph.emit("vector", lambda e: e.memset(oac[:], 0.0),
                    wait=[osb_free] if osb_free else [])
            ph.emit("vector", lambda e: e.drain(), inc="ve")
            for kb in range(qc + 1):
                bS = ("pA", "pB")[kb % 2]
                pbS = (pA, pB)[kb % 2]
                for kc in range(DC):
                    w = []
                    if kc == 0:
                        w = [("ve", ve_q)] + bwait(bS)
                    ph.emit("tensor", lambda e, _b=pbS, _k=kc, _kb=kb, _qi=qi: e.matmul(
                        _b[:, 0:128], qfm[:, _k, _qi * 128:(_qi + 1) * 128],
                        Kfm[:, _k, _kb * 128:(_kb + 1) * 128],
                        start=(_k == 0), stop=(_k == DC - 1)),
                        wait=w, inc="pe" if kc == DC - 1 else None)
                pe_s = ph.c("pe")
                if kb == qc:
                    ph.emit("vector", lambda e, _b=pbS: e.tensor_add(
                        _b[:, 0:128], _b[:, 0:128], caus[:]),
                        wait=[("pe", pe_s)])
                    ph.emit("vector", lambda e: e.drain(), inc="ve")
                    wrm = [("ve", ph.c("ve"))]
                else:
                    wrm = [("pe", pe_s)]
                ph.emit("vector", lambda e, _b=pbS: e.reduce_max(
                    mst[:, 2:3], _b[:, 0:128], axis=AX.X), wait=wrm)
                ph.emit("vector", lambda e: e.drain())
                ph.emit("vector", lambda e: e.tensor_scalar(
                    mst[:, 3:4], mst[:, 2:3], GAMMA, None, ALU.mult))
                ph.emit("vector", lambda e: e.drain())
                ph.emit("vector", lambda e: e.tensor_max(
                    mst[:, 4:5], mst[:, 0:1], mst[:, 3:4]))
                ph.emit("vector", lambda e: e.drain())
                ph.emit("vector", lambda e: e.tensor_sub(
                    mst[:, 5:6], mst[:, 0:1], mst[:, 4:5]))
                ph.emit("vector", lambda e: e.tensor_scalar_mul(
                    mst[:, 6:7], mst[:, 4:5], -1.0))
                ph.emit("vector", lambda e: e.drain(), inc="ve")
                ph.emit("scalar", lambda e: e.activation(
                    mst[:, 7:8], mst[:, 5:6], AF.Exp),
                    wait=[("ve", ph.c("ve"))], inc="se")
                wp = [("pe", pe_tr_prev)] if pe_tr_prev else []
                ph.emit("scalar", lambda e, _b=pbS: e.activation(
                    psb[:], _b[:, 0:128], AF.Exp, bias=mst[:, 6:7], scale=GAMMA,
                    accum_out=mst[:, 2:3]), wait=wp, inc="se")
                se_p = ph.c("se")
                bankcred[bS] = ("se", se_p)
                ph.emit("vector", lambda e: e.scalar_tensor_tensor(
                    mst[:, 1:2], mst[:, 1:2], mst[:, 7:8], mst[:, 2:3],
                    ALU.mult, ALU.add), wait=[("se", se_p)])
                ph.emit("vector", lambda e: e.tensor_copy(mst[:, 0:1], mst[:, 4:5]))
                ph.emit("vector", lambda e: e.drain(), inc="ve")
                w = [("se", se_p)] + bwait("pT4")
                ph.emit("tensor", lambda e: e.transpose(pT4[:], psb[:], idb[:]),
                        wait=w, inc="pe")
                pe_tr_prev = ph.c("pe")
                ph.emit("vector", lambda e: e.tensor_copy(ptb[:], pT4[:]),
                        wait=[("pe", ph.c("pe"))], inc="ve")
                bankcred["pT4"] = ("ve", ph.c("ve"))
                ve_pt = ph.c("ve")
                for half in range(2):
                    w = [("ve", ve_pt)] + (bwait("pO") if half == 0 else [])
                    ph.emit("tensor", lambda e, _h=half, _kb=kb: e.matmul(
                        pO[:, _h * 512:(_h + 1) * 512], ptb[:],
                        Vtm[:, _kb, _h * 512:(_h + 1) * 512],
                        start=True, stop=True),
                        wait=w, inc="pe" if half == 1 else None)
                pe_pv = ph.c("pe")
                ph.emit("vector", lambda e: e.drain())
                for half in range(2):
                    ph.emit("vector", lambda e, _h=half: e.scalar_tensor_tensor(
                        oac[:, _h * 512:(_h + 1) * 512],
                        oac[:, _h * 512:(_h + 1) * 512], mst[:, 7:8],
                        pO[:, _h * 512:(_h + 1) * 512], ALU.mult, ALU.add),
                        wait=[("pe", pe_pv)] if half == 0 else [],
                        inc="ve" if half == 1 else None)
                bankcred["pO"] = ("ve", ph.c("ve"))
            ph.emit("vector", lambda e: e.reciprocal(mst[:, 2:3], mst[:, 1:2]))
            ph.emit("vector", lambda e: e.drain())
            osl = qc % 2
            sno = ("ra", "rb")[osl]
            w = [sg_o[qc - 2]] if qc >= 2 else []
            ph.emit("vector", lambda e, _o=osl: e.tensor_scalar_mul(
                osb[:, _o, :], oac[:], mst[:, 2:3]), wait=w, inc="ve")
            osb_free = ("ve", ph.c("ve"))
            ph.emit("gpsimd", lambda e, _o=osl, _q=qc: e.dma_start(
                out=X1d[_q * 128:(_q + 1) * 128, :], in_=osb[:, _o, :],
                accum_op=ALU.add), wait=[("ve", ph.c("ve"))], inc=sno, incby=16)
            sg_o[qc] = (sno, ph.c(sno))
        for nmx in ("ra", "rb"):
            ph.wait("gpsimd", (nmx, ph.c(nmx)))
        ph.barrier()
        ph.run()

    T5 = 256
    with contextlib.ExitStack() as p5s:
        wgb = p5s.enter_context(nc.sbuf_tensor("p5_wg", [128, 8, 24, 128], BF16))
        wub = p5s.enter_context(nc.sbuf_tensor("p5_wu", [128, 8, 24, 128], BF16))
        wdb = p5s.enter_context(nc.sbuf_tensor("p5_wd", [128, 24, 1024], BF16))
        xnb5 = p5s.enter_context(nc.sbuf_tensor("p5_xn", [128, 2, 8, T5], BF16))
        hb5 = p5s.enter_context(nc.sbuf_tensor("p5_h", [128, 24, T5], BF16))
        st5 = p5s.enter_context(nc.sbuf_tensor("p5_st", [128, 2, T5], F32))
        x1t5 = p5s.enter_context(nc.sbuf_tensor("p5_x1", [128, 4, 1024], F32))
        ob5 = p5s.enter_context(nc.sbuf_tensor("p5_o", [128, 4, 1024], F32))
        ph = Ph(nc, sems, cnt)
        for dst, src in ((wgb, l1_wg), (wub, l1_wu), (wdb, l1_wd)):
            ph.emit("gpsimd", lambda e, _d=dst, _s=src: e.dma_start(
                out=_d[:], in_=_s), inc="lc", incby=16)
        lw5 = ph.c("lc")
        NT5 = S // T5
        NG5 = T5 // 128
        xn_cons = {}
        se_h = {}
        ve_h = {}
        st_out = {}
        gi_abs = 0
        for t in range(NT5):
            sl = t % 2
            snm = ("da", "db")[sl]
            w = [("pe", xn_cons[t - 2])] if t >= 2 else []
            ph.emit("sync", lambda e, _s=sl, _t=t: e.dma_start(
                out=xnb5[:, _s], in_=XN2d[:, :, _t * T5:(_t + 1) * T5]),
                wait=w, inc=snm, incby=16)
            ld_x = (snm, ph.c(snm))
            ld_x1w = []
            for g in range(NG5):
                gab = t * NG5 + g
                xsl = gab % 4
                sng = ("ga", "gb", "gc", "gd")[xsl]
                w = [("ve", ob_ev[gab - 4])] if gab >= 4 else []
                ph.emit("sync", lambda e, _x=xsl, _g=gab: e.dma_start(
                    out=x1t5[:, _x, :], in_=X1d[_g * 128:(_g + 1) * 128, :]),
                    wait=w, inc=sng, incby=16)
                ld_x1w.append((sng, ph.c(sng)))
            for c in range(24):
                bG = ("pA", "pC")[c % 2]
                bU = ("pB", "pD")[c % 2]
                pbG = (pA, pC)[c % 2]
                pbU = (pB, pD)[c % 2]
                for kc in range(DC):
                    w = []
                    if kc == 0:
                        w = [ld_x, ("lc", lw5)] + bwait(bG)
                    ph.emit("tensor", lambda e, _b=pbG, _k=kc, _c=c, _s=sl: e.matmul(
                        _b[:, 0:T5], wgb[:, _k, _c, :], xnb5[:, _s, _k, :],
                        start=(_k == 0), stop=(_k == DC - 1)),
                        wait=w, inc="pe" if kc == DC - 1 else None)
                pe_g = ph.c("pe")
                for kc in range(DC):
                    w = []
                    if kc == 0:
                        w = bwait(bU)
                    ph.emit("tensor", lambda e, _b=pbU, _k=kc, _c=c, _s=sl: e.matmul(
                        _b[:, 0:T5], wub[:, _k, _c, :], xnb5[:, _s, _k, :],
                        start=(_k == 0), stop=(_k == DC - 1)),
                        wait=w, inc="pe" if kc == DC - 1 else None)
                pe_u = ph.c("pe")
                ssl = c % 2
                ws = [("pe", pe_g)]
                key = c - 2
                if key in ve_h:
                    ws.append(("ve", ve_h[key]))
                elif c < 2 and t > 0:
                    ws.append(("ve", prev_ve_h))
                ph.emit("scalar", lambda e, _b=pbG, _s=ssl: e.activation(
                    st5[:, _s, :], _b[:, 0:T5], AF.Silu, scale=QSCALE),
                    wait=ws, inc="se")
                bankcred[bG] = ("se", ph.c("se"))
                ph.emit("vector", lambda e, _b=pbU, _c=c, _s=ssl: e.scalar_tensor_tensor(
                    hb5[:, _c, :], _b[:, 0:T5], QSCALE, st5[:, _s, :],
                    ALU.mult, ALU.mult),
                    wait=[("se", ph.c("se")), ("pe", pe_u)], inc="ve")
                ve_h[c] = ph.c("ve")
                bankcred[bU] = ("ve", ph.c("ve"))
            prev_ve_h = ph.c("ve")
            ve_h = {}
            ve_hall = prev_ve_h
            for g in range(NG5):
                gab = t * NG5 + g
                xsl = gab % 4
                for half in range(2):
                    for kc in range(24):
                        w = []
                        if kc == 0:
                            w = [("ve", ve_hall)]
                            if half == 0:
                                w += bwait("pO")
                        ph.emit("tensor", lambda e, _k=kc, _h=half, _g=g: e.matmul(
                            pO[:, _h * 512:(_h + 1) * 512],
                            hb5[:, _k, _g * 128:(_g + 1) * 128],
                            wdb[:, _k, _h * 512:(_h + 1) * 512],
                            start=(_k == 0), stop=(_k == 23)),
                            wait=w, inc="pe" if kc == 23 else None)
                pe_o5 = ph.c("pe")
                w = [("pe", pe_o5), ld_x1w[g]]
                if gab >= 4:
                    w.append(st_out[gab - 4])
                ph.emit("vector", lambda e, _x=xsl: e.scalar_tensor_tensor(
                    ob5[:, _x, :], pO[:], QSCALE, x1t5[:, _x, :],
                    ALU.mult, ALU.add), wait=w, inc="ve")
                if gab == 0:
                    ob_ev = {}
                ob_ev[gab] = ph.c("ve")
                bankcred["pO"] = ("ve", ph.c("ve"))
                sne = ("ea", "eb", "ec", "ed")[xsl]
                ph.emit("sync", lambda e, _x=xsl, _g=gab: e.dma_start(
                    out=out_t[_g * 128:(_g + 1) * 128, :], in_=ob5[:, _x, :]),
                    wait=[("ve", ph.c("ve"))], inc=sne, incby=16)
                st_out[gab] = (sne, ph.c(sne))
            xn_cons[t] = ph.c("pe")
        for nmx in ("ea", "eb", "ec", "ed"):
            ph.wait("sync", (nmx, ph.c(nmx)))
        ph.barrier()
        ph.run()

    ph = Ph(nc, sems, cnt)
    for dst, src in dbg_outs:
        ph.emit("sync", lambda e, _d=dst, _s=src: e.dma_start(out=_d, in_=_s),
                inc="st", incby=16)
    if dbg_comb is not None:
        ph.emit("sync", lambda e: e.dma_start(out=dbg_comb, in_=combt[:]),
                inc="st", incby=16)
    ph.wait("sync", ("st", ph.c("st")))
    ph.barrier()
    ph.run()

    stack.close()
    return nc


def _lhsT_tiles(w):
    K, M = w.shape
    return np.ascontiguousarray(
        w.reshape(K // 128, 128, M // 128, 128).transpose(1, 0, 2, 3))


def _rhs_tiles(w):
    K, N = w.shape
    return np.ascontiguousarray(w.reshape(K // 128, 128, N).transpose(1, 0, 2))


def _fmvec(v):
    n = v.shape[0] // 128
    return np.ascontiguousarray(v.reshape(n, 128).T)


def host_prep(inputs):
    import ml_dtypes
    g = {}
    wq = np.asarray(inputs["L0_attn_Wq"])
    wk = np.asarray(inputs["L0_attn_Wk"])
    wv = np.asarray(inputs["L0_attn_Wv"])
    def qkv(w):
        weff = np.ascontiguousarray(w.transpose(0, 2, 1, 3).reshape(NSL, D, D))
        return np.stack([_lhsT_tiles(weff[i]) for i in range(NSL)])
    g["l0_wq"] = qkv(wq)
    g["l0_wk"] = qkv(wk)
    g["l0_wv"] = qkv(wv)
    g["l0_wo"] = np.stack([_lhsT_tiles(np.asarray(inputs["L0_attn_Wo"])[i])
                           for i in range(NSL)])
    g["l0_w1"] = np.stack([_lhsT_tiles(np.asarray(inputs["L0_attn_W1"])[i])
                           for i in range(NSL)])
    g["l0_w2"] = np.stack([_lhsT_tiles(np.asarray(inputs["L0_attn_W2"])[i])
                           for i in range(NSL)])
    cbank = np.zeros((128, NSL, 104), np.float32)
    for i in range(NSL):
        cbank[:, i, 0:8] = _fmvec(np.asarray(inputs["L0_attn_bq"])[i].reshape(-1))
        cbank[:, i, 8:16] = _fmvec(np.asarray(inputs["L0_attn_bk"])[i].reshape(-1))
        cbank[:, i, 16:24] = _fmvec(np.asarray(inputs["L0_attn_bv"])[i].reshape(-1))
        cbank[:, i, 24:32] = _fmvec(np.asarray(inputs["L0_attn_bo"])[i])
        cbank[:, i, 32:40] = _fmvec(np.asarray(inputs["L0_attn_ln1_s"])[i])
        cbank[:, i, 40:48] = _fmvec(np.asarray(inputs["L0_attn_ln1_b"])[i])
        cbank[:, i, 48:56] = _fmvec(np.asarray(inputs["L0_attn_ln2_s"])[i])
        cbank[:, i, 56:64] = _fmvec(np.asarray(inputs["L0_attn_ln2_b"])[i])
        cbank[:, i, 64:96] = _fmvec(np.asarray(inputs["L0_attn_b1"])[i])
        cbank[:, i, 96:104] = _fmvec(np.asarray(inputs["L0_attn_b2"])[i])
    g["l0_cb"] = cbank
    g["moe_gw"] = _rhs_tiles(np.ascontiguousarray(
        np.asarray(inputs["L0_ffn_gate_w"]).T[:, EPERM]))
    w13 = np.asarray(inputs["L0_ffn_w13_weight"])
    g["moe_w13"] = np.stack([_lhsT_tiles(np.ascontiguousarray(w13[e].T))
                             for e in range(NE)])
    w2m = np.asarray(inputs["L0_ffn_w2_weight"])
    g["moe_w2"] = np.stack([_rhs_tiles(np.ascontiguousarray(w2m[e].T))
                            for e in range(NE)])
    g["l1_wq"] = _lhsT_tiles(np.asarray(inputs["L1_attn_w_q"]))
    g["l1_wk"] = _lhsT_tiles(np.asarray(inputs["L1_attn_w_k"]))
    g["l1_wv"] = _rhs_tiles(np.asarray(inputs["L1_attn_w_v"]))
    g["l1_wg"] = _lhsT_tiles(np.ascontiguousarray(
        np.asarray(inputs["L1_ffn_w_gate"]).T))
    g["l1_wu"] = _lhsT_tiles(np.ascontiguousarray(
        np.asarray(inputs["L1_ffn_w_up"]).T))
    g["l1_wd"] = _rhs_tiles(np.ascontiguousarray(
        np.asarray(inputs["L1_ffn_w_down"]).T))
    import ml_dtypes
    g["identb"] = np.eye(128, dtype=ml_dtypes.bfloat16)
    g["identf"] = np.eye(128, dtype=np.float32)
    g["ones_stat"] = np.full((128, 1), 1.0 / 1024.0, ml_dtypes.bfloat16)
    g["ones_row"] = np.ones((1, 128), np.float32)
    m = np.zeros((128, 128), np.float32)
    m[np.triu_indices(128, 1)] = NEG
    g["causal"] = m
    return g


def kernel(**inputs):
    g = host_prep(inputs)
    nc = build()
    x = np.asarray(inputs["L0_attn_x"])
    in_maps = []
    for n in range(8):
        m = dict(g)
        m["x"] = np.ascontiguousarray(x[n])
        in_maps.append(m)
    res = run_bass_kernel_spmd(nc, in_maps, core_ids=list(range(8)))
    return np.stack([res.results[n]["out"] for n in range(8)])


# version 03
# speedup vs baseline: 1.0096x; 1.0096x over previous
import numpy as np
import concourse.bass as bass
import concourse.mybir as mybir
from concourse.bass_utils import run_bass_kernel_spmd

F32 = mybir.dt.float32
BF16 = mybir.dt.bfloat16
I32 = mybir.dt.int32
AF = mybir.ActivationFunctionType
ALU = mybir.AluOpType
AX = mybir.AxisListType

S = 4096
D = 1024
DC = 8
NSL = 8
FF = 4096
FFC = 32
NE = 8
MI = 2048
L1H = 3072
L1HC = 24
NEG = -1.0e9
GAMMA = 1.0 / 32.0
QSCALE = 1.0 / 127.0
EPERM = [0, 2, 4, 6, 1, 3, 5, 7]
EPOS = [0, 4, 1, 5, 2, 6, 3, 7]

ENGINES = ("tensor", "vector", "scalar", "sync", "gpsimd")


class Ph:

    def __init__(self, nc, sems, cnt):
        self.nc = nc
        self.sems = sems
        self.cnt = cnt
        self.ops = {e: [] for e in ENGINES}

    def c(self, name):
        return self.cnt[name]

    def emit(self, eng, fn, wait=(), inc=None, incby=1):
        w = tuple(wait)
        if inc is not None:
            self.cnt[inc] += incby
        sems = self.sems

        def op(e, _w=w, _fn=fn, _inc=inc, _by=incby):
            for s, v in _w:
                e.wait_ge(sems[s], v)
            inst = _fn(e)
            if _inc is not None:
                inst.then_inc(sems[_inc], _by)
        self.ops[eng].append(op)

    def wait(self, eng, *pairs):
        sems = self.sems

        def op(e, _p=tuple(pairs)):
            for s, v in _p:
                e.wait_ge(sems[s], v)
        self.ops[eng].append(op)

    def barrier(self):
        target = self.cnt["bar"] + len(ENGINES)
        for eng in ENGINES:
            self.emit(eng, lambda e: e.nop(), inc="bar")
        for eng in ENGINES:
            self.wait(eng, ("bar", target))

    def run(self):
        with self.nc.Block() as b:
            for eng in ENGINES:
                ops = self.ops[eng]
                if ops:
                    getattr(b, eng)(lambda e, _o=ops: [f(e) for f in _o])
        self.ops = {e: [] for e in ENGINES}


def _ln_tm(ph, x_ap, sqd, tp, sd, rr, eps, nfree=D, wait_in=()):
    inv = 1.0 / nfree
    ph.emit("vector", lambda e: e.drain())
    ph.emit("vector", lambda e, _x=x_ap: e.reduce_sum(tp[:, 0:1], _x, axis=AX.X))
    ph.emit("vector", lambda e: e.drain(), inc="ve")
    ph.emit("scalar", lambda e, _x=x_ap: e.activation(
        sqd[:, :nfree], _x, AF.Square, accum_out=tp[:, 1:2]),
        wait=tuple(wait_in), inc="se")
    ph.wait("vector", ("se", ph.c("se")))
    ph.emit("vector", lambda e: e.tensor_scalar_mul(tp[:, 4:5], tp[:, 0:1], inv))
    ph.emit("vector", lambda e: e.tensor_scalar_mul(tp[:, 3:4], tp[:, 1:2], inv))
    ph.emit("vector", lambda e: e.drain())
    ph.emit("vector", lambda e: e.tensor_mul(tp[:, 2:3], tp[:, 4:5], tp[:, 4:5]))
    ph.emit("vector", lambda e: e.drain())
    ph.emit("vector", lambda e: e.tensor_sub(tp[:, 5:6], tp[:, 3:4], tp[:, 2:3]),
            inc="ve")
    ph.emit("scalar", lambda e: e.activation(sd[:], tp[:, 5:6], AF.Sqrt, bias=eps),
            wait=(("ve", ph.c("ve")),), inc="se")
    ph.emit("vector", lambda e: e.reciprocal(rr[:], sd[:]),
            wait=(("se", ph.c("se")),))
    ph.emit("vector", lambda e: e.drain(), inc="ve")


def build(dump=(), nsl=NSL):
    nc = bass.Bass("TRN2", target_bir_lowering=False)

    def din(name, shape, dt=F32):
        return nc.dram_tensor(name, shape, dt, kind="ExternalInput").ap()

    x_in = din("x", [S, D])
    l0_wq = din("l0_wq", [NSL, 128, 8, 8, 128])
    l0_wk = din("l0_wk", [NSL, 128, 8, 8, 128])
    l0_wv = din("l0_wv", [NSL, 128, 8, 8, 128])
    l0_wo = din("l0_wo", [NSL, 128, 8, 8, 128])
    l0_w1 = din("l0_w1", [NSL, 128, 8, 32, 128])
    l0_w2 = din("l0_w2", [NSL, 128, 32, 8, 128])
    l0_cb = din("l0_cb", [128, NSL, 104])
    moe_gw = din("moe_gw", [128, 8, 8])
    moe_w13 = din("moe_w13", [NE, 128, 8, 32, 128])
    moe_w2 = din("moe_w2", [NE, 128, 16, 1024])
    l1_wq = din("l1_wq", [128, 8, 8, 128])
    l1_wk = din("l1_wk", [128, 8, 8, 128])
    l1_wv = din("l1_wv", [128, 8, 1024])
    l1_wg = din("l1_wg", [128, 8, 24, 128], I32)
    l1_wu = din("l1_wu", [128, 8, 24, 128], I32)
    l1_wd = din("l1_wd", [128, 24, 1024], I32)
    identb = din("identb", [128, 128], BF16)
    identf = din("identf", [128, 128])
    ones_stat = din("ones_stat", [128, 1], BF16)
    ones_row = din("ones_row", [1, 128])
    causal = din("causal", [128, 128])

    out_t = nc.dram_tensor("out", [S, D], F32, kind="ExternalOutput").ap()

    def scr(name, shape, dt=F32):
        return nc.dram_tensor(name, shape, dt, kind="Internal").ap()

    Yd = scr("Yd", [128, DC, S])
    Ybd = scr("Ybd", [128, DC, S], BF16)
    YMd = scr("YMd", [128, DC, S], BF16)
    Hd = scr("Hd", [128, FFC, S], BF16)
    X1d = scr("X1d", [S, D])
    XN2d = scr("XN2d", [128, DC, S], BF16)

    dbg_map = {"Y": Yd, "YB": Ybd, "YM": YMd, "X1": X1d, "XN2": XN2d, "H": Hd}
    dbg_outs = []
    dbg_comb = None
    for i, name in enumerate(dump):
        if name == "COMB":
            dbg_comb = nc.dram_tensor(f"dbg{i}", [128, 32, 8], F32,
                                      kind="ExternalOutput").ap()
            continue
        src = dbg_map[name]
        dt = BF16 if name in ("XN2", "H", "YM", "YB") else F32
        dbg_outs.append((nc.dram_tensor(f"dbg{i}", list(src.shape), dt,
                                        kind="ExternalOutput").ap(), src))

    import contextlib
    stack = contextlib.ExitStack()
    pA = stack.enter_context(nc.psum_tensor("pA", [128, 512], F32))
    pB = stack.enter_context(nc.psum_tensor("pB", [128, 512], F32))
    pC = stack.enter_context(nc.psum_tensor("pC", [128, 512], F32))
    pD = stack.enter_context(nc.psum_tensor("pD", [128, 512], F32))
    pO = stack.enter_context(nc.psum_tensor("pO", [128, 1024], F32))

    sem_names = ["ld", "lc", "st", "sg", "pe", "ve", "se", "ge", "bar",
                 "w0", "w1", "w2", "ca", "cb",
                 "da", "db", "ea", "eb", "ec", "ed",
                 "fa", "fb", "ga", "gb", "gc", "gd", "ha", "hb",
                 "qa", "qb", "ra", "rb", "rc", "rd"]
    sems = {n: stack.enter_context(nc.semaphore(n)) for n in sem_names}
    cnt = {n: 0 for n in sem_names}

    idb = stack.enter_context(nc.sbuf_tensor("idb", [128, 128], BF16))
    idf = stack.enter_context(nc.sbuf_tensor("idf", [128, 128], F32))
    onst = stack.enter_context(nc.sbuf_tensor("onst", [128, 1], BF16))
    onsr = stack.enter_context(nc.sbuf_tensor("onsr", [1, 128], F32))
    caus = stack.enter_context(nc.sbuf_tensor("caus", [128, 128], F32))
    combt = stack.enter_context(nc.sbuf_tensor("combt", [128, 32, 8], F32))
    constv = stack.enter_context(nc.sbuf_tensor("constv", [128, 8], F32))
    CONST_VALS = [0.0, 1.0, 1.0e-5, 1.0e-6, GAMMA, QSCALE, -1.0, 2.0]
    for i, v in enumerate(CONST_VALS):
        nc.const_aps.register(F32, v, constv[:, i:i + 1])

    ph = Ph(nc, sems, cnt)
    for i, v in enumerate(CONST_VALS):
        ph.emit("vector", lambda e, _i=i, _v=v: e.memset(constv[:, _i:_i + 1], _v))

    for dst, src in ((idb, identb), (idf, identf), (onst, ones_stat),
                     (onsr, ones_row), (caus, causal)):
        ph.emit("sync", lambda e, _d=dst, _s=src: e.dma_start(out=_d[:], in_=_s),
                inc="ld", incby=16)
    ph.wait("sync", ("ld", ph.c("ld")))
    ph.barrier()
    ph.run()

    with (
        nc.sbuf_tensor("p0_x", [128, 2, D], F32) as xb,
        nc.sbuf_tensor("p0_xn", [128, D], F32) as xnb,
        nc.sbuf_tensor("p0_yt", [128, 2, DC, 128], F32) as ytb,
        nc.sbuf_tensor("p0_sq", [128, D], F32) as sqd,
        nc.sbuf_tensor("p0_tp", [128, 8], F32) as tp,
        nc.sbuf_tensor("p0_sd", [128, 1], F32) as sd,
        nc.sbuf_tensor("p0_rr", [128, 1], F32) as rr,
    ):
        ph = Ph(nc, sems, cnt)
        ve_done = {}
        se_done = {}
        sg_done = {}
        st_done = {}
        pe_evac = {}
        prev_last_evac = 0
        for t in range(32):
            sl = t % 2
            sld = ("da", "db")[sl]
            sle = ("ra", "rb")[sl]
            slf = ("fa", "fb")[sl]
            w = []
            if t >= 2:
                w.append(("ve", ve_done[t - 2]))
                w.append(("se", se_done[t - 2]))
                w.append(sg_done[t - 2])
            ph.emit("sync", lambda e, _s=sl, _t=t: e.dma_start(
                out=xb[:, _s, :], in_=x_in[_t * 128:(_t + 1) * 128, :]),
                wait=w, inc=sld, incby=16)
            ld_now = (sld, ph.c(sld))
            ph.emit("gpsimd", lambda e, _s=sl, _t=t: e.dma_start(
                out=X1d[_t * 128:(_t + 1) * 128, :], in_=xb[:, _s, :]),
                wait=[ld_now], inc=sle, incby=16)
            sg_done[t] = (sle, ph.c(sle))
            ph.wait("vector", ld_now)
            _ln_tm(ph, xb[:, sl, :], sqd, tp, sd, rr, 1.0e-5,
                   wait_in=[ld_now])
            se_done[t] = ph.c("se")
            wx = [("pe", ph.c("pe"))] if t >= 1 else []
            ph.emit("vector", lambda e, _s=sl: e.tensor_scalar(
                xnb[:], xb[:, _s, :], tp[:, 4:5], rr[:], ALU.subtract, ALU.mult),
                wait=wx, inc="ve")
            ve_done[t] = ph.c("ve")
            ve_now = ph.c("ve")
            for c in range(DC):
                bank = (pA, pB)[c % 2]
                w = [("ve", ve_now)]
                if c >= 2:
                    w.append(("ve", pe_evac[c - 2]))
                elif prev_last_evac:
                    w.append(("ve", prev_last_evac))
                ph.emit("tensor", lambda e, _b=bank, _c=c: e.transpose(
                    _b[:, 0:128], xnb[:, _c * 128:(_c + 1) * 128], idf[:]),
                    wait=w, inc="pe")
                pe_now = ph.c("pe")
                wv = [("pe", pe_now)]
                if t >= 2:
                    wv.append(st_done[t - 2])
                ph.emit("vector", lambda e, _b=bank, _c=c, _s=sl: e.tensor_copy(
                    ytb[:, _s, _c, :], _b[:, 0:128]), wait=wv, inc="ve")
                pe_evac[c] = ph.c("ve")
            prev_last_evac = ph.c("ve")
            ph.emit("sync", lambda e, _s=sl, _t=t: e.dma_start(
                out=Yd[:, :, _t * 128:(_t + 1) * 128], in_=ytb[:, _s]),
                wait=[("ve", ph.c("ve"))], inc=slf, incby=16)
            st_done[t] = (slf, ph.c(slf))
        for nmx in ("fa", "fb"):
            ph.wait("sync", (nmx, ph.c(nmx)))
        for nmx in ("ra", "rb"):
            ph.wait("gpsimd", (nmx, ph.c(nmx)))
        ph.barrier()
        ph.run()

    with (
        nc.sbuf_tensor("ga_y", [128, 2, 8, 512], F32) as ybf,
        nc.sbuf_tensor("ga_gw", [128, 8, 8], F32) as gwb,
        nc.sbuf_tensor("ga_sc", [128, 8], F32) as gsc,
        nc.sbuf_tensor("ga_esc", [128, 8], F32) as esc,
        nc.sbuf_tensor("ga_sm", [128, 8], F32) as smb,
        nc.sbuf_tensor("ga_gs", [128, 4], F32) as gst4,
        nc.sbuf_tensor("ga_m1g", [128, 4], F32) as m1g4,
        nc.sbuf_tensor("ga_gs2", [128, 4], F32) as gs24,
        nc.sbuf_tensor("ga_gmk", [128, 4], F32) as gmk4,
        nc.sbuf_tensor("ga_msc", [128, 8], F32) as msc,
        nc.sbuf_tensor("ga_mk1", [128, 8], F32) as mk1,
        nc.sbuf_tensor("ga_ms2", [128, 8], F32) as ms2,
        nc.sbuf_tensor("ga_mk2", [128, 8], F32) as mk2,
        nc.sbuf_tensor("ga_c1", [128, 8], F32) as c1b,
        nc.sbuf_tensor("ga_t2", [128, 16], F32) as t2,
    ):
        ph = Ph(nc, sems, cnt)
        ph.emit("sync", lambda e: e.dma_start(out=gwb[:], in_=moe_gw),
                inc="ld", incby=16)
        ld_gw = ph.c("ld")
        pe_tile = {}
        ve_grp = {}
        for tl in range(8):
            sl = tl % 2
            sld = ("da", "db")[sl]
            w = [("pe", pe_tile[tl - 2])] if tl >= 2 else []
            ph.emit("sync", lambda e, _s=sl, _t=tl: e.dma_start(
                out=ybf[:, _s], in_=Yd[:, :, _t * 512:(_t + 1) * 512]),
                wait=w, inc=sld, incby=16)
            lc_now = (sld, ph.c(sld))
            for gi in range(4):
                g = tl * 4 + gi
                bank = (pA, pB)[g % 2]
                for kc in range(DC):
                    wm = [lc_now, ("ld", ld_gw)] if kc == 0 else []
                    if kc == 0 and g >= 2:
                        wm.append(("ve", ve_grp[g - 2]))
                    ph.emit("tensor", lambda e, _b=bank, _k=kc, _s=sl, _gi=gi: e.matmul(
                        _b[:, 0:8], ybf[:, _s, _k, _gi * 128:(_gi + 1) * 128],
                        gwb[:, _k, :], start=(_k == 0), stop=(_k == DC - 1)),
                        wait=wm, inc="pe" if kc == DC - 1 else None)
                pe_now = ph.c("pe")
                E = ph.emit
                sep = lambda: E("vector", lambda e: e.drain())
                E("vector", lambda e, _b=bank: e.tensor_copy(gsc[:], _b[:, 0:8]),
                  wait=[("pe", pe_now)])
                sep()
                E("vector", lambda e: e.reduce_max(t2[:, 0:1], gsc[:], axis=AX.X))
                sep()
                E("vector", lambda e: e.tensor_scalar_mul(t2[:, 1:2], t2[:, 0:1], -1.0),
                  inc="ve")
                ve_grp[g] = ph.c("ve")
                E("scalar", lambda e: e.activation(esc[:], gsc[:], AF.Exp,
                                                   bias=t2[:, 1:2],
                                                   accum_out=t2[:, 2:3]),
                  wait=[("ve", ph.c("ve"))], inc="se")
                E("vector", lambda e: e.reciprocal(t2[:, 3:4], t2[:, 2:3]),
                  wait=[("se", ph.c("se"))])
                sep()
                E("vector", lambda e: e.tensor_scalar_mul(smb[:], esc[:], t2[:, 3:4]))
                sep()
                E("vector", lambda e: e.tensor_max(gst4[:], smb[:, 0:4], smb[:, 4:8]))
                sep()
                E("vector", lambda e: e.reduce_max(t2[:, 4:5], gst4[:], axis=AX.X))
                sep()
                E("vector", lambda e: e.tensor_scalar(
                    m1g4[:], gst4[:], t2[:, 4:5], None, ALU.is_equal))
                sep()
                E("vector", lambda e: e.scalar_tensor_tensor(
                    gs24[:], m1g4[:], NEG, gst4[:], ALU.mult, ALU.add))
                sep()
                E("vector", lambda e: e.reduce_max(t2[:, 5:6], gs24[:], axis=AX.X))
                sep()
                E("vector", lambda e: e.tensor_scalar(
                    gmk4[:], gst4[:], t2[:, 5:6], None, ALU.is_ge))
                sep()
                E("vector", lambda e: e.tensor_mul(msc[:, 0:4], smb[:, 0:4], gmk4[:]))
                E("vector", lambda e: e.tensor_mul(msc[:, 4:8], smb[:, 4:8], gmk4[:]))
                sep()
                E("vector", lambda e: e.reduce_max(t2[:, 6:7], msc[:], axis=AX.X))
                sep()
                E("vector", lambda e: e.tensor_scalar(
                    mk1[:], msc[:], t2[:, 6:7], None, ALU.is_equal))
                sep()
                E("vector", lambda e: e.scalar_tensor_tensor(
                    ms2[:], mk1[:], NEG, msc[:], ALU.mult, ALU.add))
                sep()
                E("vector", lambda e: e.reduce_max(t2[:, 7:8], ms2[:], axis=AX.X))
                sep()
                E("vector", lambda e: e.tensor_add(t2[:, 8:9], t2[:, 6:7], t2[:, 7:8]))
                sep()
                E("vector", lambda e: e.reciprocal(t2[:, 9:10], t2[:, 8:9]))
                sep()
                E("vector", lambda e: e.tensor_mul(t2[:, 10:11], t2[:, 6:7], t2[:, 9:10]))
                E("vector", lambda e: e.tensor_mul(t2[:, 11:12], t2[:, 7:8], t2[:, 9:10]))
                E("vector", lambda e: e.tensor_scalar(
                    mk2[:], msc[:], t2[:, 7:8], None, ALU.is_equal))
                sep()
                E("vector", lambda e: e.tensor_scalar_mul(c1b[:], mk1[:], t2[:, 10:11]))
                sep()
                E("vector", lambda e, _g=g: e.scalar_tensor_tensor(
                    combt[:, _g, :], mk2[:], t2[:, 11:12], c1b[:],
                    ALU.mult, ALU.add), inc="ve")
            pe_tile[tl] = ph.c("pe")
        ph.barrier()
        ph.run()

    TMOE = 256
    NTM = S // TMOE
    with (
        nc.sbuf_tensor("m_w13", [128, 2, 8, 32, 128], BF16) as w13b,
        nc.sbuf_tensor("m_w2", [128, 16, 1024], BF16) as w2b,
        nc.sbuf_tensor("m_xn", [128, 2, 8, TMOE], BF16) as xnb,
        nc.sbuf_tensor("m_act", [128, 16, TMOE], BF16) as actb,
        nc.sbuf_tensor("m_st", [128, 2, TMOE], F32) as stb,
        nc.sbuf_tensor("m_out", [128, 4, 1024], F32) as outsb,
    ):
        ph = Ph(nc, sems, cnt)
        pe_w13_done = {}
        pe_w2_done = {}
        pe_xn_done = {}
        lc_w13 = {}
        lc_w2 = {}
        ve_act_done = {}
        se_g = {}
        ve_u = {}
        sg_out = {}
        oslot_i = 0
        ph.emit("gpsimd", lambda e: e.dma_start(out=w13b[:, 0], in_=moe_w13[0]),
                inc="w0", incby=16)
        lc_w13[0] = ("w0", ph.c("w0"))
        for e_ in range(NE):
            s13 = e_ % 2
            if e_ + 1 < NE:
                w = [("pe", pe_w13_done[e_ - 1])] if e_ >= 1 else []
                snm = ("w0", "w1")[(e_ + 1) % 2]
                ph.emit("gpsimd", lambda e, _e=e_: e.dma_start(
                    out=w13b[:, (_e + 1) % 2], in_=moe_w13[_e + 1]),
                    wait=w, inc=snm, incby=16)
                lc_w13[e_ + 1] = (snm, ph.c(snm))
            w = [("pe", pe_w2_done[e_ - 1])] if e_ >= 1 else []
            ph.emit("gpsimd", lambda e, _e=e_: e.dma_start(
                out=w2b[:], in_=moe_w2[_e]), wait=w, inc="w2", incby=16)
            lc_w2[e_] = ("w2", ph.c("w2"))
            for t in range(NTM):
                sl = t % 2
                w = []
                pt = (e_, t - 2) if t >= 2 else (e_ - 1, NTM - 2 + t)
                if pt in pe_xn_done:
                    w.append(("pe", pe_xn_done[pt]))
                snm = ("qa", "qb")[sl]
                ph.emit("gpsimd", lambda e, _s=sl, _t=t: e.dma_start(
                    out=xnb[:, _s], in_=Yd[:, :, _t * TMOE:(_t + 1) * TMOE]),
                    wait=w, inc=snm, incby=16)
                lc_xn = (snm, ph.c(snm))
                for c in range(16):
                    bankG = (pA, pC)[c % 2]
                    bankU = (pB, pD)[c % 2]
                    for kc in range(DC):
                        wm = []
                        if kc == 0:
                            wm = [lc_xn, lc_w13[e_]]
                            key = (e_, t, c - 2)
                            if key in se_g:
                                wm.append(("se", se_g[key]))
                            elif c < 2 and (e_, t) != (0, 0):
                                wm.append(("se", prev_last_se))
                        ph.emit("tensor", lambda e, _b=bankG, _k=kc, _c=c, _s=sl, _sl3=s13: e.matmul(
                            _b[:, 0:TMOE], w13b[:, _sl3, _k, _c, :],
                            xnb[:, _s, _k, :], start=(_k == 0), stop=(_k == DC - 1)),
                            wait=wm, inc="pe" if kc == DC - 1 else None)
                    pe_gate = ph.c("pe")
                    for kc in range(DC):
                        wm = []
                        if kc == 0:
                            key = (e_, t, c - 2)
                            if key in ve_u:
                                wm.append(("ve", ve_u[key]))
                            elif c < 2 and (e_, t) != (0, 0):
                                wm.append(("ve", prev_last_veu))
                        ph.emit("tensor", lambda e, _b=bankU, _k=kc, _c=c, _s=sl, _sl3=s13: e.matmul(
                            _b[:, 0:TMOE], w13b[:, _sl3, _k, _c + 16, :],
                            xnb[:, _s, _k, :], start=(_k == 0), stop=(_k == DC - 1)),
                            wait=wm, inc="pe" if kc == DC - 1 else None)
                    pe_up = ph.c("pe")
                    ssl = c % 2
                    ws = [("pe", pe_gate)]
                    key = (e_, t, c - 2)
                    if key in ve_u:
                        ws.append(("ve", ve_u[key]))
                    elif c < 2 and (e_, t) != (0, 0):
                        ws.append(("ve", prev_last_veu))
                    ph.emit("scalar", lambda e, _b=bankG, _s=ssl: e.activation(
                        stb[:, _s, :], _b[:, 0:TMOE], AF.Silu),
                        wait=ws, inc="se")
                    se_g[(e_, t, c)] = ph.c("se")
                    ph.emit("vector", lambda e, _b=bankU, _c=c, _s=ssl: e.scalar_tensor_tensor(
                        actb[:, _c, :], _b[:, 0:TMOE], 1.0, stb[:, _s, :],
                        ALU.bypass, ALU.mult),
                        wait=[("se", ph.c("se")), ("pe", pe_up)], inc="ve")
                    ve_u[(e_, t, c)] = ph.c("ve")
                prev_last_se = se_g[(e_, t, 15)]
                prev_last_veu = ve_u[(e_, t, 15)]
                ve_act = ph.c("ve")
                pe_xn_done[(e_, t)] = 0
                for grp in range(TMOE // 128):
                    gabs = t * (TMOE // 128) + grp
                    for half in range(2):
                        for kc in range(16):
                            wm = []
                            if kc == 0:
                                wm = [("ve", ve_act), lc_w2[e_]]
                                if oslot_i >= 1:
                                    wm.append(("ve", ve_o_evac[oslot_i - 1]))
                            ph.emit("tensor", lambda e, _k=kc, _h=half, _g=grp: e.matmul(
                                pO[:, _h * 512:(_h + 1) * 512],
                                actb[:, _k, _g * 128:(_g + 1) * 128],
                                w2b[:, _k, _h * 512:(_h + 1) * 512],
                                start=(_k == 0), stop=(_k == 15)),
                                wait=wm, inc="pe" if kc == 15 else None)
                    pe_o = ph.c("pe")
                    osl = oslot_i % 4
                    w = [("pe", pe_o)]
                    if oslot_i >= 4:
                        w.append(sg_out[oslot_i - 4])
                    ph.emit("vector", lambda e, _o=osl, _g=gabs, _p=EPOS[e_]: e.tensor_scalar_mul(
                        outsb[:, _o, :], pO[:], combt[:, _g, _p:_p + 1]),
                        wait=w, inc="ve")
                    if oslot_i == 0:
                        ve_o_evac = {}
                    ve_o_evac[oslot_i] = ph.c("ve")
                    snm = ("ra", "rb", "rc", "rd")[osl]
                    ph.emit("gpsimd", lambda e, _o=osl, _g=gabs, _e=e_: e.dma_start(
                        out=X1d[_g * 128:(_g + 1) * 128, :], in_=outsb[:, _o, :],
                        accum_op=ALU.add),
                        wait=[("ve", ph.c("ve"))], inc=snm, incby=16)
                    sg_out[oslot_i] = (snm, ph.c(snm))
                    oslot_i += 1
                pe_xn_done[(e_, t)] = ph.c("pe")
            pe_w13_done[e_] = ph.c("pe")
            pe_w2_done[e_] = ph.c("pe")
        for nmx in ("ra", "rb", "rc", "rd"):
            ph.wait("gpsimd", (nmx, ph.c(nmx)))
        ph.barrier()
        ph.run()

    T2 = 256
    NT2 = S // T2
    bankcred = {}

    def bwait(*banks):
        return [bankcred[b] for b in banks if b in bankcred]

    with contextlib.ExitStack() as p2stk:
        WS = p2stk.enter_context(nc.sbuf_tensor("p2_ws", [128, 2, 32768], BF16))
        ycb = p2stk.enter_context(nc.sbuf_tensor("p2_yc", [128, 2, 8, T2], BF16))
        attb = p2stk.enter_context(nc.sbuf_tensor("p2_att", [128, 8, T2], BF16))
        tmin = p2stk.enter_context(nc.sbuf_tensor("p2_tm", [128, T2], F32))
        emin = p2stk.enter_context(nc.sbuf_tensor("p2_em", [128, T2], F32))
        kfb = p2stk.enter_context(nc.sbuf_tensor("p2_kf", [128, T2], F32))
        vbb = p2stk.enter_context(nc.sbuf_tensor("p2_vb", [128, T2], F32))
        prb = p2stk.enter_context(nc.sbuf_tensor("p2_pr", [128, T2], F32))
        tgb = p2stk.enter_context(nc.sbuf_tensor("p2_tg", [128, 2, T2], F32))
        zbf = p2stk.enter_context(nc.sbuf_tensor("p2_zb", [128, 8, T2], F32))
        zb16 = p2stk.enter_context(nc.sbuf_tensor("p2_z16", [128, 8, T2], BF16))
        ymb16 = p2stk.enter_context(nc.sbuf_tensor("p2_ym16", [128, 2, 8, T2], BF16))
        hshr = p2stk.enter_context(nc.sbuf_tensor("p2_h", [128, 2, 32, T2], BF16))
        mrb = p2stk.enter_context(nc.sbuf_tensor("p2_mr", [128, T2], F32))
        msb = p2stk.enter_context(nc.sbuf_tensor("p2_ms", [1, T2], F32))
        qsb = p2stk.enter_context(nc.sbuf_tensor("p2_qs", [1, T2], F32))
        kvacc = p2stk.enter_context(nc.sbuf_tensor("p2_kv", [128, 8], F32))
        skacc = p2stk.enter_context(nc.sbuf_tensor("p2_sk", [128, 8], F32))
        kvp = p2stk.enter_context(nc.sbuf_tensor("p2_kvp", [128, 8, NT2], F32))
        skp = p2stk.enter_context(nc.sbuf_tensor("p2_skp", [128, 8, NT2], F32))
        cbs = p2stk.enter_context(nc.sbuf_tensor("p2_cb", [128, 2, 104], F32))
        cbds = p2stk.enter_context(nc.sbuf_tensor("p2_cbd", [128, 2, 16], F32))
        pSM = p2stk.enter_context(nc.psum_tensor("pSM", [128, 512], F32))
        pSQ = p2stk.enter_context(nc.psum_tensor("pSQ", [128, 512], F32))
        ph = Ph(nc, sems, cnt)

        def wv_qkv(sl, half):
            return WS[:, sl, half * 8192:(half + 1) * 8192].rearrange(
                "p (a b c) -> p a b c", a=8, b=8, c=128)

        def wv_w1(sl):
            return WS[:, sl, :].rearrange("p (a b c) -> p a b c", a=8, b=32, c=128)

        def wv_w2(sl):
            return WS[:, sl, :].rearrange("p (a b c) -> p a b c", a=32, b=8, c=128)

        pe_pass_end = {}
        stY = {}
        stYM = {}
        stH = {}

        def load_pass_weights(gp):
            i, p = gp // 4, gp % 4
            sl = gp % 2
            snm = ("w0", "w1")[sl]
            w = [("pe", pe_pass_end[gp - 2])] if gp - 2 in pe_pass_end else []
            if p == 0:
                ph.emit("gpsimd", lambda e, _i=i, _s=sl: e.dma_start(
                    out=wv_qkv(_s, 0), in_=l0_wk[_i]), wait=w, inc=snm, incby=16)
                ph.emit("gpsimd", lambda e, _i=i, _s=sl: e.dma_start(
                    out=wv_qkv(_s, 1), in_=l0_wv[_i]), inc=snm, incby=16)
            elif p == 1:
                ph.emit("gpsimd", lambda e, _i=i, _s=sl: e.dma_start(
                    out=wv_qkv(_s, 0), in_=l0_wq[_i]), wait=w, inc=snm, incby=16)
                ph.emit("gpsimd", lambda e, _i=i, _s=sl: e.dma_start(
                    out=wv_qkv(_s, 1), in_=l0_wo[_i]), inc=snm, incby=16)
            elif p == 2:
                ph.emit("gpsimd", lambda e, _i=i, _s=sl: e.dma_start(
                    out=wv_w1(_s), in_=l0_w1[_i]), wait=w, inc=snm, incby=16)
            else:
                ph.emit("gpsimd", lambda e, _i=i, _s=sl: e.dma_start(
                    out=wv_w2(_s), in_=l0_w2[_i]), wait=w, inc=snm, incby=16)
            return (snm, ph.c(snm))

        def ln_fm_tail(i, t, csl, scol, bcol, store_dram, zdone_se, stsems,
                       st_prev):
            for mc in range(DC):
                w = [("ve", ph.c("ve"))] if mc == 0 else []
                if mc == 0:
                    w += bwait("pSM")
                ph.emit("tensor", lambda e, _m=mc: e.matmul(
                    pSM[0:1, 0:T2], onst[:], zb16[:, _m, :],
                    start=(_m == 0), stop=(_m == DC - 1)),
                    wait=w, inc="pe" if mc == DC - 1 else None)
            for mc in range(DC):
                w = [("se", zdone_se)] if mc == 0 else []
                if mc == 0:
                    w += bwait("pSQ")
                ph.emit("tensor", lambda e, _m=mc: e.matmul(
                    pSQ[0:1, 0:T2], onst[:], attb[:, _m, :],
                    start=(_m == 0), stop=(_m == DC - 1)),
                    wait=w, inc="pe" if mc == DC - 1 else None)
            pe_stats = ph.c("pe")
            ph.emit("vector", lambda e: e.tensor_copy(msb[0:1, 0:T2], pSM[0:1, 0:T2]),
                    wait=[("pe", pe_stats)])
            ph.emit("vector", lambda e: e.tensor_copy(qsb[0:1, 0:T2], pSQ[0:1, 0:T2]))
            ph.emit("vector", lambda e: e.drain(), inc="ve")
            bankcred["pSM"] = ("ve", ph.c("ve"))
            bankcred["pSQ"] = ("ve", ph.c("ve"))
            w = [("ve", ph.c("ve"))] + bwait("pO")
            ph.emit("tensor", lambda e: e.matmul(
                pO[:, 0:T2], onsr[:], msb[0:1, 0:T2], start=True, stop=True),
                wait=w, inc="pe")
            ph.emit("tensor", lambda e: e.matmul(
                pO[:, 512:512 + T2], onsr[:], qsb[0:1, 0:T2], start=True, stop=True),
                inc="pe")
            pe_bc = ph.c("pe")
            ph.emit("vector", lambda e: e.tensor_copy(mrb[:], pO[:, 0:T2]),
                    wait=[("pe", pe_bc)])
            ph.emit("vector", lambda e: e.drain())
            ph.emit("vector", lambda e: e.tensor_mul(tmin[:], mrb[:], mrb[:]))
            ph.emit("vector", lambda e: e.drain())
            ph.emit("vector", lambda e: e.tensor_sub(
                emin[:], pO[:, 512:512 + T2], tmin[:]), inc="ve")
            bankcred["pO"] = ("ve", ph.c("ve"))
            ph.emit("scalar", lambda e: e.activation(
                kfb[:], emin[:], AF.Sqrt, bias=1.0e-6),
                wait=[("ve", ph.c("ve"))], inc="se")
            ph.emit("vector", lambda e: e.reciprocal(vbb[:], kfb[:]),
                    wait=[("se", ph.c("se"))])
            ph.emit("vector", lambda e: e.drain())
            ssl = t % 2
            snm = stsems[ssl]
            for mp in range(DC // 2):
                m0, m1 = 2 * mp, 2 * mp + 1
                wn = [st_prev] if (mp == 0 and st_prev is not None) else []
                ph.emit("vector", lambda e, _m=m0: e.tensor_sub(
                    prb[:], zbf[:, _m, :], mrb[:]), wait=wn)
                ph.emit("vector", lambda e, _m=m1: e.tensor_sub(
                    tmin[:], zbf[:, _m, :], mrb[:]))
                ph.emit("vector", lambda e: e.drain())
                ph.emit("vector", lambda e: e.tensor_mul(emin[:], prb[:], vbb[:]))
                ph.emit("vector", lambda e: e.tensor_mul(kfb[:], tmin[:], vbb[:]))
                ph.emit("vector", lambda e: e.drain())
                for _m, src in ((m0, emin), (m1, kfb)):
                    ph.emit("vector", lambda e, _mm=_m, _sr=src, _cs=csl, _ss=ssl: e.tensor_scalar(
                        ymb16[:, _ss, _mm, :], _sr[:],
                        cbs[:, _cs, scol + _mm:scol + _mm + 1],
                        cbs[:, _cs, bcol + _mm:bcol + _mm + 1], ALU.mult, ALU.add),
                        inc="ve" if _m == DC - 1 else None)
            ve_norm = ph.c("ve")
            ph.emit("sync", lambda e, _t=t, _d=store_dram, _ss=ssl: e.dma_start(
                out=_d[:, :, _t * T2:(_t + 1) * T2], in_=ymb16[:, _ss]),
                wait=[("ve", ve_norm)], inc=snm, incby=16)
            return (snm, ph.c(snm))

        lw = {0: load_pass_weights(0), 1: load_pass_weights(1)}
        cbs_free = {}
        kvp_free = None
        for i in range(nsl):
            csl = i % 2
            snm_cb = ("ca", "cb")[csl]
            wcb = [("ve", cbs_free[i - 2])] if i >= 2 else []
            ph.emit("sync", lambda e, _i=i, _c=csl: e.dma_start(
                out=cbs[:, _c, :], in_=l0_cb[:, _i, :]),
                wait=wcb, inc=snm_cb, incby=16)
            ld_cb = (snm_cb, ph.c(snm_cb))
            ph.wait("vector", ld_cb)
            ph.emit("vector", lambda e, _c=csl: e.tensor_scalar_add(
                cbds[:, _c, 0:8], cbs[:, _c, 0:8], 1.0))
            ph.emit("vector", lambda e, _c=csl: e.tensor_scalar_add(
                cbds[:, _c, 8:16], cbs[:, _c, 8:16], 1.0))
            ph.emit("vector", lambda e: e.drain(), inc="ve")

            gp = 4 * i
            slA = gp % 2
            if gp + 1 not in lw:
                lw[gp + 1] = load_pass_weights(gp + 1)
            yc_cons = {}
            for t in range(NT2):
                sl = t % 2
                snm = ("da", "db")[sl]
                w = []
                if t >= 2:
                    w = [("ve", yc_cons[t - 2])]
                elif 4 * i - 1 in pe_pass_end:
                    w = [("ve", prev_ycb_free)]
                if (i - 1, t) in stY:
                    w.append(stY[(i - 1, t)])
                if i == 0:
                    snm = ("qa", "qb")[sl]
                    ph.emit("gpsimd", lambda e, _s=sl, _t=t: e.dma_start(
                        out=ycb[:, _s], in_=Yd[:, :, _t * T2:(_t + 1) * T2]),
                        wait=w, inc=snm, incby=16)
                else:
                    ph.emit("sync", lambda e, _s=sl, _t=t: e.dma_start(
                        out=ycb[:, _s], in_=Ybd[:, :, _t * T2:(_t + 1) * T2]),
                        wait=w, inc=snm, incby=16)
                lc_y = (snm, ph.c(snm))
                for mc in range(DC):
                    bK = ("pA", "pB", "pC", "pD")[mc % 4]
                    pbK = (pA, pB, pC, pD)[mc % 4]
                    bV = ("pSM", "pSQ", "pOl", "pOh")[mc % 4]
                    pbV = (pSM[:, 0:T2], pSQ[:, 0:T2], pO[:, 0:T2],
                           pO[:, 512:512 + T2])[mc % 4]
                    for kc in range(DC):
                        w = []
                        if kc == 0:
                            w = [lc_y, lw[gp]] + bwait(bK)
                        ph.emit("tensor", lambda e, _b=pbK, _k=kc, _m=mc, _s=sl, _sa=slA: e.matmul(
                            _b[:, 0:T2], wv_qkv(_sa, 0)[:, _k, _m, :],
                            ycb[:, _s, _k, :], start=(_k == 0), stop=(_k == DC - 1)),
                            wait=w, inc="pe" if kc == DC - 1 else None)
                    pe_K = ph.c("pe")
                    for kc in range(DC):
                        w = []
                        if kc == 0:
                            w = bwait(bV)
                        ph.emit("tensor", lambda e, _b=pbV, _k=kc, _m=mc, _s=sl, _sa=slA: e.matmul(
                            _b, wv_qkv(_sa, 1)[:, _k, _m, :],
                            ycb[:, _s, _k, :], start=(_k == 0), stop=(_k == DC - 1)),
                            wait=w, inc="pe" if kc == DC - 1 else None)
                    pe_V = ph.c("pe")
                    ph.emit("vector", lambda e, _b=pbK, _c=csl, _m=mc: e.tensor_scalar(
                        tmin[:], _b[:, 0:T2], cbs[:, _c, 8 + _m:9 + _m], 0.0,
                        ALU.add, ALU.min), wait=[("pe", pe_K)], inc="ve")
                    ph.emit("scalar", lambda e: e.activation(emin[:], tmin[:], AF.Exp),
                            wait=[("ve", ph.c("ve"))], inc="se")
                    ph.emit("vector", lambda e, _b=pbK, _c=csl, _m=mc: e.scalar_tensor_tensor(
                        kfb[:], _b[:, 0:T2], cbds[:, _c, 8 + _m:9 + _m], emin[:],
                        ALU.add, ALU.max), wait=[("se", ph.c("se"))], inc="ve")
                    bankcred[bK] = ("ve", ph.c("ve"))
                    ph.emit("vector", lambda e, _b=pbV, _c=csl, _m=mc: e.tensor_scalar_add(
                        vbb[:], _b, cbs[:, _c, 16 + _m:17 + _m]),
                        wait=[("pe", pe_V)], inc="ve")
                    bankcred[bV] = ("ve", ph.c("ve"))
                    ph.emit("vector", lambda e: e.drain())
                    ph.emit("vector", lambda e: e.tensor_mul(prb[:], kfb[:], vbb[:]),
                            inc="ve")
                    wkv = [("ve", ph.c("ve"))]
                    if kvp_free:
                        wkv.append(kvp_free)
                        kvp_free = None
                    ph.emit("scalar", lambda e, _m=mc, _t=t: e.activation(
                        zbf[:, _m % 4, :], prb[:], AF.Copy,
                        accum_out=kvp[:, _m, _t:_t + 1]), wait=wkv, inc="se")
                    ph.emit("scalar", lambda e, _m=mc, _t=t: e.activation(
                        zbf[:, 4 + _m % 4, :], kfb[:], AF.Copy,
                        accum_out=skp[:, _m, _t:_t + 1]), inc="se")
                yc_cons[t] = ph.c("ve")
            ph.wait("vector", ("se", ph.c("se")))
            for mc in range(DC):
                ph.emit("vector", lambda e, _m=mc: e.reduce_sum(
                    kvacc[:, _m:_m + 1], kvp[:, _m, :], axis=AX.X))
                ph.emit("vector", lambda e, _m=mc: e.reduce_sum(
                    skacc[:, _m:_m + 1], skp[:, _m, :], axis=AX.X))
            pe_pass_end[gp] = ph.c("pe")
            prev_ycb_free = ph.c("ve")
            ph.emit("vector", lambda e: e.drain(), inc="ve")
            kvp_free = ("ve", ph.c("ve"))

            gp = 4 * i + 1
            slB = gp % 2
            if gp + 1 not in lw:
                lw[gp + 1] = load_pass_weights(gp + 1)
            yc_cons = {}
            ve_z = {}
            pe_wo_end = 0
            for t in range(NT2):
                sl = t % 2
                snm = ("da", "db")[sl]
                w = []
                if t >= 2:
                    w = [("ve", yc_cons[t - 2])]
                else:
                    w = [("ve", prev_ycb_free)]
                if (i - 1, t) in stY:
                    w.append(stY[(i - 1, t)])
                if i == 0:
                    snm = ("qa", "qb")[sl]
                    ph.emit("gpsimd", lambda e, _s=sl, _t=t: e.dma_start(
                        out=ycb[:, _s], in_=Yd[:, :, _t * T2:(_t + 1) * T2]),
                        wait=w, inc=snm, incby=16)
                else:
                    ph.emit("sync", lambda e, _s=sl, _t=t: e.dma_start(
                        out=ycb[:, _s], in_=Ybd[:, :, _t * T2:(_t + 1) * T2]),
                        wait=w, inc=snm, incby=16)
                lc_y = (snm, ph.c(snm))
                for mc in range(DC):
                    bQ = ("pA", "pB", "pC", "pD")[mc % 4]
                    pbQ = (pA, pB, pC, pD)[mc % 4]
                    for kc in range(DC):
                        w = []
                        if kc == 0:
                            w = [lc_y, lw[gp]] + bwait(bQ)
                        ph.emit("tensor", lambda e, _b=pbQ, _k=kc, _m=mc, _s=sl, _sb=slB: e.matmul(
                            _b[:, 0:T2], wv_qkv(_sb, 0)[:, _k, _m, :],
                            ycb[:, _s, _k, :], start=(_k == 0), stop=(_k == DC - 1)),
                            wait=w, inc="pe" if kc == DC - 1 else None)
                    pe_Q = ph.c("pe")
                    ph.emit("vector", lambda e, _b=pbQ, _c=csl, _m=mc: e.tensor_scalar(
                        tmin[:], _b[:, 0:T2], cbs[:, _c, 0 + _m:1 + _m], 0.0,
                        ALU.add, ALU.min), wait=[("pe", pe_Q)], inc="ve")
                    ph.emit("scalar", lambda e: e.activation(emin[:], tmin[:], AF.Exp),
                            wait=[("ve", ph.c("ve"))], inc="se")
                    ph.emit("vector", lambda e, _b=pbQ, _c=csl, _m=mc: e.scalar_tensor_tensor(
                        kfb[:], _b[:, 0:T2], cbds[:, _c, 0 + _m:1 + _m], emin[:],
                        ALU.add, ALU.max), wait=[("se", ph.c("se"))], inc="ve")
                    bankcred[bQ] = ("ve", ph.c("ve"))
                    ph.emit("scalar", lambda e, _m=mc: e.activation(
                        vbb[:], kfb[:], AF.Copy, bias=1.0e-6,
                        scale=skacc[:, _m:_m + 1]),
                        wait=[("ve", ph.c("ve"))], inc="se")
                    ph.emit("vector", lambda e: e.reciprocal(prb[:], vbb[:]),
                            wait=[("se", ph.c("se"))])
                    ph.emit("vector", lambda e: e.drain())
                    wat = [("pe", pe_wo_end)] if (mc == 0 and pe_wo_end) else []
                    ph.emit("vector", lambda e, _m=mc: e.scalar_tensor_tensor(
                        attb[:, _m, :], kfb[:], kvacc[:, _m:_m + 1], prb[:],
                        ALU.mult, ALU.mult), wait=wat, inc="ve")
                ve_att = ph.c("ve")
                pe_wo_prev = ph.c("pe")
                zdone_se = 0
                for mc in range(DC):
                    bW = ("pA", "pB", "pC", "pD")[mc % 4]
                    pbW = (pA, pB, pC, pD)[mc % 4]
                    for kc in range(DC):
                        w = []
                        if kc == 0:
                            w = [("ve", ve_att)] + bwait(bW)
                        ph.emit("tensor", lambda e, _b=pbW, _k=kc, _m=mc, _sb=slB: e.matmul(
                            _b[:, 0:T2], wv_qkv(_sb, 1)[:, _k, _m, :],
                            attb[:, _k, :], start=(_k == 0), stop=(_k == DC - 1)),
                            wait=w, inc="pe" if kc == DC - 1 else None)
                    pe_W = ph.c("pe")
                    tsl = mc % 2
                    w = [("pe", pe_W)]
                    zkey = mc - 2 if mc >= 2 else mc + 6
                    if zkey in ve_z:
                        w.append(("ve", ve_z[zkey]))
                    ph.emit("scalar", lambda e, _b=pbW, _c=csl, _m=mc, _t=tsl: e.activation(
                        tgb[:, _t, :], _b[:, 0:T2], AF.Gelu_apprx_tanh,
                        bias=cbs[:, _c, 24 + _m:25 + _m]), wait=w, inc="se")
                    bankcred[bW] = ("se", ph.c("se"))
                    ph.emit("vector", lambda e, _m=mc, _t=tsl, _s=sl: e.tensor_add(
                        zbf[:, _m, :], tgb[:, _t, :], ycb[:, _s, _m, :]),
                        wait=[("se", ph.c("se"))], inc="ve")
                    ve_z[mc] = ph.c("ve")
                pe_wo_end = pe_W
                ph.emit("vector", lambda e: e.drain())
                for mc in range(DC):
                    ph.emit("vector", lambda e, _m=mc: e.tensor_copy(
                        zb16[:, _m, :], zbf[:, _m, :]),
                        inc="ve" if mc == DC - 1 else None)
                vz16 = ph.c("ve")
                for mc in range(DC):
                    wsq = [("ve", vz16), ("pe", pe_wo_end)] if mc == 0 else []
                    ph.emit("scalar", lambda e, _m=mc: e.activation(
                        attb[:, _m, :], zbf[:, _m, :], AF.Square),
                        wait=wsq, inc="se" if mc == DC - 1 else None)
                zdone_se = ph.c("se")
                yc_cons[t] = ph.c("ve")
                stp = stYM.get((i, t - 2)) or stY.get((i - 1, NT2 - 2 + t))
                stYM[(i, t)] = ln_fm_tail(i, t, csl, 32, 40, YMd, zdone_se,
                                          ("ea", "eb"), stp)
                pe_wo_end = ph.c("pe")
            pe_pass_end[gp] = ph.c("pe")
            prev_ycb_free = ph.c("ve")

            gp = 4 * i + 2
            slC = gp % 2
            if gp + 1 not in lw:
                lw[gp + 1] = load_pass_weights(gp + 1)
            yc_cons = {}
            st_h = {}
            for t in range(NT2):
                sl = t % 2
                w = []
                if t >= 2:
                    w = [("pe", yc_cons[t - 2])]
                else:
                    w = [("ve", prev_ycb_free)]
                w.append(stYM[(i, t)])
                snm = ("da", "db")[sl]
                ph.emit("sync", lambda e, _s=sl, _t=t: e.dma_start(
                    out=ycb[:, _s], in_=YMd[:, :, _t * T2:(_t + 1) * T2]),
                    wait=w, inc=snm, incby=16)
                lc_y = (snm, ph.c(snm))
                for mc in range(32):
                    bW = ("pA", "pB", "pC", "pD")[mc % 4]
                    pbW = (pA, pB, pC, pD)[mc % 4]
                    for kc in range(DC):
                        w = []
                        if kc == 0:
                            w = [lc_y, lw[gp]] + bwait(bW)
                        ph.emit("tensor", lambda e, _b=pbW, _k=kc, _m=mc, _s=sl, _sc=slC: e.matmul(
                            _b[:, 0:T2], wv_w1(_sc)[:, _k, _m, :],
                            ycb[:, _s, _k, :], start=(_k == 0), stop=(_k == DC - 1)),
                            wait=w, inc="pe" if kc == DC - 1 else None)
                    pe_W = ph.c("pe")
                    w = [("pe", pe_W)]
                    if t >= 2 and mc == 0:
                        w.append(st_h[t - 2])
                    ph.emit("scalar", lambda e, _b=pbW, _c=csl, _m=mc, _s=sl: e.activation(
                        hshr[:, _s, _m, :], _b[:, 0:T2], AF.Gelu_apprx_tanh,
                        bias=cbs[:, _c, 64 + _m:65 + _m]), wait=w, inc="se")
                    bankcred[bW] = ("se", ph.c("se"))
                yc_cons[t] = ph.c("pe")
                snmh = ("ha", "hb")[sl]
                ph.emit("sync", lambda e, _s=sl, _t=t: e.dma_start(
                    out=Hd[:, :, _t * T2:(_t + 1) * T2], in_=hshr[:, _s]),
                    wait=[("se", ph.c("se"))], inc=snmh, incby=16)
                st_h[t] = (snmh, ph.c(snmh))
                stH[(i, t)] = (snmh, ph.c(snmh))
            pe_pass_end[gp] = ph.c("pe")
            prev_ycb_free = ph.c("pe")

            gp = 4 * i + 3
            slD = gp % 2
            if gp + 1 not in lw and gp + 1 < 4 * NSL:
                lw[gp + 1] = load_pass_weights(gp + 1)
            hb_cons = {}
            ym_cons = {}
            ve_z = {}
            for t in range(NT2):
                sl = t % 2
                w = [stH[(i, t)]]
                if t >= 2:
                    w.append(("pe", hb_cons[t - 2]))
                else:
                    w.append(stH[(i, NT2 - 2 + t)])
                snmg = ("ga", "gb")[sl]
                ph.emit("sync", lambda e, _s=sl, _t=t: e.dma_start(
                    out=hshr[:, _s], in_=Hd[:, :, _t * T2:(_t + 1) * T2]),
                    wait=w, inc=snmg, incby=16)
                ld_h = (snmg, ph.c(snmg))
                w = [stYM[(i, t)]]
                if t >= 2:
                    w.append(("ve", ym_cons[t - 2]))
                else:
                    w.append(("pe", pe_pass_end[gp - 1]))
                snm = ("da", "db")[sl]
                ph.emit("sync", lambda e, _s=sl, _t=t: e.dma_start(
                    out=ycb[:, _s], in_=YMd[:, :, _t * T2:(_t + 1) * T2]),
                    wait=w, inc=snm, incby=16)
                lc_y = (snm, ph.c(snm))
                zdone_se = 0
                for mc in range(DC):
                    bW = ("pA", "pB", "pC", "pD")[mc % 4]
                    pbW = (pA, pB, pC, pD)[mc % 4]
                    for kc in range(32):
                        w = []
                        if kc == 0:
                            w = [ld_h, lc_y, lw[gp]] + bwait(bW)
                        ph.emit("tensor", lambda e, _b=pbW, _k=kc, _m=mc, _s=sl, _sd=slD: e.matmul(
                            _b[:, 0:T2], wv_w2(_sd)[:, _k, _m, :],
                            hshr[:, _s, _k, :], start=(_k == 0), stop=(_k == 31)),
                            wait=w, inc="pe" if kc == 31 else None)
                    pe_W = ph.c("pe")
                    tsl = mc % 2
                    w = [("pe", pe_W)]
                    zkey = mc - 2 if mc >= 2 else mc + 6
                    if zkey in ve_z:
                        w.append(("ve", ve_z[zkey]))
                    ph.emit("scalar", lambda e, _b=pbW, _c=csl, _m=mc, _t=tsl: e.activation(
                        tgb[:, _t, :], _b[:, 0:T2], AF.Gelu_apprx_tanh,
                        bias=cbs[:, _c, 96 + _m:97 + _m]), wait=w, inc="se")
                    bankcred[bW] = ("se", ph.c("se"))
                    ph.emit("vector", lambda e, _m=mc, _t=tsl, _s=sl: e.tensor_add(
                        zbf[:, _m, :], tgb[:, _t, :], ycb[:, _s, _m, :]),
                        wait=[("se", ph.c("se"))], inc="ve")
                    ve_z[mc] = ph.c("ve")
                pe_w2_end = pe_W
                ph.emit("vector", lambda e: e.drain())
                for mc in range(DC):
                    ph.emit("vector", lambda e, _m=mc: e.tensor_copy(
                        zb16[:, _m, :], zbf[:, _m, :]),
                        inc="ve" if mc == DC - 1 else None)
                vz16 = ph.c("ve")
                for mc in range(DC):
                    wsq = [("ve", vz16), ("pe", pe_w2_end)] if mc == 0 else []
                    ph.emit("scalar", lambda e, _m=mc: e.activation(
                        attb[:, _m, :], zbf[:, _m, :], AF.Square),
                        wait=wsq, inc="se" if mc == DC - 1 else None)
                zdone_se = ph.c("se")
                hb_cons[t] = ph.c("pe")
                ym_cons[t] = ph.c("ve")
                stp = stY.get((i, t - 2)) or stYM.get((i, NT2 - 2 + t))
                stY[(i, t)] = ln_fm_tail(i, t, csl, 48, 56, Ybd, zdone_se,
                                         ("fa", "fb"), stp)
            pe_pass_end[gp] = ph.c("pe")
            prev_ycb_free = ph.c("ve")
            cbs_free[i] = ph.c("ve")
        for nmx in ("ea", "eb", "fa", "fb", "ha", "hb"):
            ph.wait("sync", (nmx, ph.c(nmx)))
        ph.barrier()
        ph.run()

    with contextlib.ExitStack() as p3s:
        x1b = p3s.enter_context(nc.sbuf_tensor("p3_x1", [128, 2, D], F32))
        x1n = p3s.enter_context(nc.sbuf_tensor("p3_x1n", [128, D], F32))
        ytb = p3s.enter_context(nc.sbuf_tensor("p3_yt", [128, 2, 8, 128], BF16))
        pTY = p3s.enter_context(nc.psum_tensor("pTY", [128, 1024], BF16))
        sqd3 = p3s.enter_context(nc.sbuf_tensor("p3_sq", [128, D], F32))
        tp3 = p3s.enter_context(nc.sbuf_tensor("p3_tp", [128, 8], F32))
        sd3 = p3s.enter_context(nc.sbuf_tensor("p3_sd", [128, 1], F32))
        rr3 = p3s.enter_context(nc.sbuf_tensor("p3_rr", [128, 1], F32))
        xnt = p3s.enter_context(nc.sbuf_tensor("p3_xnt", [128, D], BF16))
        xnf = p3s.enter_context(nc.sbuf_tensor("p3_xnf", [128, 2, 8, 128], BF16))
        pT3 = p3s.enter_context(nc.psum_tensor("pT3", [128, 1024], BF16))
        ph = Ph(nc, sems, cnt)
        ld_c = {}
        ve_c = {}
        st_c = {}
        pe_xn = {}
        stx1_p3 = {}
        for t in range(32):
            sl = t % 2
            snd = ("da", "db")[sl]
            sng = ("ga", "gb")[sl]
            w = []
            if t >= 2:
                w.append(("ve", ve_c[t - 2]))
            ph.emit("sync", lambda e, _s=sl, _t=t: e.dma_start(
                out=x1b[:, _s], in_=X1d[_t * 128:(_t + 1) * 128, :]),
                wait=w, inc=snd, incby=16)
            ld_x1p = (snd, ph.c(snd))
            wy = [("pe", pe_xn[t - 2])] if t >= 2 else []
            ph.emit("sync", lambda e, _s=sl, _t=t: e.dma_start(
                out=ytb[:, _s], in_=Ybd[:, :, _t * 128:(_t + 1) * 128]),
                wait=wy, inc=sng, incby=16)
            ld_c[t] = (sng, ph.c(sng))
            for c in range(DC):
                w = []
                if c == 0:
                    w = [ld_x1p, ld_c[t]] + bwait("pTY")
                ph.emit("tensor", lambda e, _c=c, _s=sl: e.transpose(
                    pTY[:, _c * 128:(_c + 1) * 128],
                    ytb[:, _s, _c, :], idb[:]),
                    wait=w, inc="pe" if c == DC - 1 else None)
            pe_now = ph.c("pe")
            for c2 in range(DC):
                wtt = [("pe", pe_now)] if c2 == 0 else []
                if c2 == 0 and t >= 1:
                    wtt.append(stx1_p3[t - 1])
                ph.emit("vector", lambda e, _c2=c2, _s=sl: e.tensor_add(
                    x1n[:, _c2 * 128:(_c2 + 1) * 128],
                    x1b[:, _s, _c2 * 128:(_c2 + 1) * 128],
                    pTY[:, _c2 * 128:(_c2 + 1) * 128]),
                    wait=wtt,
                    inc="ve" if c2 == DC - 1 else None)
            bankcred["pTY"] = ("ve", ph.c("ve"))
            ph.emit("sync", lambda e, _t=t: e.dma_start(
                out=X1d[_t * 128:(_t + 1) * 128, :], in_=x1n[:]),
                wait=[("ve", ph.c("ve"))], inc="ea", incby=16)
            stx1_p3[t] = ("ea", ph.c("ea"))
            _ln_tm(ph, x1n[:], sqd3, tp3, sd3, rr3, 1.0e-5,
                   wait_in=[("ve", ph.c("ve"))])
            wx = [("pe", pe_xn[t - 1])] if t >= 1 else []
            ph.emit("vector", lambda e: e.tensor_scalar(
                xnt[:], x1n[:], tp3[:, 4:5], rr3[:], ALU.subtract, ALU.mult),
                wait=wx, inc="ve")
            ve_now = ph.c("ve")
            w = [("ve", ve_now)]
            if t >= 1:
                w.append(("ve", xnf_ev[t - 1]))
            for c in range(DC):
                ph.emit("tensor", lambda e, _c=c: e.transpose(
                    pT3[:, _c * 128:(_c + 1) * 128],
                    xnt[:, _c * 128:(_c + 1) * 128], idb[:]),
                    wait=w if c == 0 else (), inc="pe" if c == DC - 1 else None)
            pe_xn[t] = ph.c("pe")
            wv = [("pe", pe_xn[t])]
            if t >= 2:
                wv.append(st_c[t - 2])
            ph.emit("vector", lambda e, _s=sl: e.tensor_copy(
                xnf[:, _s].rearrange("p a b -> p (a b)"), pT3[:]),
                wait=wv, inc="ve")
            if t == 0:
                xnf_ev = {}
            xnf_ev[t] = ph.c("ve")
            snf = ("fa", "fb")[sl]
            ph.emit("sync", lambda e, _s=sl, _t=t: e.dma_start(
                out=XN2d[:, :, _t * 128:(_t + 1) * 128], in_=xnf[:, _s]),
                wait=[("ve", ph.c("ve"))], inc=snf, incby=16)
            st_c[t] = (snf, ph.c(snf))
            ve_c[t] = ph.c("ve")
        for nmx in ("ea", "fa", "fb"):
            ph.wait("sync", (nmx, ph.c(nmx)))
        ph.barrier()
        ph.run()

    with contextlib.ExitStack() as p4s:
        Kfm = p4s.enter_context(nc.sbuf_tensor("p4_k", [128, 8, S], BF16))
        Vtm = p4s.enter_context(nc.sbuf_tensor("p4_v", [128, 32, 1024], BF16))
        xnb4 = p4s.enter_context(nc.sbuf_tensor("p4_xn", [128, 2, 8, 512], BF16))
        pT4 = p4s.enter_context(nc.psum_tensor("pT4", [128, 128], BF16))
        p4i = p4s.enter_context(contextlib.ExitStack())
        wqb = p4i.enter_context(nc.sbuf_tensor("p4_wq", [128, 8, 8, 128], BF16))
        wkb = p4i.enter_context(nc.sbuf_tensor("p4_wk", [128, 8, 8, 128], BF16))
        wvb = p4i.enter_context(nc.sbuf_tensor("p4_wv", [128, 8, 1024], BF16))
        ph = Ph(nc, sems, cnt)
        for dst, src in ((wqb, l1_wq), (wkb, l1_wk), (wvb, l1_wv)):
            ph.emit("gpsimd", lambda e, _d=dst, _s=src: e.dma_start(
                out=_d[:], in_=_s), inc="lc", incby=16)
        lw4 = ph.c("lc")
        xn_cons = {}
        for t in range(8):
            sl = t % 2
            snm = ("da", "db")[sl]
            w = [("pe", xn_cons[t - 2])] if t >= 2 else []
            ph.emit("sync", lambda e, _s=sl, _t=t: e.dma_start(
                out=xnb4[:, _s], in_=XN2d[:, :, _t * 512:(_t + 1) * 512]),
                wait=w, inc=snm, incby=16)
            ld_x = (snm, ph.c(snm))
            for mc in range(DC):
                bK = ("pA", "pB")[mc % 2]
                pbK = (pA, pB)[mc % 2]
                for kc in range(DC):
                    w = []
                    if kc == 0:
                        w = [ld_x, ("lc", lw4)] + bwait(bK)
                    ph.emit("tensor", lambda e, _b=pbK, _k=kc, _m=mc, _s=sl: e.matmul(
                        _b[:, 0:512], wkb[:, _k, _m, :], xnb4[:, _s, _k, :],
                        start=(_k == 0), stop=(_k == DC - 1)),
                        wait=w, inc="pe" if kc == DC - 1 else None)
                ph.emit("vector", lambda e, _b=pbK, _m=mc, _t=t: e.tensor_copy(
                    Kfm[:, _m, _t * 512:(_t + 1) * 512], _b[:, 0:512]),
                    wait=[("pe", ph.c("pe"))], inc="ve")
                bankcred[bK] = ("ve", ph.c("ve"))
            for g in range(4):
                gabs = t * 4 + g
                for half in range(2):
                    for kc in range(DC):
                        w = []
                        if kc == 0 and half == 0:
                            w = bwait("pO")
                        ph.emit("tensor", lambda e, _k=kc, _h=half, _g=g, _s=sl: e.matmul(
                            pO[:, _h * 512:(_h + 1) * 512],
                            xnb4[:, _s, _k, _g * 128:(_g + 1) * 128],
                            wvb[:, _k, _h * 512:(_h + 1) * 512],
                            start=(_k == 0), stop=(_k == DC - 1)),
                            wait=w, inc="pe" if (kc == DC - 1 and half == 1) else None)
                ph.emit("vector", lambda e, _g=gabs: e.tensor_copy(
                    Vtm[:, _g, :], pO[:]), wait=[("pe", ph.c("pe"))], inc="ve")
                bankcred["pO"] = ("ve", ph.c("ve"))
            xn_cons[t] = ph.c("pe")
        pe_p4a_end = ph.c("pe")
        ph.barrier()
        ph.run()
        p4i.close()
        wqb = p4s.enter_context(nc.sbuf_tensor("p4_wq2", [128, 8, 8, 128], BF16))
        qfm = p4s.enter_context(nc.sbuf_tensor("p4_q", [128, 8, 512], BF16))
        psb = p4s.enter_context(nc.sbuf_tensor("p4_p", [128, 128], BF16))
        ptb = p4s.enter_context(nc.sbuf_tensor("p4_pt", [128, 128], BF16))
        osb = p4s.enter_context(nc.sbuf_tensor("p4_o", [128, 2, 1024], F32))
        oac = p4s.enter_context(nc.sbuf_tensor("p4_oa", [128, 1024], F32))
        mst = p4s.enter_context(nc.sbuf_tensor("p4_m", [128, 8], F32))
        ph = Ph(nc, sems, cnt)
        ph.emit("gpsimd", lambda e: e.dma_start(out=wqb[:], in_=l1_wq),
                inc="w0", incby=16)
        ph.wait("tensor", ("w0", ph.c("w0")))
        sg_o = {}
        pe_tr_prev = 0
        osb_free = None
        for qc in range(32):
            qg, qi = qc // 4, qc % 4
            if qi == 0:
                sl = qg % 2
                snm = ("da", "db")[sl]
                w = ([("pe", xn_cons[qg - 2])] if qg >= 2
                     else [("pe", pe_p4a_end)])
                ph.emit("sync", lambda e, _s=sl, _g=qg: e.dma_start(
                    out=xnb4[:, _s], in_=XN2d[:, :, _g * 512:(_g + 1) * 512]),
                    wait=w, inc=snm, incby=16)
                ld_x = (snm, ph.c(snm))
                for mc in range(DC):
                    bQ = ("pC", "pD")[mc % 2]
                    pbQ = (pC, pD)[mc % 2]
                    for kc in range(DC):
                        w = []
                        if kc == 0:
                            w = [ld_x] + bwait(bQ)
                        ph.emit("tensor", lambda e, _b=pbQ, _k=kc, _m=mc, _s=sl: e.matmul(
                            _b[:, 0:512], wqb[:, _k, _m, :], xnb4[:, _s, _k, :],
                            start=(_k == 0), stop=(_k == DC - 1)),
                            wait=w, inc="pe" if kc == DC - 1 else None)
                    ph.emit("vector", lambda e, _b=pbQ, _m=mc: e.tensor_copy(
                        qfm[:, _m, :], _b[:, 0:512]),
                        wait=[("pe", ph.c("pe"))], inc="ve")
                    bankcred[bQ] = ("ve", ph.c("ve"))
                xn_cons[qg] = ph.c("pe")
                ve_q = ph.c("ve")
            ph.emit("vector", lambda e: e.memset(mst[:, 0:1], -1.0e30))
            ph.emit("vector", lambda e: e.memset(mst[:, 1:2], 0.0))
            ph.emit("vector", lambda e: e.memset(oac[:], 0.0),
                    wait=[osb_free] if osb_free else [])
            ph.emit("vector", lambda e: e.drain(), inc="ve")
            for kb in range(qc + 1):
                bS = ("pA", "pB")[kb % 2]
                pbS = (pA, pB)[kb % 2]
                for kc in range(DC):
                    w = []
                    if kc == 0:
                        w = [("ve", ve_q)] + bwait(bS)
                    ph.emit("tensor", lambda e, _b=pbS, _k=kc, _kb=kb, _qi=qi: e.matmul(
                        _b[:, 0:128], qfm[:, _k, _qi * 128:(_qi + 1) * 128],
                        Kfm[:, _k, _kb * 128:(_kb + 1) * 128],
                        start=(_k == 0), stop=(_k == DC - 1)),
                        wait=w, inc="pe" if kc == DC - 1 else None)
                pe_s = ph.c("pe")
                if kb == qc:
                    ph.emit("vector", lambda e, _b=pbS: e.tensor_add(
                        _b[:, 0:128], _b[:, 0:128], caus[:]),
                        wait=[("pe", pe_s)])
                    ph.emit("vector", lambda e: e.drain(), inc="ve")
                    wrm = [("ve", ph.c("ve"))]
                else:
                    wrm = [("pe", pe_s)]
                ph.emit("vector", lambda e, _b=pbS: e.reduce_max(
                    mst[:, 2:3], _b[:, 0:128], axis=AX.X), wait=wrm)
                ph.emit("vector", lambda e: e.drain())
                ph.emit("vector", lambda e: e.tensor_scalar(
                    mst[:, 3:4], mst[:, 2:3], GAMMA, None, ALU.mult))
                ph.emit("vector", lambda e: e.drain())
                ph.emit("vector", lambda e: e.tensor_max(
                    mst[:, 4:5], mst[:, 0:1], mst[:, 3:4]))
                ph.emit("vector", lambda e: e.drain())
                ph.emit("vector", lambda e: e.tensor_sub(
                    mst[:, 5:6], mst[:, 0:1], mst[:, 4:5]))
                ph.emit("vector", lambda e: e.tensor_scalar_mul(
                    mst[:, 6:7], mst[:, 4:5], -1.0))
                ph.emit("vector", lambda e: e.drain(), inc="ve")
                ph.emit("scalar", lambda e: e.activation(
                    mst[:, 7:8], mst[:, 5:6], AF.Exp),
                    wait=[("ve", ph.c("ve"))], inc="se")
                wp = [("pe", pe_tr_prev)] if pe_tr_prev else []
                ph.emit("scalar", lambda e, _b=pbS: e.activation(
                    psb[:], _b[:, 0:128], AF.Exp, bias=mst[:, 6:7], scale=GAMMA,
                    accum_out=mst[:, 2:3]), wait=wp, inc="se")
                se_p = ph.c("se")
                bankcred[bS] = ("se", se_p)
                ph.emit("vector", lambda e: e.scalar_tensor_tensor(
                    mst[:, 1:2], mst[:, 1:2], mst[:, 7:8], mst[:, 2:3],
                    ALU.mult, ALU.add), wait=[("se", se_p)])
                ph.emit("vector", lambda e: e.tensor_copy(mst[:, 0:1], mst[:, 4:5]))
                ph.emit("vector", lambda e: e.drain(), inc="ve")
                w = [("se", se_p)] + bwait("pT4")
                ph.emit("tensor", lambda e: e.transpose(pT4[:], psb[:], idb[:]),
                        wait=w, inc="pe")
                pe_tr_prev = ph.c("pe")
                ph.emit("vector", lambda e: e.tensor_copy(ptb[:], pT4[:]),
                        wait=[("pe", ph.c("pe"))], inc="ve")
                bankcred["pT4"] = ("ve", ph.c("ve"))
                ve_pt = ph.c("ve")
                for half in range(2):
                    w = [("ve", ve_pt)] + (bwait("pO") if half == 0 else [])
                    ph.emit("tensor", lambda e, _h=half, _kb=kb: e.matmul(
                        pO[:, _h * 512:(_h + 1) * 512], ptb[:],
                        Vtm[:, _kb, _h * 512:(_h + 1) * 512],
                        start=True, stop=True),
                        wait=w, inc="pe" if half == 1 else None)
                pe_pv = ph.c("pe")
                ph.emit("vector", lambda e: e.drain())
                for half in range(2):
                    ph.emit("vector", lambda e, _h=half: e.scalar_tensor_tensor(
                        oac[:, _h * 512:(_h + 1) * 512],
                        oac[:, _h * 512:(_h + 1) * 512], mst[:, 7:8],
                        pO[:, _h * 512:(_h + 1) * 512], ALU.mult, ALU.add),
                        wait=[("pe", pe_pv)] if half == 0 else [],
                        inc="ve" if half == 1 else None)
                bankcred["pO"] = ("ve", ph.c("ve"))
            ph.emit("vector", lambda e: e.reciprocal(mst[:, 2:3], mst[:, 1:2]))
            ph.emit("vector", lambda e: e.drain())
            osl = qc % 2
            sno = ("ra", "rb")[osl]
            w = [sg_o[qc - 2]] if qc >= 2 else []
            ph.emit("vector", lambda e, _o=osl: e.tensor_scalar_mul(
                osb[:, _o, :], oac[:], mst[:, 2:3]), wait=w, inc="ve")
            osb_free = ("ve", ph.c("ve"))
            ph.emit("gpsimd", lambda e, _o=osl, _q=qc: e.dma_start(
                out=X1d[_q * 128:(_q + 1) * 128, :], in_=osb[:, _o, :],
                accum_op=ALU.add), wait=[("ve", ph.c("ve"))], inc=sno, incby=16)
            sg_o[qc] = (sno, ph.c(sno))
        for nmx in ("ra", "rb"):
            ph.wait("gpsimd", (nmx, ph.c(nmx)))
        ph.barrier()
        ph.run()

    T5 = 256
    with contextlib.ExitStack() as p5s:
        wgb = p5s.enter_context(nc.sbuf_tensor("p5_wg", [128, 8, 24, 128], BF16))
        wub = p5s.enter_context(nc.sbuf_tensor("p5_wu", [128, 8, 24, 128], BF16))
        wdb = p5s.enter_context(nc.sbuf_tensor("p5_wd", [128, 24, 1024], BF16))
        xnb5 = p5s.enter_context(nc.sbuf_tensor("p5_xn", [128, 2, 8, T5], BF16))
        hb5 = p5s.enter_context(nc.sbuf_tensor("p5_h", [128, 24, T5], BF16))
        st5 = p5s.enter_context(nc.sbuf_tensor("p5_st", [128, 2, T5], F32))
        x1t5 = p5s.enter_context(nc.sbuf_tensor("p5_x1", [128, 4, 1024], F32))
        ob5 = p5s.enter_context(nc.sbuf_tensor("p5_o", [128, 4, 1024], F32))
        ph = Ph(nc, sems, cnt)
        for dst, src in ((wgb, l1_wg), (wub, l1_wu), (wdb, l1_wd)):
            ph.emit("gpsimd", lambda e, _d=dst, _s=src: e.dma_start(
                out=_d[:], in_=_s), inc="lc", incby=16)
        lw5 = ph.c("lc")
        NT5 = S // T5
        NG5 = T5 // 128
        xn_cons = {}
        se_h = {}
        ve_h = {}
        st_out = {}
        gi_abs = 0
        for t in range(NT5):
            sl = t % 2
            snm = ("da", "db")[sl]
            w = [("pe", xn_cons[t - 2])] if t >= 2 else []
            ph.emit("sync", lambda e, _s=sl, _t=t: e.dma_start(
                out=xnb5[:, _s], in_=XN2d[:, :, _t * T5:(_t + 1) * T5]),
                wait=w, inc=snm, incby=16)
            ld_x = (snm, ph.c(snm))
            ld_x1w = []
            for g in range(NG5):
                gab = t * NG5 + g
                xsl = gab % 4
                sng = ("ga", "gb", "gc", "gd")[xsl]
                w = [("ve", ob_ev[gab - 4])] if gab >= 4 else []
                ph.emit("sync", lambda e, _x=xsl, _g=gab: e.dma_start(
                    out=x1t5[:, _x, :], in_=X1d[_g * 128:(_g + 1) * 128, :]),
                    wait=w, inc=sng, incby=16)
                ld_x1w.append((sng, ph.c(sng)))
            for c in range(24):
                bG = ("pA", "pC")[c % 2]
                bU = ("pB", "pD")[c % 2]
                pbG = (pA, pC)[c % 2]
                pbU = (pB, pD)[c % 2]
                for kc in range(DC):
                    w = []
                    if kc == 0:
                        w = [ld_x, ("lc", lw5)] + bwait(bG)
                    ph.emit("tensor", lambda e, _b=pbG, _k=kc, _c=c, _s=sl: e.matmul(
                        _b[:, 0:T5], wgb[:, _k, _c, :], xnb5[:, _s, _k, :],
                        start=(_k == 0), stop=(_k == DC - 1)),
                        wait=w, inc="pe" if kc == DC - 1 else None)
                pe_g = ph.c("pe")
                for kc in range(DC):
                    w = []
                    if kc == 0:
                        w = bwait(bU)
                    ph.emit("tensor", lambda e, _b=pbU, _k=kc, _c=c, _s=sl: e.matmul(
                        _b[:, 0:T5], wub[:, _k, _c, :], xnb5[:, _s, _k, :],
                        start=(_k == 0), stop=(_k == DC - 1)),
                        wait=w, inc="pe" if kc == DC - 1 else None)
                pe_u = ph.c("pe")
                ssl = c % 2
                ws = [("pe", pe_g)]
                key = c - 2
                if key in ve_h:
                    ws.append(("ve", ve_h[key]))
                elif c < 2 and t > 0:
                    ws.append(("ve", prev_ve_h))
                ph.emit("scalar", lambda e, _b=pbG, _s=ssl: e.activation(
                    st5[:, _s, :], _b[:, 0:T5], AF.Silu, scale=QSCALE),
                    wait=ws, inc="se")
                bankcred[bG] = ("se", ph.c("se"))
                ph.emit("vector", lambda e, _b=pbU, _c=c, _s=ssl: e.scalar_tensor_tensor(
                    hb5[:, _c, :], _b[:, 0:T5], QSCALE, st5[:, _s, :],
                    ALU.mult, ALU.mult),
                    wait=[("se", ph.c("se")), ("pe", pe_u)], inc="ve")
                ve_h[c] = ph.c("ve")
                bankcred[bU] = ("ve", ph.c("ve"))
            prev_ve_h = ph.c("ve")
            ve_h = {}
            ve_hall = prev_ve_h
            for g in range(NG5):
                gab = t * NG5 + g
                xsl = gab % 4
                for half in range(2):
                    for kc in range(24):
                        w = []
                        if kc == 0:
                            w = [("ve", ve_hall)]
                            if half == 0:
                                w += bwait("pO")
                        ph.emit("tensor", lambda e, _k=kc, _h=half, _g=g: e.matmul(
                            pO[:, _h * 512:(_h + 1) * 512],
                            hb5[:, _k, _g * 128:(_g + 1) * 128],
                            wdb[:, _k, _h * 512:(_h + 1) * 512],
                            start=(_k == 0), stop=(_k == 23)),
                            wait=w, inc="pe" if kc == 23 else None)
                pe_o5 = ph.c("pe")
                w = [("pe", pe_o5), ld_x1w[g]]
                if gab >= 4:
                    w.append(st_out[gab - 4])
                ph.emit("vector", lambda e, _x=xsl: e.scalar_tensor_tensor(
                    ob5[:, _x, :], pO[:], QSCALE, x1t5[:, _x, :],
                    ALU.mult, ALU.add), wait=w, inc="ve")
                if gab == 0:
                    ob_ev = {}
                ob_ev[gab] = ph.c("ve")
                bankcred["pO"] = ("ve", ph.c("ve"))
                sne = ("ea", "eb", "ec", "ed")[xsl]
                ph.emit("sync", lambda e, _x=xsl, _g=gab: e.dma_start(
                    out=out_t[_g * 128:(_g + 1) * 128, :], in_=ob5[:, _x, :]),
                    wait=[("ve", ph.c("ve"))], inc=sne, incby=16)
                st_out[gab] = (sne, ph.c(sne))
            xn_cons[t] = ph.c("pe")
        for nmx in ("ea", "eb", "ec", "ed"):
            ph.wait("sync", (nmx, ph.c(nmx)))
        ph.barrier()
        ph.run()

    ph = Ph(nc, sems, cnt)
    for dst, src in dbg_outs:
        ph.emit("sync", lambda e, _d=dst, _s=src: e.dma_start(out=_d, in_=_s),
                inc="st", incby=16)
    if dbg_comb is not None:
        ph.emit("sync", lambda e: e.dma_start(out=dbg_comb, in_=combt[:]),
                inc="st", incby=16)
    ph.wait("sync", ("st", ph.c("st")))
    ph.barrier()
    ph.run()

    stack.close()
    return nc


def _lhsT_tiles(w):
    K, M = w.shape
    return np.ascontiguousarray(
        w.reshape(K // 128, 128, M // 128, 128).transpose(1, 0, 2, 3))


def _rhs_tiles(w):
    K, N = w.shape
    return np.ascontiguousarray(w.reshape(K // 128, 128, N).transpose(1, 0, 2))


def _fmvec(v):
    n = v.shape[0] // 128
    return np.ascontiguousarray(v.reshape(n, 128).T)


def host_prep(inputs):
    import ml_dtypes
    g = {}
    wq = np.asarray(inputs["L0_attn_Wq"])
    wk = np.asarray(inputs["L0_attn_Wk"])
    wv = np.asarray(inputs["L0_attn_Wv"])
    def qkv(w):
        weff = np.ascontiguousarray(w.transpose(0, 2, 1, 3).reshape(NSL, D, D))
        return np.stack([_lhsT_tiles(weff[i]) for i in range(NSL)])
    g["l0_wq"] = qkv(wq)
    g["l0_wk"] = qkv(wk)
    g["l0_wv"] = qkv(wv)
    g["l0_wo"] = np.stack([_lhsT_tiles(np.asarray(inputs["L0_attn_Wo"])[i])
                           for i in range(NSL)])
    g["l0_w1"] = np.stack([_lhsT_tiles(np.asarray(inputs["L0_attn_W1"])[i])
                           for i in range(NSL)])
    g["l0_w2"] = np.stack([_lhsT_tiles(np.asarray(inputs["L0_attn_W2"])[i])
                           for i in range(NSL)])
    cbank = np.zeros((128, NSL, 104), np.float32)
    for i in range(NSL):
        cbank[:, i, 0:8] = _fmvec(np.asarray(inputs["L0_attn_bq"])[i].reshape(-1))
        cbank[:, i, 8:16] = _fmvec(np.asarray(inputs["L0_attn_bk"])[i].reshape(-1))
        cbank[:, i, 16:24] = _fmvec(np.asarray(inputs["L0_attn_bv"])[i].reshape(-1))
        cbank[:, i, 24:32] = _fmvec(np.asarray(inputs["L0_attn_bo"])[i])
        cbank[:, i, 32:40] = _fmvec(np.asarray(inputs["L0_attn_ln1_s"])[i])
        cbank[:, i, 40:48] = _fmvec(np.asarray(inputs["L0_attn_ln1_b"])[i])
        cbank[:, i, 48:56] = _fmvec(np.asarray(inputs["L0_attn_ln2_s"])[i])
        cbank[:, i, 56:64] = _fmvec(np.asarray(inputs["L0_attn_ln2_b"])[i])
        cbank[:, i, 64:96] = _fmvec(np.asarray(inputs["L0_attn_b1"])[i])
        cbank[:, i, 96:104] = _fmvec(np.asarray(inputs["L0_attn_b2"])[i])
    g["l0_cb"] = cbank
    g["moe_gw"] = _rhs_tiles(np.ascontiguousarray(
        np.asarray(inputs["L0_ffn_gate_w"]).T[:, EPERM]))
    w13 = np.asarray(inputs["L0_ffn_w13_weight"])
    g["moe_w13"] = np.stack([_lhsT_tiles(np.ascontiguousarray(w13[e].T))
                             for e in range(NE)])
    w2m = np.asarray(inputs["L0_ffn_w2_weight"])
    g["moe_w2"] = np.stack([_rhs_tiles(np.ascontiguousarray(w2m[e].T))
                            for e in range(NE)])
    g["l1_wq"] = _lhsT_tiles(np.asarray(inputs["L1_attn_w_q"]))
    g["l1_wk"] = _lhsT_tiles(np.asarray(inputs["L1_attn_w_k"]))
    g["l1_wv"] = _rhs_tiles(np.asarray(inputs["L1_attn_w_v"]))
    g["l1_wg"] = _lhsT_tiles(np.ascontiguousarray(
        np.asarray(inputs["L1_ffn_w_gate"]).T))
    g["l1_wu"] = _lhsT_tiles(np.ascontiguousarray(
        np.asarray(inputs["L1_ffn_w_up"]).T))
    g["l1_wd"] = _rhs_tiles(np.ascontiguousarray(
        np.asarray(inputs["L1_ffn_w_down"]).T))
    import ml_dtypes
    g["identb"] = np.eye(128, dtype=ml_dtypes.bfloat16)
    g["identf"] = np.eye(128, dtype=np.float32)
    g["ones_stat"] = np.full((128, 1), 1.0 / 1024.0, ml_dtypes.bfloat16)
    g["ones_row"] = np.ones((1, 128), np.float32)
    m = np.zeros((128, 128), np.float32)
    m[np.triu_indices(128, 1)] = NEG
    g["causal"] = m
    return g


def kernel(**inputs):
    g = host_prep(inputs)
    nc = build()
    x = np.asarray(inputs["L0_attn_x"])
    in_maps = []
    for n in range(8):
        m = dict(g)
        m["x"] = np.ascontiguousarray(x[n])
        in_maps.append(m)
    res = run_bass_kernel_spmd(nc, in_maps, core_ids=list(range(8)))
    return np.stack([res.results[n]["out"] for n in range(8)])


# version 04
# speedup vs baseline: 1.0394x; 1.0296x over previous
import numpy as np
import concourse.bass as bass
import concourse.mybir as mybir
from concourse.bass_utils import run_bass_kernel_spmd

F32 = mybir.dt.float32
BF16 = mybir.dt.bfloat16
I32 = mybir.dt.int32
AF = mybir.ActivationFunctionType
ALU = mybir.AluOpType
AX = mybir.AxisListType

S = 4096
D = 1024
DC = 8
NSL = 8
FF = 4096
FFC = 32
NE = 8
MI = 2048
L1H = 3072
L1HC = 24
NEG = -1.0e9
GAMMA = 1.0 / 32.0
QSCALE = 1.0 / 127.0
EPERM = [0, 2, 4, 6, 1, 3, 5, 7]
EPOS = [0, 4, 1, 5, 2, 6, 3, 7]

ENGINES = ("tensor", "vector", "scalar", "sync", "gpsimd")


class Ph:

    def __init__(self, nc, sems, cnt):
        self.nc = nc
        self.sems = sems
        self.cnt = cnt
        self.ops = {e: [] for e in ENGINES}

    def c(self, name):
        return self.cnt[name]

    def emit(self, eng, fn, wait=(), inc=None, incby=1):
        w = tuple(wait)
        if inc is not None:
            self.cnt[inc] += incby
        sems = self.sems

        def op(e, _w=w, _fn=fn, _inc=inc, _by=incby):
            for s, v in _w:
                e.wait_ge(sems[s], v)
            inst = _fn(e)
            if _inc is not None:
                inst.then_inc(sems[_inc], _by)
        self.ops[eng].append(op)

    def wait(self, eng, *pairs):
        sems = self.sems

        def op(e, _p=tuple(pairs)):
            for s, v in _p:
                e.wait_ge(sems[s], v)
        self.ops[eng].append(op)

    def barrier(self):
        target = self.cnt["bar"] + len(ENGINES)
        for eng in ENGINES:
            self.emit(eng, lambda e: e.nop(), inc="bar")
        for eng in ENGINES:
            self.wait(eng, ("bar", target))

    def run(self, scope=None):
        if scope is not None:
            with self.nc.named_scope(scope):
                return self._run_inner()
        return self._run_inner()

    def _run_inner(self):
        with self.nc.Block() as b:
            for eng in ENGINES:
                ops = self.ops[eng]
                if ops:
                    getattr(b, eng)(lambda e, _o=ops: [f(e) for f in _o])
        self.ops = {e: [] for e in ENGINES}


def _ln_tm(ph, x_ap, sqd, tp, sd, rr, eps, nfree=D, wait_in=()):
    inv = 1.0 / nfree
    ph.emit("vector", lambda e: e.drain())
    ph.emit("vector", lambda e, _x=x_ap: e.reduce_sum(tp[:, 0:1], _x, axis=AX.X))
    ph.emit("vector", lambda e: e.drain(), inc="ve")
    ph.emit("scalar", lambda e, _x=x_ap: e.activation(
        sqd[:, :nfree], _x, AF.Square, accum_out=tp[:, 1:2]),
        wait=tuple(wait_in), inc="se")
    ph.wait("vector", ("se", ph.c("se")))
    ph.emit("vector", lambda e: e.tensor_scalar_mul(tp[:, 4:5], tp[:, 0:1], inv))
    ph.emit("vector", lambda e: e.tensor_scalar_mul(tp[:, 3:4], tp[:, 1:2], inv))
    ph.emit("vector", lambda e: e.drain())
    ph.emit("vector", lambda e: e.tensor_mul(tp[:, 2:3], tp[:, 4:5], tp[:, 4:5]))
    ph.emit("vector", lambda e: e.drain())
    ph.emit("vector", lambda e: e.tensor_sub(tp[:, 5:6], tp[:, 3:4], tp[:, 2:3]),
            inc="ve")
    ph.emit("scalar", lambda e: e.activation(sd[:], tp[:, 5:6], AF.Sqrt, bias=eps),
            wait=(("ve", ph.c("ve")),), inc="se")
    ph.emit("vector", lambda e: e.reciprocal(rr[:], sd[:]),
            wait=(("se", ph.c("se")),))
    ph.emit("vector", lambda e: e.drain(), inc="ve")


def build(dump=(), nsl=NSL):
    nc = bass.Bass("TRN2", target_bir_lowering=False)

    def din(name, shape, dt=F32):
        return nc.dram_tensor(name, shape, dt, kind="ExternalInput").ap()

    x_in = din("x", [S, D])
    l0_wq = din("l0_wq", [NSL, 128, 8, 8, 128])
    l0_wk = din("l0_wk", [NSL, 128, 8, 8, 128])
    l0_wv = din("l0_wv", [NSL, 128, 8, 8, 128])
    l0_wo = din("l0_wo", [NSL, 128, 8, 8, 128])
    l0_w1 = din("l0_w1", [NSL, 128, 8, 32, 128])
    l0_w2 = din("l0_w2", [NSL, 128, 32, 8, 128])
    l0_cb = din("l0_cb", [128, NSL, 104])
    moe_gw = din("moe_gw", [128, 8, 8])
    moe_w13 = din("moe_w13", [NE, 128, 8, 32, 128])
    moe_w2 = din("moe_w2", [NE, 128, 16, 1024])
    l1_wq = din("l1_wq", [128, 8, 8, 128])
    l1_wk = din("l1_wk", [128, 8, 8, 128])
    l1_wv = din("l1_wv", [128, 8, 1024])
    l1_wg = din("l1_wg", [128, 8, 24, 128], I32)
    l1_wu = din("l1_wu", [128, 8, 24, 128], I32)
    l1_wd = din("l1_wd", [128, 24, 1024], I32)
    identb = din("identb", [128, 128], BF16)
    identf = din("identf", [128, 128])
    ones_stat = din("ones_stat", [128, 1], BF16)
    ones_row = din("ones_row", [1, 128])
    causal = din("causal", [128, 128])

    out_t = nc.dram_tensor("out", [S, D], F32, kind="ExternalOutput").ap()

    def scr(name, shape, dt=F32):
        return nc.dram_tensor(name, shape, dt, kind="Internal").ap()

    Yd = scr("Yd", [128, DC, S])
    Ybd = scr("Ybd", [128, DC, S], BF16)
    YMd = scr("YMd", [128, DC, S], BF16)
    Hd = scr("Hd", [128, FFC, S], BF16)
    X1d = scr("X1d", [S, D])
    XN2d = scr("XN2d", [128, DC, S], BF16)

    dbg_map = {"Y": Yd, "YB": Ybd, "YM": YMd, "X1": X1d, "XN2": XN2d, "H": Hd}
    dbg_outs = []
    dbg_comb = None
    for i, name in enumerate(dump):
        if name == "COMB":
            dbg_comb = nc.dram_tensor(f"dbg{i}", [128, 32, 8], F32,
                                      kind="ExternalOutput").ap()
            continue
        src = dbg_map[name]
        dt = BF16 if name in ("XN2", "H", "YM", "YB") else F32
        dbg_outs.append((nc.dram_tensor(f"dbg{i}", list(src.shape), dt,
                                        kind="ExternalOutput").ap(), src))

    import contextlib
    stack = contextlib.ExitStack()
    pA = stack.enter_context(nc.psum_tensor("pA", [128, 512], F32))
    pB = stack.enter_context(nc.psum_tensor("pB", [128, 512], F32))
    pC = stack.enter_context(nc.psum_tensor("pC", [128, 512], F32))
    pD = stack.enter_context(nc.psum_tensor("pD", [128, 512], F32))
    pO = stack.enter_context(nc.psum_tensor("pO", [128, 1024], F32))

    sem_names = ["ld", "lc", "st", "sg", "pe", "ve", "se", "ge", "bar",
                 "w0", "w1", "w2", "ca", "cb",
                 "da", "db", "ea", "eb", "ec", "ed",
                 "fa", "fb", "ga", "gb", "gc", "gd", "ha", "hb",
                 "qa", "qb", "ra", "rb", "rc", "rd"]
    sems = {n: stack.enter_context(nc.semaphore(n)) for n in sem_names}
    cnt = {n: 0 for n in sem_names}

    idb = stack.enter_context(nc.sbuf_tensor("idb", [128, 128], BF16))
    idf = stack.enter_context(nc.sbuf_tensor("idf", [128, 128], F32))
    onst = stack.enter_context(nc.sbuf_tensor("onst", [128, 1], BF16))
    onsr = stack.enter_context(nc.sbuf_tensor("onsr", [1, 128], F32))
    caus = stack.enter_context(nc.sbuf_tensor("caus", [128, 128], F32))
    combt = stack.enter_context(nc.sbuf_tensor("combt", [128, 32, 8], F32))
    constv = stack.enter_context(nc.sbuf_tensor("constv", [128, 8], F32))
    CONST_VALS = [0.0, 1.0, 1.0e-5, 1.0e-6, GAMMA, QSCALE, -1.0, 2.0]
    for i, v in enumerate(CONST_VALS):
        nc.const_aps.register(F32, v, constv[:, i:i + 1])

    ph = Ph(nc, sems, cnt)
    for i, v in enumerate(CONST_VALS):
        ph.emit("vector", lambda e, _i=i, _v=v: e.memset(constv[:, _i:_i + 1], _v))

    for dst, src in ((idb, identb), (idf, identf), (onst, ones_stat),
                     (onsr, ones_row), (caus, causal)):
        ph.emit("sync", lambda e, _d=dst, _s=src: e.dma_start(out=_d[:], in_=_s),
                inc="ld", incby=16)
    ph.wait("sync", ("ld", ph.c("ld")))
    ph.barrier()
    ph.run()

    with (
        nc.sbuf_tensor("p0_x", [128, 2, D], F32) as xb,
        nc.sbuf_tensor("p0_xn", [128, D], F32) as xnb,
        nc.sbuf_tensor("p0_yt", [128, 2, DC, 128], F32) as ytb,
        nc.sbuf_tensor("p0_sq", [128, D], F32) as sqd,
        nc.sbuf_tensor("p0_tp", [128, 8], F32) as tp,
        nc.sbuf_tensor("p0_sd", [128, 1], F32) as sd,
        nc.sbuf_tensor("p0_rr", [128, 1], F32) as rr,
    ):
        ph = Ph(nc, sems, cnt)
        ve_done = {}
        se_done = {}
        sg_done = {}
        st_done = {}
        pe_evac = {}
        prev_last_evac = 0
        for t in range(32):
            sl = t % 2
            sld = ("da", "db")[sl]
            sle = ("ra", "rb")[sl]
            slf = ("fa", "fb")[sl]
            w = []
            if t >= 2:
                w.append(("ve", ve_done[t - 2]))
                w.append(("se", se_done[t - 2]))
                w.append(sg_done[t - 2])
            ph.emit("sync", lambda e, _s=sl, _t=t: e.dma_start(
                out=xb[:, _s, :], in_=x_in[_t * 128:(_t + 1) * 128, :]),
                wait=w, inc=sld, incby=16)
            ld_now = (sld, ph.c(sld))
            ph.emit("gpsimd", lambda e, _s=sl, _t=t: e.dma_start(
                out=X1d[_t * 128:(_t + 1) * 128, :], in_=xb[:, _s, :]),
                wait=[ld_now], inc=sle, incby=16)
            sg_done[t] = (sle, ph.c(sle))
            ph.wait("vector", ld_now)
            _ln_tm(ph, xb[:, sl, :], sqd, tp, sd, rr, 1.0e-5,
                   wait_in=[ld_now])
            se_done[t] = ph.c("se")
            wx = [("pe", ph.c("pe"))] if t >= 1 else []
            ph.emit("vector", lambda e, _s=sl: e.tensor_scalar(
                xnb[:], xb[:, _s, :], tp[:, 4:5], rr[:], ALU.subtract, ALU.mult),
                wait=wx, inc="ve")
            ve_done[t] = ph.c("ve")
            ve_now = ph.c("ve")
            for c in range(DC):
                bank = (pA, pB)[c % 2]
                w = [("ve", ve_now)]
                if c >= 2:
                    w.append(("ve", pe_evac[c - 2]))
                elif prev_last_evac:
                    w.append(("ve", prev_last_evac))
                ph.emit("tensor", lambda e, _b=bank, _c=c: e.transpose(
                    _b[:, 0:128], xnb[:, _c * 128:(_c + 1) * 128], idf[:]),
                    wait=w, inc="pe")
                pe_now = ph.c("pe")
                wv = [("pe", pe_now)]
                if t >= 2:
                    wv.append(st_done[t - 2])
                ph.emit("vector", lambda e, _b=bank, _c=c, _s=sl: e.tensor_copy(
                    ytb[:, _s, _c, :], _b[:, 0:128]), wait=wv, inc="ve")
                pe_evac[c] = ph.c("ve")
            prev_last_evac = ph.c("ve")
            ph.emit("sync", lambda e, _s=sl, _t=t: e.dma_start(
                out=Yd[:, :, _t * 128:(_t + 1) * 128], in_=ytb[:, _s]),
                wait=[("ve", ph.c("ve"))], inc=slf, incby=16)
            st_done[t] = (slf, ph.c(slf))
        for nmx in ("fa", "fb"):
            ph.wait("sync", (nmx, ph.c(nmx)))
        for nmx in ("ra", "rb"):
            ph.wait("gpsimd", (nmx, ph.c(nmx)))
        ph.barrier()
        ph.run("P0")

    with (
        nc.sbuf_tensor("ga_y", [128, 2, 8, 512], F32) as ybf,
        nc.sbuf_tensor("ga_gw", [128, 8, 8], F32) as gwb,
        nc.sbuf_tensor("ga_sc", [128, 8], F32) as gsc,
        nc.sbuf_tensor("ga_esc", [128, 8], F32) as esc,
        nc.sbuf_tensor("ga_sm", [128, 8], F32) as smb,
        nc.sbuf_tensor("ga_gs", [128, 4], F32) as gst4,
        nc.sbuf_tensor("ga_m1g", [128, 4], F32) as m1g4,
        nc.sbuf_tensor("ga_gs2", [128, 4], F32) as gs24,
        nc.sbuf_tensor("ga_gmk", [128, 4], F32) as gmk4,
        nc.sbuf_tensor("ga_msc", [128, 8], F32) as msc,
        nc.sbuf_tensor("ga_mk1", [128, 8], F32) as mk1,
        nc.sbuf_tensor("ga_ms2", [128, 8], F32) as ms2,
        nc.sbuf_tensor("ga_mk2", [128, 8], F32) as mk2,
        nc.sbuf_tensor("ga_c1", [128, 8], F32) as c1b,
        nc.sbuf_tensor("ga_t2", [128, 16], F32) as t2,
    ):
        ph = Ph(nc, sems, cnt)
        ph.emit("sync", lambda e: e.dma_start(out=gwb[:], in_=moe_gw),
                inc="ld", incby=16)
        ld_gw = ph.c("ld")
        pe_tile = {}
        ve_grp = {}
        for tl in range(8):
            sl = tl % 2
            sld = ("da", "db")[sl]
            w = [("pe", pe_tile[tl - 2])] if tl >= 2 else []
            ph.emit("sync", lambda e, _s=sl, _t=tl: e.dma_start(
                out=ybf[:, _s], in_=Yd[:, :, _t * 512:(_t + 1) * 512]),
                wait=w, inc=sld, incby=16)
            lc_now = (sld, ph.c(sld))
            for gi in range(4):
                g = tl * 4 + gi
                bank = (pA, pB)[g % 2]
                for kc in range(DC):
                    wm = [lc_now, ("ld", ld_gw)] if kc == 0 else []
                    if kc == 0 and g >= 2:
                        wm.append(("ve", ve_grp[g - 2]))
                    ph.emit("tensor", lambda e, _b=bank, _k=kc, _s=sl, _gi=gi: e.matmul(
                        _b[:, 0:8], ybf[:, _s, _k, _gi * 128:(_gi + 1) * 128],
                        gwb[:, _k, :], start=(_k == 0), stop=(_k == DC - 1)),
                        wait=wm, inc="pe" if kc == DC - 1 else None)
                pe_now = ph.c("pe")
                E = ph.emit
                sep = lambda: E("vector", lambda e: e.drain())
                E("vector", lambda e, _b=bank: e.tensor_copy(gsc[:], _b[:, 0:8]),
                  wait=[("pe", pe_now)])
                sep()
                E("vector", lambda e: e.reduce_max(t2[:, 0:1], gsc[:], axis=AX.X))
                sep()
                E("vector", lambda e: e.tensor_scalar_mul(t2[:, 1:2], t2[:, 0:1], -1.0),
                  inc="ve")
                ve_grp[g] = ph.c("ve")
                E("scalar", lambda e: e.activation(esc[:], gsc[:], AF.Exp,
                                                   bias=t2[:, 1:2],
                                                   accum_out=t2[:, 2:3]),
                  wait=[("ve", ph.c("ve"))], inc="se")
                E("vector", lambda e: e.reciprocal(t2[:, 3:4], t2[:, 2:3]),
                  wait=[("se", ph.c("se"))])
                sep()
                E("vector", lambda e: e.tensor_scalar_mul(smb[:], esc[:], t2[:, 3:4]))
                sep()
                E("vector", lambda e: e.tensor_max(gst4[:], smb[:, 0:4], smb[:, 4:8]))
                sep()
                E("vector", lambda e: e.reduce_max(t2[:, 4:5], gst4[:], axis=AX.X))
                sep()
                E("vector", lambda e: e.tensor_scalar(
                    m1g4[:], gst4[:], t2[:, 4:5], None, ALU.is_equal))
                sep()
                E("vector", lambda e: e.scalar_tensor_tensor(
                    gs24[:], m1g4[:], NEG, gst4[:], ALU.mult, ALU.add))
                sep()
                E("vector", lambda e: e.reduce_max(t2[:, 5:6], gs24[:], axis=AX.X))
                sep()
                E("vector", lambda e: e.tensor_scalar(
                    gmk4[:], gst4[:], t2[:, 5:6], None, ALU.is_ge))
                sep()
                E("vector", lambda e: e.tensor_mul(msc[:, 0:4], smb[:, 0:4], gmk4[:]))
                E("vector", lambda e: e.tensor_mul(msc[:, 4:8], smb[:, 4:8], gmk4[:]))
                sep()
                E("vector", lambda e: e.reduce_max(t2[:, 6:7], msc[:], axis=AX.X))
                sep()
                E("vector", lambda e: e.tensor_scalar(
                    mk1[:], msc[:], t2[:, 6:7], None, ALU.is_equal))
                sep()
                E("vector", lambda e: e.scalar_tensor_tensor(
                    ms2[:], mk1[:], NEG, msc[:], ALU.mult, ALU.add))
                sep()
                E("vector", lambda e: e.reduce_max(t2[:, 7:8], ms2[:], axis=AX.X))
                sep()
                E("vector", lambda e: e.tensor_add(t2[:, 8:9], t2[:, 6:7], t2[:, 7:8]))
                sep()
                E("vector", lambda e: e.reciprocal(t2[:, 9:10], t2[:, 8:9]))
                sep()
                E("vector", lambda e: e.tensor_mul(t2[:, 10:11], t2[:, 6:7], t2[:, 9:10]))
                E("vector", lambda e: e.tensor_mul(t2[:, 11:12], t2[:, 7:8], t2[:, 9:10]))
                E("vector", lambda e: e.tensor_scalar(
                    mk2[:], msc[:], t2[:, 7:8], None, ALU.is_equal))
                sep()
                E("vector", lambda e: e.tensor_scalar_mul(c1b[:], mk1[:], t2[:, 10:11]))
                sep()
                E("vector", lambda e, _g=g: e.scalar_tensor_tensor(
                    combt[:, _g, :], mk2[:], t2[:, 11:12], c1b[:],
                    ALU.mult, ALU.add), inc="ve")
            pe_tile[tl] = ph.c("pe")
        ph.barrier()
        ph.run("P1a")

    TMOE = 256
    NTM = S // TMOE
    with (
        nc.sbuf_tensor("m_w13", [128, 2, 8, 32, 128], BF16) as w13b,
        nc.sbuf_tensor("m_w2", [128, 16, 1024], BF16) as w2b,
        nc.sbuf_tensor("m_xn", [128, 2, 8, TMOE], BF16) as xnb,
        nc.sbuf_tensor("m_act", [128, 16, TMOE], BF16) as actb,
        nc.sbuf_tensor("m_st", [128, 2, TMOE], F32) as stb,
        nc.sbuf_tensor("m_out", [128, 4, 1024], F32) as outsb,
    ):
        ph = Ph(nc, sems, cnt)
        pe_w13_done = {}
        pe_w2_done = {}
        pe_xn_done = {}
        lc_w13 = {}
        lc_w2 = {}
        ve_act_done = {}
        se_g = {}
        ve_u = {}
        sg_out = {}
        oslot_i = 0
        ph.emit("gpsimd", lambda e: e.dma_start(out=w13b[:, 0], in_=moe_w13[0]),
                inc="w0", incby=16)
        lc_w13[0] = ("w0", ph.c("w0"))
        for e_ in range(NE):
            s13 = e_ % 2
            if e_ + 1 < NE:
                w = [("pe", pe_w13_done[e_ - 1])] if e_ >= 1 else []
                snm = ("w0", "w1")[(e_ + 1) % 2]
                ph.emit("gpsimd", lambda e, _e=e_: e.dma_start(
                    out=w13b[:, (_e + 1) % 2], in_=moe_w13[_e + 1]),
                    wait=w, inc=snm, incby=16)
                lc_w13[e_ + 1] = (snm, ph.c(snm))
            w = [("pe", pe_w2_done[e_ - 1])] if e_ >= 1 else []
            ph.emit("gpsimd", lambda e, _e=e_: e.dma_start(
                out=w2b[:], in_=moe_w2[_e]), wait=w, inc="w2", incby=16)
            lc_w2[e_] = ("w2", ph.c("w2"))
            for t in range(NTM):
                sl = t % 2
                w = []
                pt = (e_, t - 2) if t >= 2 else (e_ - 1, NTM - 2 + t)
                if pt in pe_xn_done:
                    w.append(("pe", pe_xn_done[pt]))
                snm = ("qa", "qb")[sl]
                ph.emit("gpsimd", lambda e, _s=sl, _t=t: e.dma_start(
                    out=xnb[:, _s], in_=Yd[:, :, _t * TMOE:(_t + 1) * TMOE]),
                    wait=w, inc=snm, incby=16)
                lc_xn = (snm, ph.c(snm))
                for c in range(16):
                    bankG = (pA, pC)[c % 2]
                    bankU = (pB, pD)[c % 2]
                    for kc in range(DC):
                        wm = []
                        if kc == 0:
                            wm = [lc_xn, lc_w13[e_]]
                            key = (e_, t, c - 2)
                            if key in se_g:
                                wm.append(("se", se_g[key]))
                            elif c < 2 and (e_, t) != (0, 0):
                                wm.append(("se", prev_last_se))
                        ph.emit("tensor", lambda e, _b=bankG, _k=kc, _c=c, _s=sl, _sl3=s13: e.matmul(
                            _b[:, 0:TMOE], w13b[:, _sl3, _k, _c, :],
                            xnb[:, _s, _k, :], start=(_k == 0), stop=(_k == DC - 1)),
                            wait=wm, inc="pe" if kc == DC - 1 else None)
                    pe_gate = ph.c("pe")
                    for kc in range(DC):
                        wm = []
                        if kc == 0:
                            key = (e_, t, c - 2)
                            if key in ve_u:
                                wm.append(("ve", ve_u[key]))
                            elif c < 2 and (e_, t) != (0, 0):
                                wm.append(("ve", prev_last_veu))
                        ph.emit("tensor", lambda e, _b=bankU, _k=kc, _c=c, _s=sl, _sl3=s13: e.matmul(
                            _b[:, 0:TMOE], w13b[:, _sl3, _k, _c + 16, :],
                            xnb[:, _s, _k, :], start=(_k == 0), stop=(_k == DC - 1)),
                            wait=wm, inc="pe" if kc == DC - 1 else None)
                    pe_up = ph.c("pe")
                    ssl = c % 2
                    ws = [("pe", pe_gate)]
                    key = (e_, t, c - 2)
                    if key in ve_u:
                        ws.append(("ve", ve_u[key]))
                    elif c < 2 and (e_, t) != (0, 0):
                        ws.append(("ve", prev_last_veu))
                    ph.emit("scalar", lambda e, _b=bankG, _s=ssl: e.activation(
                        stb[:, _s, :], _b[:, 0:TMOE], AF.Silu),
                        wait=ws, inc="se")
                    se_g[(e_, t, c)] = ph.c("se")
                    ph.emit("vector", lambda e, _b=bankU, _c=c, _s=ssl: e.scalar_tensor_tensor(
                        actb[:, _c, :], _b[:, 0:TMOE], 1.0, stb[:, _s, :],
                        ALU.bypass, ALU.mult),
                        wait=[("se", ph.c("se")), ("pe", pe_up)], inc="ve")
                    ve_u[(e_, t, c)] = ph.c("ve")
                prev_last_se = se_g[(e_, t, 15)]
                prev_last_veu = ve_u[(e_, t, 15)]
                ve_act = ph.c("ve")
                pe_xn_done[(e_, t)] = 0
                for grp in range(TMOE // 128):
                    gabs = t * (TMOE // 128) + grp
                    for half in range(2):
                        for kc in range(16):
                            wm = [("ve", ve_u[(e_, t, kc)])]
                            if kc == 0:
                                wm.append(lc_w2[e_])
                                if oslot_i >= 1:
                                    wm.append(("ve", ve_o_evac[oslot_i - 1]))
                            ph.emit("tensor", lambda e, _k=kc, _h=half, _g=grp: e.matmul(
                                pO[:, _h * 512:(_h + 1) * 512],
                                actb[:, _k, _g * 128:(_g + 1) * 128],
                                w2b[:, _k, _h * 512:(_h + 1) * 512],
                                start=(_k == 0), stop=(_k == 15)),
                                wait=wm, inc="pe" if kc == 15 else None)
                    pe_o = ph.c("pe")
                    osl = oslot_i % 4
                    w = [("pe", pe_o)]
                    if oslot_i >= 4:
                        w.append(sg_out[oslot_i - 4])
                    ph.emit("vector", lambda e, _o=osl, _g=gabs, _p=EPOS[e_]: e.tensor_scalar_mul(
                        outsb[:, _o, :], pO[:], combt[:, _g, _p:_p + 1]),
                        wait=w, inc="ve")
                    if oslot_i == 0:
                        ve_o_evac = {}
                    ve_o_evac[oslot_i] = ph.c("ve")
                    snm = ("ra", "rb", "rc", "rd")[osl]
                    ph.emit("gpsimd", lambda e, _o=osl, _g=gabs, _e=e_: e.dma_start(
                        out=X1d[_g * 128:(_g + 1) * 128, :], in_=outsb[:, _o, :],
                        accum_op=ALU.add),
                        wait=[("ve", ph.c("ve"))], inc=snm, incby=16)
                    sg_out[oslot_i] = (snm, ph.c(snm))
                    oslot_i += 1
                pe_xn_done[(e_, t)] = ph.c("pe")
            pe_w13_done[e_] = ph.c("pe")
            pe_w2_done[e_] = ph.c("pe")
        for nmx in ("ra", "rb", "rc", "rd"):
            ph.wait("gpsimd", (nmx, ph.c(nmx)))
        ph.barrier()
        ph.run("P1")

    T2 = 256
    NT2 = S // T2
    bankcred = {}

    def bwait(*banks):
        return [bankcred[b] for b in banks if b in bankcred]

    with contextlib.ExitStack() as p2stk:
        WS = p2stk.enter_context(nc.sbuf_tensor("p2_ws", [128, 2, 32768], BF16))
        ycb = p2stk.enter_context(nc.sbuf_tensor("p2_yc", [128, 2, 8, T2], BF16))
        attb = p2stk.enter_context(nc.sbuf_tensor("p2_att", [128, 8, T2], BF16))
        tmin = p2stk.enter_context(nc.sbuf_tensor("p2_tm", [128, T2], F32))
        emin = p2stk.enter_context(nc.sbuf_tensor("p2_em", [128, T2], F32))
        kfb = p2stk.enter_context(nc.sbuf_tensor("p2_kf", [128, T2], F32))
        vbb = p2stk.enter_context(nc.sbuf_tensor("p2_vb", [128, T2], F32))
        prb = p2stk.enter_context(nc.sbuf_tensor("p2_pr", [128, T2], F32))
        tgb = p2stk.enter_context(nc.sbuf_tensor("p2_tg", [128, 2, T2], F32))
        zbf = p2stk.enter_context(nc.sbuf_tensor("p2_zb", [128, 8, T2], F32))
        zb16 = p2stk.enter_context(nc.sbuf_tensor("p2_z16", [128, 2, T2], BF16))
        z2b = p2stk.enter_context(nc.sbuf_tensor("p2_z2b", [128, 2, T2], BF16))
        ymb16 = p2stk.enter_context(nc.sbuf_tensor("p2_ym16", [128, 2, 8, T2], BF16))
        hshr = p2stk.enter_context(nc.sbuf_tensor("p2_h", [128, 2, 32, T2], BF16))
        mrb = p2stk.enter_context(nc.sbuf_tensor("p2_mr", [128, T2], F32))
        msb = p2stk.enter_context(nc.sbuf_tensor("p2_ms", [1, T2], F32))
        qsb = p2stk.enter_context(nc.sbuf_tensor("p2_qs", [1, T2], F32))
        kvacc = p2stk.enter_context(nc.sbuf_tensor("p2_kv", [128, 8], F32))
        skacc = p2stk.enter_context(nc.sbuf_tensor("p2_sk", [128, 8], F32))
        kvp = p2stk.enter_context(nc.sbuf_tensor("p2_kvp", [128, 8, NT2], F32))
        skp = p2stk.enter_context(nc.sbuf_tensor("p2_skp", [128, 8, NT2], F32))
        cbs = p2stk.enter_context(nc.sbuf_tensor("p2_cb", [128, 2, 104], F32))
        cbds = p2stk.enter_context(nc.sbuf_tensor("p2_cbd", [128, 2, 16], F32))
        pSM = p2stk.enter_context(nc.psum_tensor("pSM", [128, 512], F32))
        pSQ = p2stk.enter_context(nc.psum_tensor("pSQ", [128, 512], F32))
        ph = Ph(nc, sems, cnt)

        def wv_qkv(sl, half):
            return WS[:, sl, half * 8192:(half + 1) * 8192].rearrange(
                "p (a b c) -> p a b c", a=8, b=8, c=128)

        def wv_w1(sl):
            return WS[:, sl, :].rearrange("p (a b c) -> p a b c", a=8, b=32, c=128)

        def wv_w2(sl):
            return WS[:, sl, :].rearrange("p (a b c) -> p a b c", a=32, b=8, c=128)

        pe_pass_end = {}
        stY = {}
        stYM = {}
        stH = {}

        def load_pass_weights(gp):
            i, p = gp // 4, gp % 4
            sl = gp % 2
            snm = ("w0", "w1")[sl]
            w = [("pe", pe_pass_end[gp - 2])] if gp - 2 in pe_pass_end else []
            if p == 0:
                ph.emit("gpsimd", lambda e, _i=i, _s=sl: e.dma_start(
                    out=wv_qkv(_s, 0), in_=l0_wk[_i]), wait=w, inc=snm, incby=16)
                ph.emit("gpsimd", lambda e, _i=i, _s=sl: e.dma_start(
                    out=wv_qkv(_s, 1), in_=l0_wv[_i]), inc=snm, incby=16)
            elif p == 1:
                ph.emit("gpsimd", lambda e, _i=i, _s=sl: e.dma_start(
                    out=wv_qkv(_s, 0), in_=l0_wq[_i]), wait=w, inc=snm, incby=16)
                ph.emit("gpsimd", lambda e, _i=i, _s=sl: e.dma_start(
                    out=wv_qkv(_s, 1), in_=l0_wo[_i]), inc=snm, incby=16)
            elif p == 2:
                ph.emit("gpsimd", lambda e, _i=i, _s=sl: e.dma_start(
                    out=wv_w1(_s), in_=l0_w1[_i]), wait=w, inc=snm, incby=16)
            else:
                ph.emit("gpsimd", lambda e, _i=i, _s=sl: e.dma_start(
                    out=wv_w2(_s), in_=l0_w2[_i]), wait=w, inc=snm, incby=16)
            return (snm, ph.c(snm))

        def emit_stat(j, vz, sq, statp):
            s2 = j % 2
            w = [("ve", vz[j])]
            if j == 0:
                w += bwait("pSM")
            ph.emit("tensor", lambda e, _s2=s2, _j=j: e.matmul(
                pSM[0:1, 0:T2], onst[:], zb16[:, _s2, :],
                start=(_j == 0), stop=(_j == DC - 1)), wait=w, inc="pe")
            w = [("se", sq[j])]
            if j == 0:
                w += bwait("pSQ")
            ph.emit("tensor", lambda e, _s2=s2, _j=j: e.matmul(
                pSQ[0:1, 0:T2], onst[:], z2b[:, _s2, :],
                start=(_j == 0), stop=(_j == DC - 1)), wait=w, inc="pe")
            statp[s2] = ("pe", ph.c("pe"))

        def ln_fm_tail(i, t, csl, scol, bcol, store_dram, stsems, st_prev):
            pe_stats = ph.c("pe")
            ph.emit("vector", lambda e: e.tensor_copy(msb[0:1, 0:T2], pSM[0:1, 0:T2]),
                    wait=[("pe", pe_stats)])
            ph.emit("vector", lambda e: e.tensor_copy(qsb[0:1, 0:T2], pSQ[0:1, 0:T2]))
            ph.emit("vector", lambda e: e.drain(), inc="ve")
            bankcred["pSM"] = ("ve", ph.c("ve"))
            bankcred["pSQ"] = ("ve", ph.c("ve"))
            w = [("ve", ph.c("ve"))] + bwait("pO")
            ph.emit("tensor", lambda e: e.matmul(
                pO[:, 0:T2], onsr[:], msb[0:1, 0:T2], start=True, stop=True),
                wait=w, inc="pe")
            ph.emit("tensor", lambda e: e.matmul(
                pO[:, 512:512 + T2], onsr[:], qsb[0:1, 0:T2], start=True, stop=True),
                inc="pe")
            pe_bc = ph.c("pe")
            ph.emit("vector", lambda e: e.tensor_copy(mrb[:], pO[:, 0:T2]),
                    wait=[("pe", pe_bc)])
            ph.emit("vector", lambda e: e.drain())
            ph.emit("vector", lambda e: e.tensor_mul(tmin[:], mrb[:], mrb[:]))
            ph.emit("vector", lambda e: e.drain())
            ph.emit("vector", lambda e: e.tensor_sub(
                emin[:], pO[:, 512:512 + T2], tmin[:]), inc="ve")
            bankcred["pO"] = ("ve", ph.c("ve"))
            ph.emit("scalar", lambda e: e.activation(
                kfb[:], emin[:], AF.Sqrt, bias=1.0e-6),
                wait=[("ve", ph.c("ve"))], inc="se")
            ph.emit("vector", lambda e: e.reciprocal(vbb[:], kfb[:]),
                    wait=[("se", ph.c("se"))])
            ph.emit("vector", lambda e: e.drain())
            ssl = t % 2
            snm = stsems[ssl]
            for mp in range(DC // 2):
                m0, m1 = 2 * mp, 2 * mp + 1
                wn = [st_prev] if (mp == 0 and st_prev is not None) else []
                ph.emit("vector", lambda e, _m=m0: e.tensor_sub(
                    prb[:], zbf[:, _m, :], mrb[:]), wait=wn)
                ph.emit("vector", lambda e, _m=m1: e.tensor_sub(
                    tmin[:], zbf[:, _m, :], mrb[:]))
                ph.emit("vector", lambda e: e.drain())
                ph.emit("vector", lambda e: e.tensor_mul(emin[:], prb[:], vbb[:]))
                ph.emit("vector", lambda e: e.tensor_mul(kfb[:], tmin[:], vbb[:]))
                ph.emit("vector", lambda e: e.drain())
                for _m, src in ((m0, emin), (m1, kfb)):
                    ph.emit("vector", lambda e, _mm=_m, _sr=src, _cs=csl, _ss=ssl: e.tensor_scalar(
                        ymb16[:, _ss, _mm, :], _sr[:],
                        cbs[:, _cs, scol + _mm:scol + _mm + 1],
                        cbs[:, _cs, bcol + _mm:bcol + _mm + 1], ALU.mult, ALU.add),
                        inc="ve" if _m == DC - 1 else None)
            ve_norm = ph.c("ve")
            ph.emit("sync", lambda e, _t=t, _d=store_dram, _ss=ssl: e.dma_start(
                out=_d[:, :, _t * T2:(_t + 1) * T2], in_=ymb16[:, _ss]),
                wait=[("ve", ve_norm)], inc=snm, incby=16)
            return (snm, ph.c(snm))

        lw = {0: load_pass_weights(0), 1: load_pass_weights(1)}
        cbs_free = {}
        kvp_free = None
        for i in range(nsl):
            csl = i % 2
            snm_cb = ("ca", "cb")[csl]
            wcb = [("ve", cbs_free[i - 2])] if i >= 2 else []
            ph.emit("sync", lambda e, _i=i, _c=csl: e.dma_start(
                out=cbs[:, _c, :], in_=l0_cb[:, _i, :]),
                wait=wcb, inc=snm_cb, incby=16)
            ld_cb = (snm_cb, ph.c(snm_cb))
            ph.wait("vector", ld_cb)
            ph.emit("vector", lambda e, _c=csl: e.tensor_scalar_add(
                cbds[:, _c, 0:8], cbs[:, _c, 0:8], 1.0))
            ph.emit("vector", lambda e, _c=csl: e.tensor_scalar_add(
                cbds[:, _c, 8:16], cbs[:, _c, 8:16], 1.0))
            ph.emit("vector", lambda e: e.drain(), inc="ve")

            gp = 4 * i
            slA = gp % 2
            if gp + 1 not in lw:
                lw[gp + 1] = load_pass_weights(gp + 1)
            yc_cons = {}
            for t in range(NT2):
                sl = t % 2
                snm = ("da", "db")[sl]
                w = []
                if t >= 2:
                    w = [("ve", yc_cons[t - 2])]
                elif 4 * i - 1 in pe_pass_end:
                    w = [("ve", prev_ycb_free)]
                if (i - 1, t) in stY:
                    w.append(stY[(i - 1, t)])
                if i == 0:
                    snm = ("qa", "qb")[sl]
                    ph.emit("gpsimd", lambda e, _s=sl, _t=t: e.dma_start(
                        out=ycb[:, _s], in_=Yd[:, :, _t * T2:(_t + 1) * T2]),
                        wait=w, inc=snm, incby=16)
                else:
                    ph.emit("sync", lambda e, _s=sl, _t=t: e.dma_start(
                        out=ycb[:, _s], in_=Ybd[:, :, _t * T2:(_t + 1) * T2]),
                        wait=w, inc=snm, incby=16)
                lc_y = (snm, ph.c(snm))
                for mc in range(DC):
                    bK = ("pA", "pB", "pC", "pD")[mc % 4]
                    pbK = (pA, pB, pC, pD)[mc % 4]
                    bV = ("pSM", "pSQ", "pOl", "pOh")[mc % 4]
                    pbV = (pSM[:, 0:T2], pSQ[:, 0:T2], pO[:, 0:T2],
                           pO[:, 512:512 + T2])[mc % 4]
                    for kc in range(DC):
                        w = []
                        if kc == 0:
                            w = [lc_y, lw[gp]] + bwait(bK)
                        ph.emit("tensor", lambda e, _b=pbK, _k=kc, _m=mc, _s=sl, _sa=slA: e.matmul(
                            _b[:, 0:T2], wv_qkv(_sa, 0)[:, _k, _m, :],
                            ycb[:, _s, _k, :], start=(_k == 0), stop=(_k == DC - 1)),
                            wait=w, inc="pe" if kc == DC - 1 else None)
                    pe_K = ph.c("pe")
                    for kc in range(DC):
                        w = []
                        if kc == 0:
                            w = bwait(bV)
                        ph.emit("tensor", lambda e, _b=pbV, _k=kc, _m=mc, _s=sl, _sa=slA: e.matmul(
                            _b, wv_qkv(_sa, 1)[:, _k, _m, :],
                            ycb[:, _s, _k, :], start=(_k == 0), stop=(_k == DC - 1)),
                            wait=w, inc="pe" if kc == DC - 1 else None)
                    pe_V = ph.c("pe")
                    ph.emit("vector", lambda e, _b=pbK, _c=csl, _m=mc: e.tensor_scalar(
                        tmin[:], _b[:, 0:T2], cbs[:, _c, 8 + _m:9 + _m], 0.0,
                        ALU.add, ALU.min), wait=[("pe", pe_K)], inc="ve")
                    ph.emit("scalar", lambda e: e.activation(emin[:], tmin[:], AF.Exp),
                            wait=[("ve", ph.c("ve"))], inc="se")
                    ph.emit("vector", lambda e, _b=pbK, _c=csl, _m=mc: e.scalar_tensor_tensor(
                        kfb[:], _b[:, 0:T2], cbds[:, _c, 8 + _m:9 + _m], emin[:],
                        ALU.add, ALU.max), wait=[("se", ph.c("se"))], inc="ve")
                    bankcred[bK] = ("ve", ph.c("ve"))
                    ph.emit("vector", lambda e, _b=pbV, _c=csl, _m=mc: e.tensor_scalar_add(
                        vbb[:], _b, cbs[:, _c, 16 + _m:17 + _m]),
                        wait=[("pe", pe_V)], inc="ve")
                    bankcred[bV] = ("ve", ph.c("ve"))
                    ph.emit("vector", lambda e: e.drain())
                    ph.emit("vector", lambda e: e.tensor_mul(prb[:], kfb[:], vbb[:]),
                            inc="ve")
                    wkv = [("ve", ph.c("ve"))]
                    if kvp_free:
                        wkv.append(kvp_free)
                        kvp_free = None
                    ph.emit("scalar", lambda e, _m=mc, _t=t: e.activation(
                        zbf[:, _m % 4, :], prb[:], AF.Copy,
                        accum_out=kvp[:, _m, _t:_t + 1]), wait=wkv, inc="se")
                    ph.emit("scalar", lambda e, _m=mc, _t=t: e.activation(
                        zbf[:, 4 + _m % 4, :], kfb[:], AF.Copy,
                        accum_out=skp[:, _m, _t:_t + 1]), inc="se")
                yc_cons[t] = ph.c("ve")
            ph.wait("vector", ("se", ph.c("se")))
            for mc in range(DC):
                ph.emit("vector", lambda e, _m=mc: e.reduce_sum(
                    kvacc[:, _m:_m + 1], kvp[:, _m, :], axis=AX.X))
                ph.emit("vector", lambda e, _m=mc: e.reduce_sum(
                    skacc[:, _m:_m + 1], skp[:, _m, :], axis=AX.X))
            pe_pass_end[gp] = ph.c("pe")
            prev_ycb_free = ph.c("ve")
            ph.emit("vector", lambda e: e.drain(), inc="ve")
            kvp_free = ("ve", ph.c("ve"))
            ph.run(f"P2_{i}_A")

            gp = 4 * i + 1
            slB = gp % 2
            if gp + 1 not in lw:
                lw[gp + 1] = load_pass_weights(gp + 1)
            yc_cons = {}
            ve_z = {}
            statp = {}
            if 4 * (i - 1) + 3 in pe_pass_end:
                statp = {0: ("pe", pe_pass_end[4 * (i - 1) + 3]),
                         1: ("pe", pe_pass_end[4 * (i - 1) + 3])}
            pe_wo_end = 0
            for t in range(NT2):
                sl = t % 2
                snm = ("da", "db")[sl]
                w = []
                if t >= 2:
                    w = [("ve", yc_cons[t - 2])]
                else:
                    w = [("ve", prev_ycb_free)]
                if (i - 1, t) in stY:
                    w.append(stY[(i - 1, t)])
                if i == 0:
                    snm = ("qa", "qb")[sl]
                    ph.emit("gpsimd", lambda e, _s=sl, _t=t: e.dma_start(
                        out=ycb[:, _s], in_=Yd[:, :, _t * T2:(_t + 1) * T2]),
                        wait=w, inc=snm, incby=16)
                else:
                    ph.emit("sync", lambda e, _s=sl, _t=t: e.dma_start(
                        out=ycb[:, _s], in_=Ybd[:, :, _t * T2:(_t + 1) * T2]),
                        wait=w, inc=snm, incby=16)
                lc_y = (snm, ph.c(snm))
                for mc in range(DC):
                    bQ = ("pA", "pB", "pC", "pD")[mc % 4]
                    pbQ = (pA, pB, pC, pD)[mc % 4]
                    for kc in range(DC):
                        w = []
                        if kc == 0:
                            w = [lc_y, lw[gp]] + bwait(bQ)
                        ph.emit("tensor", lambda e, _b=pbQ, _k=kc, _m=mc, _s=sl, _sb=slB: e.matmul(
                            _b[:, 0:T2], wv_qkv(_sb, 0)[:, _k, _m, :],
                            ycb[:, _s, _k, :], start=(_k == 0), stop=(_k == DC - 1)),
                            wait=w, inc="pe" if kc == DC - 1 else None)
                    pe_Q = ph.c("pe")
                    ph.emit("vector", lambda e, _b=pbQ, _c=csl, _m=mc: e.tensor_scalar(
                        tmin[:], _b[:, 0:T2], cbs[:, _c, 0 + _m:1 + _m], 0.0,
                        ALU.add, ALU.min), wait=[("pe", pe_Q)], inc="ve")
                    ph.emit("scalar", lambda e: e.activation(emin[:], tmin[:], AF.Exp),
                            wait=[("ve", ph.c("ve"))], inc="se")
                    ph.emit("vector", lambda e, _b=pbQ, _c=csl, _m=mc: e.scalar_tensor_tensor(
                        kfb[:], _b[:, 0:T2], cbds[:, _c, 0 + _m:1 + _m], emin[:],
                        ALU.add, ALU.max), wait=[("se", ph.c("se"))], inc="ve")
                    bankcred[bQ] = ("ve", ph.c("ve"))
                    ph.emit("scalar", lambda e, _m=mc: e.activation(
                        vbb[:], kfb[:], AF.Copy, bias=1.0e-6,
                        scale=skacc[:, _m:_m + 1]),
                        wait=[("ve", ph.c("ve"))], inc="se")
                    ph.emit("vector", lambda e: e.reciprocal(prb[:], vbb[:]),
                            wait=[("se", ph.c("se"))])
                    ph.emit("vector", lambda e: e.drain())
                    wat = [("pe", pe_wo_end)] if (mc == 0 and pe_wo_end) else []
                    ph.emit("vector", lambda e, _m=mc: e.scalar_tensor_tensor(
                        attb[:, _m, :], kfb[:], kvacc[:, _m:_m + 1], prb[:],
                        ALU.mult, ALU.mult), wait=wat, inc="ve")
                ve_att = ph.c("ve")
                vz = {}
                sq = {}
                for mc in range(DC):
                    bW = ("pA", "pB", "pC", "pD")[mc % 4]
                    pbW = (pA, pB, pC, pD)[mc % 4]
                    for kc in range(DC):
                        w = []
                        if kc == 0:
                            w = [("ve", ve_att)] + bwait(bW)
                        ph.emit("tensor", lambda e, _b=pbW, _k=kc, _m=mc, _sb=slB: e.matmul(
                            _b[:, 0:T2], wv_qkv(_sb, 1)[:, _k, _m, :],
                            attb[:, _k, :], start=(_k == 0), stop=(_k == DC - 1)),
                            wait=w, inc="pe" if kc == DC - 1 else None)
                    pe_W = ph.c("pe")
                    tsl = mc % 2
                    w = [("pe", pe_W)]
                    zkey = mc - 2 if mc >= 2 else mc + 6
                    if zkey in ve_z:
                        w.append(("ve", ve_z[zkey]))
                    ph.emit("scalar", lambda e, _b=pbW, _c=csl, _m=mc, _t=tsl: e.activation(
                        tgb[:, _t, :], _b[:, 0:T2], AF.Gelu_apprx_tanh,
                        bias=cbs[:, _c, 24 + _m:25 + _m]), wait=w, inc="se")
                    bankcred[bW] = ("se", ph.c("se"))
                    ph.emit("vector", lambda e, _m=mc, _t=tsl, _s=sl: e.tensor_add(
                        zbf[:, _m, :], tgb[:, _t, :], ycb[:, _s, _m, :]),
                        wait=[("se", ph.c("se"))], inc="ve")
                    ve_z[mc] = ph.c("ve")
                    if mc >= 2:
                        emit_stat(mc - 2, vz, sq, statp)
                    s2 = mc % 2
                    wcp = [("ve", ve_z[mc])]
                    if s2 in statp:
                        wcp.append(statp[s2])
                    ph.emit("vector", lambda e, _m=mc, _s2=s2: e.tensor_copy(
                        zb16[:, _s2, :], zbf[:, _m, :]), wait=wcp, inc="ve")
                    vz[mc] = ph.c("ve")
                    ph.emit("scalar", lambda e, _m=mc, _s2=s2: e.activation(
                        z2b[:, _s2, :], zbf[:, _m, :], AF.Square),
                        wait=[("ve", vz[mc])], inc="se")
                    sq[mc] = ph.c("se")
                pe_wo_end = pe_W
                emit_stat(DC - 2, vz, sq, statp)
                emit_stat(DC - 1, vz, sq, statp)
                yc_cons[t] = ph.c("ve")
                stp = stYM.get((i, t - 2)) or stY.get((i - 1, NT2 - 2 + t))
                stYM[(i, t)] = ln_fm_tail(i, t, csl, 32, 40, YMd,
                                          ("ea", "eb"), stp)
            pe_pass_end[gp] = ph.c("pe")
            prev_ycb_free = ph.c("ve")
            ph.run(f"P2_{i}_B1")

            gp = 4 * i + 2
            slC = gp % 2
            if gp + 1 not in lw:
                lw[gp + 1] = load_pass_weights(gp + 1)
            yc_cons = {}
            st_h = {}
            for t in range(NT2):
                sl = t % 2
                w = []
                if t >= 2:
                    w = [("pe", yc_cons[t - 2])]
                else:
                    w = [("ve", prev_ycb_free)]
                w.append(stYM[(i, t)])
                snm = ("da", "db")[sl]
                ph.emit("sync", lambda e, _s=sl, _t=t: e.dma_start(
                    out=ycb[:, _s], in_=YMd[:, :, _t * T2:(_t + 1) * T2]),
                    wait=w, inc=snm, incby=16)
                lc_y = (snm, ph.c(snm))
                for mc in range(32):
                    bW = ("pA", "pB", "pC", "pD")[mc % 4]
                    pbW = (pA, pB, pC, pD)[mc % 4]
                    for kc in range(DC):
                        w = []
                        if kc == 0:
                            w = [lc_y, lw[gp]] + bwait(bW)
                        ph.emit("tensor", lambda e, _b=pbW, _k=kc, _m=mc, _s=sl, _sc=slC: e.matmul(
                            _b[:, 0:T2], wv_w1(_sc)[:, _k, _m, :],
                            ycb[:, _s, _k, :], start=(_k == 0), stop=(_k == DC - 1)),
                            wait=w, inc="pe" if kc == DC - 1 else None)
                    pe_W = ph.c("pe")
                    w = [("pe", pe_W)]
                    if t >= 2 and mc == 0:
                        w.append(st_h[t - 2])
                    ph.emit("scalar", lambda e, _b=pbW, _c=csl, _m=mc, _s=sl: e.activation(
                        hshr[:, _s, _m, :], _b[:, 0:T2], AF.Gelu_apprx_tanh,
                        bias=cbs[:, _c, 64 + _m:65 + _m]), wait=w, inc="se")
                    bankcred[bW] = ("se", ph.c("se"))
                yc_cons[t] = ph.c("pe")
                snmh = ("ha", "hb")[sl]
                ph.emit("sync", lambda e, _s=sl, _t=t: e.dma_start(
                    out=Hd[:, :, _t * T2:(_t + 1) * T2], in_=hshr[:, _s]),
                    wait=[("se", ph.c("se"))], inc=snmh, incby=16)
                st_h[t] = (snmh, ph.c(snmh))
                stH[(i, t)] = (snmh, ph.c(snmh))
            pe_pass_end[gp] = ph.c("pe")
            prev_ycb_free = ph.c("pe")
            ph.run(f"P2_{i}_B2a")

            gp = 4 * i + 3
            slD = gp % 2
            if gp + 1 not in lw and gp + 1 < 4 * NSL:
                lw[gp + 1] = load_pass_weights(gp + 1)
            hb_cons = {}
            ym_cons = {}
            ve_z = {}
            statp = {0: ("pe", pe_pass_end[gp - 2]),
                     1: ("pe", pe_pass_end[gp - 2])}
            for t in range(NT2):
                sl = t % 2
                w = [stH[(i, t)]]
                if t >= 2:
                    w.append(("pe", hb_cons[t - 2]))
                else:
                    w.append(stH[(i, NT2 - 2 + t)])
                snmg = ("ga", "gb")[sl]
                ph.emit("sync", lambda e, _s=sl, _t=t: e.dma_start(
                    out=hshr[:, _s], in_=Hd[:, :, _t * T2:(_t + 1) * T2]),
                    wait=w, inc=snmg, incby=16)
                ld_h = (snmg, ph.c(snmg))
                w = [stYM[(i, t)]]
                if t >= 2:
                    w.append(("ve", ym_cons[t - 2]))
                else:
                    w.append(("pe", pe_pass_end[gp - 1]))
                snm = ("da", "db")[sl]
                ph.emit("sync", lambda e, _s=sl, _t=t: e.dma_start(
                    out=ycb[:, _s], in_=YMd[:, :, _t * T2:(_t + 1) * T2]),
                    wait=w, inc=snm, incby=16)
                lc_y = (snm, ph.c(snm))
                vz = {}
                sq = {}
                for mc in range(DC):
                    bW = ("pA", "pB", "pC", "pD")[mc % 4]
                    pbW = (pA, pB, pC, pD)[mc % 4]
                    for kc in range(32):
                        w = []
                        if kc == 0:
                            w = [ld_h, lc_y, lw[gp]] + bwait(bW)
                        ph.emit("tensor", lambda e, _b=pbW, _k=kc, _m=mc, _s=sl, _sd=slD: e.matmul(
                            _b[:, 0:T2], wv_w2(_sd)[:, _k, _m, :],
                            hshr[:, _s, _k, :], start=(_k == 0), stop=(_k == 31)),
                            wait=w, inc="pe" if kc == 31 else None)
                    pe_W = ph.c("pe")
                    tsl = mc % 2
                    w = [("pe", pe_W)]
                    zkey = mc - 2 if mc >= 2 else mc + 6
                    if zkey in ve_z:
                        w.append(("ve", ve_z[zkey]))
                    ph.emit("scalar", lambda e, _b=pbW, _c=csl, _m=mc, _t=tsl: e.activation(
                        tgb[:, _t, :], _b[:, 0:T2], AF.Gelu_apprx_tanh,
                        bias=cbs[:, _c, 96 + _m:97 + _m]), wait=w, inc="se")
                    bankcred[bW] = ("se", ph.c("se"))
                    ph.emit("vector", lambda e, _m=mc, _t=tsl, _s=sl: e.tensor_add(
                        zbf[:, _m, :], tgb[:, _t, :], ycb[:, _s, _m, :]),
                        wait=[("se", ph.c("se"))], inc="ve")
                    ve_z[mc] = ph.c("ve")
                    if mc >= 2:
                        emit_stat(mc - 2, vz, sq, statp)
                    s2 = mc % 2
                    wcp = [("ve", ve_z[mc])]
                    if s2 in statp:
                        wcp.append(statp[s2])
                    ph.emit("vector", lambda e, _m=mc, _s2=s2: e.tensor_copy(
                        zb16[:, _s2, :], zbf[:, _m, :]), wait=wcp, inc="ve")
                    vz[mc] = ph.c("ve")
                    ph.emit("scalar", lambda e, _m=mc, _s2=s2: e.activation(
                        z2b[:, _s2, :], zbf[:, _m, :], AF.Square),
                        wait=[("ve", vz[mc])], inc="se")
                    sq[mc] = ph.c("se")
                emit_stat(DC - 2, vz, sq, statp)
                emit_stat(DC - 1, vz, sq, statp)
                hb_cons[t] = ph.c("pe")
                ym_cons[t] = ph.c("ve")
                stp = stY.get((i, t - 2)) or stYM.get((i, NT2 - 2 + t))
                stY[(i, t)] = ln_fm_tail(i, t, csl, 48, 56, Ybd,
                                         ("fa", "fb"), stp)
            pe_pass_end[gp] = ph.c("pe")
            prev_ycb_free = ph.c("ve")
            cbs_free[i] = ph.c("ve")
            ph.run(f"P2_{i}_B2b")
        for nmx in ("ea", "eb", "fa", "fb", "ha", "hb"):
            ph.wait("sync", (nmx, ph.c(nmx)))
        ph.barrier()
        ph.run("P2end")

    with contextlib.ExitStack() as p3s:
        x1b = p3s.enter_context(nc.sbuf_tensor("p3_x1", [128, 2, D], F32))
        x1n = p3s.enter_context(nc.sbuf_tensor("p3_x1n", [128, D], F32))
        ytb = p3s.enter_context(nc.sbuf_tensor("p3_yt", [128, 2, 8, 128], BF16))
        pTY = p3s.enter_context(nc.psum_tensor("pTY", [128, 1024], BF16))
        sqd3 = p3s.enter_context(nc.sbuf_tensor("p3_sq", [128, D], F32))
        tp3 = p3s.enter_context(nc.sbuf_tensor("p3_tp", [128, 8], F32))
        sd3 = p3s.enter_context(nc.sbuf_tensor("p3_sd", [128, 1], F32))
        rr3 = p3s.enter_context(nc.sbuf_tensor("p3_rr", [128, 1], F32))
        xnt = p3s.enter_context(nc.sbuf_tensor("p3_xnt", [128, D], BF16))
        xnf = p3s.enter_context(nc.sbuf_tensor("p3_xnf", [128, 2, 8, 128], BF16))
        pT3 = p3s.enter_context(nc.psum_tensor("pT3", [128, 1024], BF16))
        ph = Ph(nc, sems, cnt)
        ld_c = {}
        ve_c = {}
        st_c = {}
        pe_xn = {}
        stx1_p3 = {}
        for t in range(32):
            sl = t % 2
            snd = ("da", "db")[sl]
            sng = ("ga", "gb")[sl]
            w = []
            if t >= 2:
                w.append(("ve", ve_c[t - 2]))
            ph.emit("sync", lambda e, _s=sl, _t=t: e.dma_start(
                out=x1b[:, _s], in_=X1d[_t * 128:(_t + 1) * 128, :]),
                wait=w, inc=snd, incby=16)
            ld_x1p = (snd, ph.c(snd))
            wy = [("pe", pe_xn[t - 2])] if t >= 2 else []
            ph.emit("sync", lambda e, _s=sl, _t=t: e.dma_start(
                out=ytb[:, _s], in_=Ybd[:, :, _t * 128:(_t + 1) * 128]),
                wait=wy, inc=sng, incby=16)
            ld_c[t] = (sng, ph.c(sng))
            for c in range(DC):
                w = []
                if c == 0:
                    w = [ld_x1p, ld_c[t]] + bwait("pTY")
                ph.emit("tensor", lambda e, _c=c, _s=sl: e.transpose(
                    pTY[:, _c * 128:(_c + 1) * 128],
                    ytb[:, _s, _c, :], idb[:]),
                    wait=w, inc="pe" if c == DC - 1 else None)
            pe_now = ph.c("pe")
            for c2 in range(DC):
                wtt = [("pe", pe_now)] if c2 == 0 else []
                if c2 == 0 and t >= 1:
                    wtt.append(stx1_p3[t - 1])
                ph.emit("vector", lambda e, _c2=c2, _s=sl: e.tensor_add(
                    x1n[:, _c2 * 128:(_c2 + 1) * 128],
                    x1b[:, _s, _c2 * 128:(_c2 + 1) * 128],
                    pTY[:, _c2 * 128:(_c2 + 1) * 128]),
                    wait=wtt,
                    inc="ve" if c2 == DC - 1 else None)
            bankcred["pTY"] = ("ve", ph.c("ve"))
            ph.emit("sync", lambda e, _t=t: e.dma_start(
                out=X1d[_t * 128:(_t + 1) * 128, :], in_=x1n[:]),
                wait=[("ve", ph.c("ve"))], inc="ea", incby=16)
            stx1_p3[t] = ("ea", ph.c("ea"))
            _ln_tm(ph, x1n[:], sqd3, tp3, sd3, rr3, 1.0e-5,
                   wait_in=[("ve", ph.c("ve"))])
            wx = [("pe", pe_xn[t - 1])] if t >= 1 else []
            ph.emit("vector", lambda e: e.tensor_scalar(
                xnt[:], x1n[:], tp3[:, 4:5], rr3[:], ALU.subtract, ALU.mult),
                wait=wx, inc="ve")
            ve_now = ph.c("ve")
            w = [("ve", ve_now)]
            if t >= 1:
                w.append(("ve", xnf_ev[t - 1]))
            for c in range(DC):
                ph.emit("tensor", lambda e, _c=c: e.transpose(
                    pT3[:, _c * 128:(_c + 1) * 128],
                    xnt[:, _c * 128:(_c + 1) * 128], idb[:]),
                    wait=w if c == 0 else (), inc="pe" if c == DC - 1 else None)
            pe_xn[t] = ph.c("pe")
            wv = [("pe", pe_xn[t])]
            if t >= 2:
                wv.append(st_c[t - 2])
            ph.emit("vector", lambda e, _s=sl: e.tensor_copy(
                xnf[:, _s].rearrange("p a b -> p (a b)"), pT3[:]),
                wait=wv, inc="ve")
            if t == 0:
                xnf_ev = {}
            xnf_ev[t] = ph.c("ve")
            snf = ("fa", "fb")[sl]
            ph.emit("sync", lambda e, _s=sl, _t=t: e.dma_start(
                out=XN2d[:, :, _t * 128:(_t + 1) * 128], in_=xnf[:, _s]),
                wait=[("ve", ph.c("ve"))], inc=snf, incby=16)
            st_c[t] = (snf, ph.c(snf))
            ve_c[t] = ph.c("ve")
        for nmx in ("ea", "fa", "fb"):
            ph.wait("sync", (nmx, ph.c(nmx)))
        ph.barrier()
        ph.run("P3")

    with contextlib.ExitStack() as p4s:
        Kfm = p4s.enter_context(nc.sbuf_tensor("p4_k", [128, 8, S], BF16))
        Vtm = p4s.enter_context(nc.sbuf_tensor("p4_v", [128, 32, 1024], BF16))
        xnb4 = p4s.enter_context(nc.sbuf_tensor("p4_xn", [128, 2, 8, 512], BF16))
        pT4 = p4s.enter_context(nc.psum_tensor("pT4", [128, 128], BF16))
        p4i = p4s.enter_context(contextlib.ExitStack())
        wqb = p4i.enter_context(nc.sbuf_tensor("p4_wq", [128, 8, 8, 128], BF16))
        wkb = p4i.enter_context(nc.sbuf_tensor("p4_wk", [128, 8, 8, 128], BF16))
        wvb = p4i.enter_context(nc.sbuf_tensor("p4_wv", [128, 8, 1024], BF16))
        ph = Ph(nc, sems, cnt)
        for dst, src in ((wqb, l1_wq), (wkb, l1_wk), (wvb, l1_wv)):
            ph.emit("gpsimd", lambda e, _d=dst, _s=src: e.dma_start(
                out=_d[:], in_=_s), inc="lc", incby=16)
        lw4 = ph.c("lc")
        xn_cons = {}
        for t in range(8):
            sl = t % 2
            snm = ("da", "db")[sl]
            w = [("pe", xn_cons[t - 2])] if t >= 2 else []
            ph.emit("sync", lambda e, _s=sl, _t=t: e.dma_start(
                out=xnb4[:, _s], in_=XN2d[:, :, _t * 512:(_t + 1) * 512]),
                wait=w, inc=snm, incby=16)
            ld_x = (snm, ph.c(snm))
            for mc in range(DC):
                bK = ("pA", "pB")[mc % 2]
                pbK = (pA, pB)[mc % 2]
                for kc in range(DC):
                    w = []
                    if kc == 0:
                        w = [ld_x, ("lc", lw4)] + bwait(bK)
                    ph.emit("tensor", lambda e, _b=pbK, _k=kc, _m=mc, _s=sl: e.matmul(
                        _b[:, 0:512], wkb[:, _k, _m, :], xnb4[:, _s, _k, :],
                        start=(_k == 0), stop=(_k == DC - 1)),
                        wait=w, inc="pe" if kc == DC - 1 else None)
                ph.emit("vector", lambda e, _b=pbK, _m=mc, _t=t: e.tensor_copy(
                    Kfm[:, _m, _t * 512:(_t + 1) * 512], _b[:, 0:512]),
                    wait=[("pe", ph.c("pe"))], inc="ve")
                bankcred[bK] = ("ve", ph.c("ve"))
            for g in range(4):
                gabs = t * 4 + g
                for half in range(2):
                    for kc in range(DC):
                        w = []
                        if kc == 0 and half == 0:
                            w = bwait("pO")
                        ph.emit("tensor", lambda e, _k=kc, _h=half, _g=g, _s=sl: e.matmul(
                            pO[:, _h * 512:(_h + 1) * 512],
                            xnb4[:, _s, _k, _g * 128:(_g + 1) * 128],
                            wvb[:, _k, _h * 512:(_h + 1) * 512],
                            start=(_k == 0), stop=(_k == DC - 1)),
                            wait=w, inc="pe" if (kc == DC - 1 and half == 1) else None)
                ph.emit("vector", lambda e, _g=gabs: e.tensor_copy(
                    Vtm[:, _g, :], pO[:]), wait=[("pe", ph.c("pe"))], inc="ve")
                bankcred["pO"] = ("ve", ph.c("ve"))
            xn_cons[t] = ph.c("pe")
        pe_p4a_end = ph.c("pe")
        ph.barrier()
        ph.run("P4a")
        p4i.close()
        wqb = p4s.enter_context(nc.sbuf_tensor("p4_wq2", [128, 8, 8, 128], BF16))
        qfm = p4s.enter_context(nc.sbuf_tensor("p4_q", [128, 8, 512], BF16))
        psb = p4s.enter_context(nc.sbuf_tensor("p4_p", [128, 128], BF16))
        ptb = p4s.enter_context(nc.sbuf_tensor("p4_pt", [128, 128], BF16))
        osb = p4s.enter_context(nc.sbuf_tensor("p4_o", [128, 2, 1024], F32))
        oac = p4s.enter_context(nc.sbuf_tensor("p4_oa", [128, 1024], F32))
        mst = p4s.enter_context(nc.sbuf_tensor("p4_m", [128, 8], F32))
        ph = Ph(nc, sems, cnt)
        ph.emit("gpsimd", lambda e: e.dma_start(out=wqb[:], in_=l1_wq),
                inc="w0", incby=16)
        ph.wait("tensor", ("w0", ph.c("w0")))
        sg_o = {}
        pe_tr_prev = 0
        osb_free = None
        for qc in range(32):
            qg, qi = qc // 4, qc % 4
            if qi == 0:
                sl = qg % 2
                snm = ("da", "db")[sl]
                w = ([("pe", xn_cons[qg - 2])] if qg >= 2
                     else [("pe", pe_p4a_end)])
                ph.emit("sync", lambda e, _s=sl, _g=qg: e.dma_start(
                    out=xnb4[:, _s], in_=XN2d[:, :, _g * 512:(_g + 1) * 512]),
                    wait=w, inc=snm, incby=16)
                ld_x = (snm, ph.c(snm))
                for mc in range(DC):
                    bQ = ("pC", "pD")[mc % 2]
                    pbQ = (pC, pD)[mc % 2]
                    for kc in range(DC):
                        w = []
                        if kc == 0:
                            w = [ld_x] + bwait(bQ)
                        ph.emit("tensor", lambda e, _b=pbQ, _k=kc, _m=mc, _s=sl: e.matmul(
                            _b[:, 0:512], wqb[:, _k, _m, :], xnb4[:, _s, _k, :],
                            start=(_k == 0), stop=(_k == DC - 1)),
                            wait=w, inc="pe" if kc == DC - 1 else None)
                    ph.emit("vector", lambda e, _b=pbQ, _m=mc: e.tensor_copy(
                        qfm[:, _m, :], _b[:, 0:512]),
                        wait=[("pe", ph.c("pe"))], inc="ve")
                    bankcred[bQ] = ("ve", ph.c("ve"))
                xn_cons[qg] = ph.c("pe")
                ve_q = ph.c("ve")
            ph.emit("vector", lambda e: e.memset(mst[:, 0:1], -1.0e30))
            ph.emit("vector", lambda e: e.memset(mst[:, 1:2], 0.0))
            ph.emit("vector", lambda e: e.memset(oac[:], 0.0),
                    wait=[osb_free] if osb_free else [])
            ph.emit("vector", lambda e: e.drain(), inc="ve")
            for kb in range(qc + 1):
                bS = ("pA", "pB")[kb % 2]
                pbS = (pA, pB)[kb % 2]
                for kc in range(DC):
                    w = []
                    if kc == 0:
                        w = [("ve", ve_q)] + bwait(bS)
                    ph.emit("tensor", lambda e, _b=pbS, _k=kc, _kb=kb, _qi=qi: e.matmul(
                        _b[:, 0:128], qfm[:, _k, _qi * 128:(_qi + 1) * 128],
                        Kfm[:, _k, _kb * 128:(_kb + 1) * 128],
                        start=(_k == 0), stop=(_k == DC - 1)),
                        wait=w, inc="pe" if kc == DC - 1 else None)
                pe_s = ph.c("pe")
                if kb == qc:
                    ph.emit("vector", lambda e, _b=pbS: e.tensor_add(
                        _b[:, 0:128], _b[:, 0:128], caus[:]),
                        wait=[("pe", pe_s)])
                    ph.emit("vector", lambda e: e.drain(), inc="ve")
                    wrm = [("ve", ph.c("ve"))]
                else:
                    wrm = [("pe", pe_s)]
                ph.emit("vector", lambda e, _b=pbS: e.reduce_max(
                    mst[:, 2:3], _b[:, 0:128], axis=AX.X), wait=wrm)
                ph.emit("vector", lambda e: e.drain())
                ph.emit("vector", lambda e: e.tensor_scalar(
                    mst[:, 3:4], mst[:, 2:3], GAMMA, None, ALU.mult))
                ph.emit("vector", lambda e: e.drain())
                ph.emit("vector", lambda e: e.tensor_max(
                    mst[:, 4:5], mst[:, 0:1], mst[:, 3:4]))
                ph.emit("vector", lambda e: e.drain())
                ph.emit("vector", lambda e: e.tensor_sub(
                    mst[:, 5:6], mst[:, 0:1], mst[:, 4:5]))
                ph.emit("vector", lambda e: e.tensor_scalar_mul(
                    mst[:, 6:7], mst[:, 4:5], -1.0))
                ph.emit("vector", lambda e: e.drain(), inc="ve")
                ph.emit("scalar", lambda e: e.activation(
                    mst[:, 7:8], mst[:, 5:6], AF.Exp),
                    wait=[("ve", ph.c("ve"))], inc="se")
                wp = [("pe", pe_tr_prev)] if pe_tr_prev else []
                ph.emit("scalar", lambda e, _b=pbS: e.activation(
                    psb[:], _b[:, 0:128], AF.Exp, bias=mst[:, 6:7], scale=GAMMA,
                    accum_out=mst[:, 2:3]), wait=wp, inc="se")
                se_p = ph.c("se")
                bankcred[bS] = ("se", se_p)
                ph.emit("vector", lambda e: e.scalar_tensor_tensor(
                    mst[:, 1:2], mst[:, 1:2], mst[:, 7:8], mst[:, 2:3],
                    ALU.mult, ALU.add), wait=[("se", se_p)])
                ph.emit("vector", lambda e: e.tensor_copy(mst[:, 0:1], mst[:, 4:5]))
                ph.emit("vector", lambda e: e.drain(), inc="ve")
                w = [("se", se_p)] + bwait("pT4")
                ph.emit("tensor", lambda e: e.transpose(pT4[:], psb[:], idb[:]),
                        wait=w, inc="pe")
                pe_tr_prev = ph.c("pe")
                ph.emit("vector", lambda e: e.tensor_copy(ptb[:], pT4[:]),
                        wait=[("pe", ph.c("pe"))], inc="ve")
                bankcred["pT4"] = ("ve", ph.c("ve"))
                ve_pt = ph.c("ve")
                for half in range(2):
                    w = [("ve", ve_pt)] + (bwait("pO") if half == 0 else [])
                    ph.emit("tensor", lambda e, _h=half, _kb=kb: e.matmul(
                        pO[:, _h * 512:(_h + 1) * 512], ptb[:],
                        Vtm[:, _kb, _h * 512:(_h + 1) * 512],
                        start=True, stop=True),
                        wait=w, inc="pe" if half == 1 else None)
                pe_pv = ph.c("pe")
                ph.emit("vector", lambda e: e.drain())
                for half in range(2):
                    ph.emit("vector", lambda e, _h=half: e.scalar_tensor_tensor(
                        oac[:, _h * 512:(_h + 1) * 512],
                        oac[:, _h * 512:(_h + 1) * 512], mst[:, 7:8],
                        pO[:, _h * 512:(_h + 1) * 512], ALU.mult, ALU.add),
                        wait=[("pe", pe_pv)] if half == 0 else [],
                        inc="ve" if half == 1 else None)
                bankcred["pO"] = ("ve", ph.c("ve"))
            ph.emit("vector", lambda e: e.reciprocal(mst[:, 2:3], mst[:, 1:2]))
            ph.emit("vector", lambda e: e.drain())
            osl = qc % 2
            sno = ("ra", "rb")[osl]
            w = [sg_o[qc - 2]] if qc >= 2 else []
            ph.emit("vector", lambda e, _o=osl: e.tensor_scalar_mul(
                osb[:, _o, :], oac[:], mst[:, 2:3]), wait=w, inc="ve")
            osb_free = ("ve", ph.c("ve"))
            ph.emit("gpsimd", lambda e, _o=osl, _q=qc: e.dma_start(
                out=X1d[_q * 128:(_q + 1) * 128, :], in_=osb[:, _o, :],
                accum_op=ALU.add), wait=[("ve", ph.c("ve"))], inc=sno, incby=16)
            sg_o[qc] = (sno, ph.c(sno))
        for nmx in ("ra", "rb"):
            ph.wait("gpsimd", (nmx, ph.c(nmx)))
        ph.barrier()
        ph.run("P4")

    T5 = 256
    with contextlib.ExitStack() as p5s:
        wgb = p5s.enter_context(nc.sbuf_tensor("p5_wg", [128, 8, 24, 128], BF16))
        wub = p5s.enter_context(nc.sbuf_tensor("p5_wu", [128, 8, 24, 128], BF16))
        wdb = p5s.enter_context(nc.sbuf_tensor("p5_wd", [128, 24, 1024], BF16))
        xnb5 = p5s.enter_context(nc.sbuf_tensor("p5_xn", [128, 2, 8, T5], BF16))
        hb5 = p5s.enter_context(nc.sbuf_tensor("p5_h", [128, 24, T5], BF16))
        st5 = p5s.enter_context(nc.sbuf_tensor("p5_st", [128, 2, T5], F32))
        x1t5 = p5s.enter_context(nc.sbuf_tensor("p5_x1", [128, 4, 1024], F32))
        ob5 = p5s.enter_context(nc.sbuf_tensor("p5_o", [128, 4, 1024], F32))
        ph = Ph(nc, sems, cnt)
        for dst, src in ((wgb, l1_wg), (wub, l1_wu), (wdb, l1_wd)):
            ph.emit("gpsimd", lambda e, _d=dst, _s=src: e.dma_start(
                out=_d[:], in_=_s), inc="lc", incby=16)
        lw5 = ph.c("lc")
        NT5 = S // T5
        NG5 = T5 // 128
        xn_cons = {}
        se_h = {}
        ve_h = {}
        st_out = {}
        gi_abs = 0
        for t in range(NT5):
            sl = t % 2
            snm = ("da", "db")[sl]
            w = [("pe", xn_cons[t - 2])] if t >= 2 else []
            ph.emit("sync", lambda e, _s=sl, _t=t: e.dma_start(
                out=xnb5[:, _s], in_=XN2d[:, :, _t * T5:(_t + 1) * T5]),
                wait=w, inc=snm, incby=16)
            ld_x = (snm, ph.c(snm))
            ld_x1w = []
            for g in range(NG5):
                gab = t * NG5 + g
                xsl = gab % 4
                sng = ("ga", "gb", "gc", "gd")[xsl]
                w = [("ve", ob_ev[gab - 4])] if gab >= 4 else []
                ph.emit("sync", lambda e, _x=xsl, _g=gab: e.dma_start(
                    out=x1t5[:, _x, :], in_=X1d[_g * 128:(_g + 1) * 128, :]),
                    wait=w, inc=sng, incby=16)
                ld_x1w.append((sng, ph.c(sng)))
            for c in range(24):
                bG = ("pA", "pC")[c % 2]
                bU = ("pB", "pD")[c % 2]
                pbG = (pA, pC)[c % 2]
                pbU = (pB, pD)[c % 2]
                for kc in range(DC):
                    w = []
                    if kc == 0:
                        w = [ld_x, ("lc", lw5)] + bwait(bG)
                    ph.emit("tensor", lambda e, _b=pbG, _k=kc, _c=c, _s=sl: e.matmul(
                        _b[:, 0:T5], wgb[:, _k, _c, :], xnb5[:, _s, _k, :],
                        start=(_k == 0), stop=(_k == DC - 1)),
                        wait=w, inc="pe" if kc == DC - 1 else None)
                pe_g = ph.c("pe")
                for kc in range(DC):
                    w = []
                    if kc == 0:
                        w = bwait(bU)
                    ph.emit("tensor", lambda e, _b=pbU, _k=kc, _c=c, _s=sl: e.matmul(
                        _b[:, 0:T5], wub[:, _k, _c, :], xnb5[:, _s, _k, :],
                        start=(_k == 0), stop=(_k == DC - 1)),
                        wait=w, inc="pe" if kc == DC - 1 else None)
                pe_u = ph.c("pe")
                ssl = c % 2
                ws = [("pe", pe_g)]
                key = c - 2
                if key in ve_h:
                    ws.append(("ve", ve_h[key]))
                elif c < 2 and t > 0:
                    ws.append(("ve", prev_ve_h))
                ph.emit("scalar", lambda e, _b=pbG, _s=ssl: e.activation(
                    st5[:, _s, :], _b[:, 0:T5], AF.Silu, scale=QSCALE),
                    wait=ws, inc="se")
                bankcred[bG] = ("se", ph.c("se"))
                ph.emit("vector", lambda e, _b=pbU, _c=c, _s=ssl: e.scalar_tensor_tensor(
                    hb5[:, _c, :], _b[:, 0:T5], QSCALE, st5[:, _s, :],
                    ALU.mult, ALU.mult),
                    wait=[("se", ph.c("se")), ("pe", pe_u)], inc="ve")
                ve_h[c] = ph.c("ve")
                bankcred[bU] = ("ve", ph.c("ve"))
            prev_ve_h = ph.c("ve")
            ve_h = {}
            ve_hall = prev_ve_h
            for g in range(NG5):
                gab = t * NG5 + g
                xsl = gab % 4
                for half in range(2):
                    for kc in range(24):
                        w = []
                        if kc == 0:
                            w = [("ve", ve_hall)]
                            if half == 0:
                                w += bwait("pO")
                        ph.emit("tensor", lambda e, _k=kc, _h=half, _g=g: e.matmul(
                            pO[:, _h * 512:(_h + 1) * 512],
                            hb5[:, _k, _g * 128:(_g + 1) * 128],
                            wdb[:, _k, _h * 512:(_h + 1) * 512],
                            start=(_k == 0), stop=(_k == 23)),
                            wait=w, inc="pe" if kc == 23 else None)
                pe_o5 = ph.c("pe")
                w = [("pe", pe_o5), ld_x1w[g]]
                if gab >= 4:
                    w.append(st_out[gab - 4])
                ph.emit("vector", lambda e, _x=xsl: e.scalar_tensor_tensor(
                    ob5[:, _x, :], pO[:], QSCALE, x1t5[:, _x, :],
                    ALU.mult, ALU.add), wait=w, inc="ve")
                if gab == 0:
                    ob_ev = {}
                ob_ev[gab] = ph.c("ve")
                bankcred["pO"] = ("ve", ph.c("ve"))
                sne = ("ea", "eb", "ec", "ed")[xsl]
                ph.emit("sync", lambda e, _x=xsl, _g=gab: e.dma_start(
                    out=out_t[_g * 128:(_g + 1) * 128, :], in_=ob5[:, _x, :]),
                    wait=[("ve", ph.c("ve"))], inc=sne, incby=16)
                st_out[gab] = (sne, ph.c(sne))
            xn_cons[t] = ph.c("pe")
        for nmx in ("ea", "eb", "ec", "ed"):
            ph.wait("sync", (nmx, ph.c(nmx)))
        ph.barrier()
        ph.run("P5")

    ph = Ph(nc, sems, cnt)
    for dst, src in dbg_outs:
        ph.emit("sync", lambda e, _d=dst, _s=src: e.dma_start(out=_d, in_=_s),
                inc="st", incby=16)
    if dbg_comb is not None:
        ph.emit("sync", lambda e: e.dma_start(out=dbg_comb, in_=combt[:]),
                inc="st", incby=16)
    ph.wait("sync", ("st", ph.c("st")))
    ph.barrier()
    ph.run()

    stack.close()
    return nc


def _lhsT_tiles(w):
    K, M = w.shape
    return np.ascontiguousarray(
        w.reshape(K // 128, 128, M // 128, 128).transpose(1, 0, 2, 3))


def _rhs_tiles(w):
    K, N = w.shape
    return np.ascontiguousarray(w.reshape(K // 128, 128, N).transpose(1, 0, 2))


def _fmvec(v):
    n = v.shape[0] // 128
    return np.ascontiguousarray(v.reshape(n, 128).T)


def host_prep(inputs):
    import ml_dtypes
    g = {}
    wq = np.asarray(inputs["L0_attn_Wq"])
    wk = np.asarray(inputs["L0_attn_Wk"])
    wv = np.asarray(inputs["L0_attn_Wv"])
    def qkv(w):
        weff = np.ascontiguousarray(w.transpose(0, 2, 1, 3).reshape(NSL, D, D))
        return np.stack([_lhsT_tiles(weff[i]) for i in range(NSL)])
    g["l0_wq"] = qkv(wq)
    g["l0_wk"] = qkv(wk)
    g["l0_wv"] = qkv(wv)
    g["l0_wo"] = np.stack([_lhsT_tiles(np.asarray(inputs["L0_attn_Wo"])[i])
                           for i in range(NSL)])
    g["l0_w1"] = np.stack([_lhsT_tiles(np.asarray(inputs["L0_attn_W1"])[i])
                           for i in range(NSL)])
    g["l0_w2"] = np.stack([_lhsT_tiles(np.asarray(inputs["L0_attn_W2"])[i])
                           for i in range(NSL)])
    cbank = np.zeros((128, NSL, 104), np.float32)
    for i in range(NSL):
        cbank[:, i, 0:8] = _fmvec(np.asarray(inputs["L0_attn_bq"])[i].reshape(-1))
        cbank[:, i, 8:16] = _fmvec(np.asarray(inputs["L0_attn_bk"])[i].reshape(-1))
        cbank[:, i, 16:24] = _fmvec(np.asarray(inputs["L0_attn_bv"])[i].reshape(-1))
        cbank[:, i, 24:32] = _fmvec(np.asarray(inputs["L0_attn_bo"])[i])
        cbank[:, i, 32:40] = _fmvec(np.asarray(inputs["L0_attn_ln1_s"])[i])
        cbank[:, i, 40:48] = _fmvec(np.asarray(inputs["L0_attn_ln1_b"])[i])
        cbank[:, i, 48:56] = _fmvec(np.asarray(inputs["L0_attn_ln2_s"])[i])
        cbank[:, i, 56:64] = _fmvec(np.asarray(inputs["L0_attn_ln2_b"])[i])
        cbank[:, i, 64:96] = _fmvec(np.asarray(inputs["L0_attn_b1"])[i])
        cbank[:, i, 96:104] = _fmvec(np.asarray(inputs["L0_attn_b2"])[i])
    g["l0_cb"] = cbank
    g["moe_gw"] = _rhs_tiles(np.ascontiguousarray(
        np.asarray(inputs["L0_ffn_gate_w"]).T[:, EPERM]))
    w13 = np.asarray(inputs["L0_ffn_w13_weight"])
    g["moe_w13"] = np.stack([_lhsT_tiles(np.ascontiguousarray(w13[e].T))
                             for e in range(NE)])
    w2m = np.asarray(inputs["L0_ffn_w2_weight"])
    g["moe_w2"] = np.stack([_rhs_tiles(np.ascontiguousarray(w2m[e].T))
                            for e in range(NE)])
    g["l1_wq"] = _lhsT_tiles(np.asarray(inputs["L1_attn_w_q"]))
    g["l1_wk"] = _lhsT_tiles(np.asarray(inputs["L1_attn_w_k"]))
    g["l1_wv"] = _rhs_tiles(np.asarray(inputs["L1_attn_w_v"]))
    g["l1_wg"] = _lhsT_tiles(np.ascontiguousarray(
        np.asarray(inputs["L1_ffn_w_gate"]).T))
    g["l1_wu"] = _lhsT_tiles(np.ascontiguousarray(
        np.asarray(inputs["L1_ffn_w_up"]).T))
    g["l1_wd"] = _rhs_tiles(np.ascontiguousarray(
        np.asarray(inputs["L1_ffn_w_down"]).T))
    import ml_dtypes
    g["identb"] = np.eye(128, dtype=ml_dtypes.bfloat16)
    g["identf"] = np.eye(128, dtype=np.float32)
    g["ones_stat"] = np.full((128, 1), 1.0 / 1024.0, ml_dtypes.bfloat16)
    g["ones_row"] = np.ones((1, 128), np.float32)
    m = np.zeros((128, 128), np.float32)
    m[np.triu_indices(128, 1)] = NEG
    g["causal"] = m
    return g


def kernel(**inputs):
    g = host_prep(inputs)
    nc = build()
    x = np.asarray(inputs["L0_attn_x"])
    in_maps = []
    for n in range(8):
        m = dict(g)
        m["x"] = np.ascontiguousarray(x[n])
        in_maps.append(m)
    res = run_bass_kernel_spmd(nc, in_maps, core_ids=list(range(8)))
    return np.stack([res.results[n]["out"] for n in range(8)])
